# version 25 (final)
# baseline (speedup 1.0000x reference)
import numpy as np
import ml_dtypes
from contextlib import ExitStack

import concourse.bass as bass
import concourse.mybir as mybir
from concourse.bass_utils import run_bass_kernel_spmd

F32 = mybir.dt.float32
BF16 = mybir.dt.bfloat16
I32 = mybir.dt.int32
AF = mybir.ActivationFunctionType
ALU = mybir.AluOpType
AX = mybir.AxisListType

P = 128
S = 4096
H = 1024
D = 256
FF = 4096
NCH = S // P
NHB = NCH // 2
EPS_LN = 1e-5
EPS_ATT = 1e-6

ENGS = ("sync", "scalar", "vector", "tensor", "gpsimd")
DEBUG = False
DEBUG2 = False
LAST_DBG = {}
LAST_DBG2 = {}


class Prog:
    def __init__(self, nc, ctx):
        self.nc = nc
        self.ops = {e: [] for e in ENGS}
        self.sem = {}
        self.cnt = {}
        self.ctx = ctx

    def new_sem(self, name):
        self.sem[name] = self.ctx.enter_context(self.nc.semaphore(name))
        self.cnt[name] = 0

    def op(self, eng, fn, waits=(), inc=None, inc_amt=1):
        waits = tuple((s, v) for (s, v) in waits if v > 0)
        self.ops[eng].append((waits, fn, inc, inc_amt))
        if inc is not None:
            self.cnt[inc] += inc_amt
            return self.cnt[inc]
        return None

    def dma(self, eng, out, in_, waits=(), inc=None, **kw):
        def fn(e, out=out, in_=in_, kw=kw):
            return e.dma_start(out=out, in_=in_, **kw)
        return self.op(eng, fn, waits, inc, 16)

    def replay(self, block):
        for ename in ENGS:
            ops = self.ops[ename]

            def make(ops, sems):
                def body(eng):
                    last = {}
                    for waits, fn, inc, inc_amt in ops:
                        for sname, val in waits:
                            if last.get(sname, 0) < val:
                                eng.wait_ge(sems[sname], val)
                                last[sname] = val
                        inst = fn(eng)
                        if inc is not None:
                            inst.then_inc(sems[inc], inc_amt)
                return body
            getattr(block, ename)(make(ops, self.sem))


def build_nc():
    nc = bass.Bass("TRN2", target_bir_lowering=False, debug=False)

    x_h = nc.dram_tensor("x", [S, H], F32, kind="ExternalInput").ap()
    wq_h = nc.dram_tensor("wq", [D, H], F32, kind="ExternalInput").ap()
    wk_h = nc.dram_tensor("wk", [D, H], F32, kind="ExternalInput").ap()
    wv_h = nc.dram_tensor("wv", [D, H], F32, kind="ExternalInput").ap()
    wo_h = nc.dram_tensor("wo", [H, D], F32, kind="ExternalInput").ap()
    gate_h = nc.dram_tensor("gate", [1], F32, kind="ExternalInput").ap()
    mem_h = nc.dram_tensor("mem", [D, D], F32, kind="ExternalInput").ap()
    mn_h = nc.dram_tensor("mn", [D], F32, kind="ExternalInput").ap()
    gam_h = nc.dram_tensor("gam", [H], F32, kind="ExternalInput").ap()
    bet_h = nc.dram_tensor("bet", [H], F32, kind="ExternalInput").ap()
    q1_h = nc.dram_tensor("q1", [H, FF], I32, kind="ExternalInput").ap()
    s1_h = nc.dram_tensor("s1", [8, FF], F32, kind="ExternalInput").ap()
    b1_h = nc.dram_tensor("b1", [FF], F32, kind="ExternalInput").ap()
    q2_h = nc.dram_tensor("q2", [FF, H], I32, kind="ExternalInput").ap()
    s2_h = nc.dram_tensor("s2", [32, H], F32, kind="ExternalInput").ap()
    b2_h = nc.dram_tensor("b2", [H], F32, kind="ExternalInput").ap()
    idb_h = nc.dram_tensor("idb_d", [P, P], BF16, kind="ExternalInput").ap()
    idf_h = nc.dram_tensor("idf_d", [P, P], F32, kind="ExternalInput").ap()
    msk_h = nc.dram_tensor("msk_d", [P, P], BF16, kind="ExternalInput").ap()

    out_h = nc.dram_tensor("out", [S, H], F32, kind="ExternalOutput").ap()
    dbg_h = nc.dram_tensor("dbg", [P, 4096], F32, kind="ExternalOutput").ap() if DEBUG else None
    w1d_h = nc.dram_tensor("w1dump", [P, 8 * FF], BF16, kind="ExternalOutput").ap() if DEBUG2 else None
    w2d_h = nc.dram_tensor("w2dump", [32, P, H], BF16, kind="ExternalOutput").ap() if DEBUG2 else None
    w2s_h = nc.dram_tensor("w2scr", [32, P, H], BF16, kind="Internal").ap()
    s2h_h = nc.dram_tensor("s2half", [32, H], F32, kind="Internal").ap()

    with ExitStack() as ctx:
        def sb(name, shape, dtype):
            return ctx.enter_context(nc.sbuf_tensor(name, shape, dtype))

        w1s = sb("w1s", [P, 8, FF], BF16)
        w2r = sb("w2r", [P, 8, H], BF16)
        wqT = sb("wqT", [P, 8, D], BF16)
        wkT = sb("wkT", [P, 8, D], BF16)
        wvT = sb("wvT", [P, 8, D], BF16)
        woT = sb("woT", [P, 2, H], BF16)
        idb = sb("idb", [P, P], BF16)
        idf = sb("idf", [P, P], F32)
        msk = sb("msk", [P, P], BF16)
        xbuf = sb("xbuf", [P, 3, H], F32)
        xnf = sb("xnf", [P, 2, H], F32)
        ov1 = sb("ov1", [P, 2, H], F32)
        ov2 = sb("ov2", [P, 2, H], F32)
        ov3 = sb("ov3", [P, 2, H], F32)
        ov4 = sb("ov4", [P, 2, H], BF16)
        ubuf = ov1[:]
        orow = ov2[:]
        wst = ov3[:]
        _ov3b = ov3[:, 0:2, :].rearrange("p t h -> p (t h)").bitcast(BF16)
        yT = _ov3b[:, 0:2048].rearrange("p (f x) -> p f x", f=8)
        r2sb = _ov3b[:, 2048:3072].rearrange("p (a i x) -> p a i x", a=2, i=2)
        elu = _ov3b[:, 3072:4096].rearrange("p (i x) -> p i x", i=2)
        wstb = ov4[:]
        _ov4b = ov4[:, 0:2, :].rearrange("p t h -> p (t h)")
        skR = _ov4b[:, 0:1024].rearrange("p (a x) -> p a x", a=4)
        cmb = _ov4b[:, 1024:1536].rearrange("p (a x) -> p a x", a=2)
        cbT = _ov4b[:, 1536:2048].rearrange("p (a t x) -> p a t x", a=2, t=2)
        dump = sb("dump", [P, 384], F32)
        xnb = sb("xnb", [P, 4, H], BF16)
        xnT = sb("xnT", [P, 2, 8, D], BF16)
        xnnT = sb("xnnT", [P, 2, 8, D], BF16)
        sqT = sb("sqT", [P, 2, 2, D], BF16)
        skT = sb("skT", [P, 2, 2, D], BF16)
        vE = sb("vE", [P, 4, 258], BF16)
        vEg = sb("vEg", [P, 4, 258], BF16)
        smx = sb("smx", [P, 2, P], BF16)
        stat = sb("stat", [P, 8, 16], F32)
        kvst = sb("kvst", [P, 2, 2, D], F32)
        kss = sb("kss", [P, 4], F32)
        stE = sb("stE", [P, 2, 2, 258], BF16)
        mext = sb("mext", [P, 2, 258], BF16)
        mncol = sb("mncol", [P, 2], F32)
        b1c = sb("b1c", [P, 32], F32)
        sbc = sb("sbc", [P, 2, 512], F32)
        bstg = sbc[0:1, 0:2, :].rearrange("o t h -> o (t h)")
        b2r = sb("b2r", [1, H], BF16)
        onesb = sb("onesb", [1, 512], BF16)
        onesf = sb("onesf", [1, P], F32)
        halvesf = sb("halvesf", [1, P], F32)
        qf = sb("qf", [P, 2, 1024], I32)
        qff = qf[:, :, :].bitcast(F32)
        memstg = qff[:, 1, 0:2 * D].rearrange("p (t e) -> p t e", t=2)
        grow = sb("grow", [1, 16], F32)
        gcol = sb("gcol", [P, 4], F32)
        mnr = sb("mnr", [1, D], F32)
        gbc = sb("gbc", [P, 16], F32)
        dbg = w1s[:, 6:8, :].rearrange("p g f -> p (g f)").bitcast(F32) if DEBUG else None

        PS_t = ctx.enter_context(nc.psum_tensor("ps", [P, 8 * 512], F32))
        PS = PS_t[:]
        PSH = PS.bitcast(BF16)

        prg = Prog(nc, ctx)
        for s in ("x0", "x1", "x2", "ot0", "ot1", "wl0", "wl1", "wl2", "wl3",
                  "sp0", "sp1", "sp2", "sp3", "sp4", "sp5", "sp6", "sp7",
                  "qf0", "qf1", "sb0", "sb1", "pcon", "pmem", "pb1", "pgb",
                  "pb2", "ps2", "pwst", "w2fq", "dqc", "prc",
                  "st1", "ln1", "lnx", "st2", "xbf", "xtr", "xnt",
                  "bca", "bce", "xnn", "qk", "sgm", "sge", "sg",
                  "vev", "skt", "skr", "sct", "smc", "gst", "ste",
                  "attd", "cmbs", "cbt", "cbe", "yq", "peyq", "pez",
                  "zu", "zev", "woz", "sfv", "sfa", "xsm", "s2h"):
            prg.new_sem(s)
        C = prg.cnt

        block = ctx.enter_context(nc.Block())

        prg.dma("sync", idb[:], idb_h, inc="pcon")
        prg.dma("sync", idf[:], idf_h, inc="pcon")
        prg.dma("sync", msk[:], msk_h, inc="pcon")
        prg.dma("sync", grow[0:1, 0:1], gate_h.unsqueeze(0), inc="pcon")
        prg.dma("sync", mnr[:], mn_h.unsqueeze(0), inc="pcon")
        pre_consts = C["pcon"]
        prg.dma("sync", mncol[:, 0:1], mn_h[0:P].unsqueeze(1), inc="pmem")
        prg.dma("sync", mncol[:, 1:2], mn_h[P:D].unsqueeze(1), inc="pmem")
        prg.dma("sync", memstg,
                mem_h.rearrange("(t p) e -> p t e", p=P), inc="pmem")
        pre_mem = C["pmem"]
        prg.dma("gpsimd", qff[0:32, 0, 0:P], b1_h.rearrange("(a p) -> a p", p=P),
                inc="pb1")
        pre_b1 = C["pb1"]
        prg.op("tensor", lambda e: e.transpose(out=PS[:, 0:32], in_=qff[0:32, 0, 0:P],
                                               identity=idf[0:32, 0:32]),
               waits=[("pb1", pre_b1), ("pcon", pre_consts)], inc="prc")
        prc_b1 = C["prc"]
        prg.op("vector", lambda e: e.tensor_copy(out=b1c[:], in_=PS[:, 0:32]),
               waits=[("prc", prc_b1)], inc="dqc")
        dq_b1 = C["dqc"]
        prg.dma("sync", bstg[0:1, 0:H], b2_h.unsqueeze(0),
                waits=[("dqc", dq_b1)], inc="pb2")
        pre_b2 = C["pb2"]
        prg.op("vector", lambda e: e.tensor_scalar(
            out=b2r[0:1, 0:H], in0=bstg[0:1, 0:H], scalar1=0.5, scalar2=None,
            op0=ALU.mult), waits=[("pb2", pre_b2)], inc="dqc")
        dq_b2 = C["dqc"]
        for fn in (lambda e: e.memset(onesf[:], 1.0),
                   lambda e: e.memset(halvesf[:], 0.5),
                   lambda e: e.memset(onesb[:], 1.0),
                   lambda e: e.memset(kvst[:, :, :, :].rearrange("p a t x -> p (a t x)"), 0.0),
                   lambda e: e.memset(kss[:], 0.0),
                   lambda e: e.memset(stE[:, :, :, :].rearrange("p a t x -> p (a t x)"), 0.0),
                   lambda e: e.memset(vE[:, :, :].rearrange("p a x -> p (a x)"), 1.0),
                   lambda e: e.memset(vEg[:, :, :].rearrange("p a x -> p (a x)"), 1.0)):
            prg.op("vector", fn, inc="dqc")
        dq_memset = C["dqc"]

        prg.dma("gpsimd", qff[0:8, 0, 0:P], gam_h.rearrange("(g p) -> g p", p=P),
                waits=[("prc", prc_b1)], inc="pgb")
        prg.dma("gpsimd", qff[0:8, 0, P:2 * P], bet_h.rearrange("(g p) -> g p", p=P),
                inc="pgb")
        pre_gb = C["pgb"]
        prg.op("vector", lambda e: e.tensor_copy(out=wstb[0:8, 0, 0:2 * P],
                                                 in_=qff[0:8, 0, 0:2 * P]),
               waits=[("pgb", pre_gb)], inc="dqc")
        dq_gbc = C["dqc"]
        prg.op("tensor", lambda e: e.transpose(out=PSH[:, 0:8],
                                               in_=wstb[0:8, 0, 0:P],
                                               identity=idb[0:8, 0:8]),
               waits=[("dqc", dq_gbc), ("pcon", pre_consts)], inc="prc")
        prg.op("tensor", lambda e: e.transpose(out=PSH[:, 8:16],
                                               in_=wstb[0:8, 0, P:2 * P],
                                               identity=idb[0:8, 0:8]), inc="prc")
        prc_gb = C["prc"]
        prg.op("vector", lambda e: e.tensor_copy(out=gbc[:, 0:16], in_=PSH[:, 0:16]),
               waits=[("prc", prc_gb)], inc="dqc")
        dq_gb2 = C["dqc"]

        prg.op("scalar", lambda e: e.activation(out=grow[0:1, 1:2], in_=grow[0:1, 0:1],
                                                func=AF.Tanh, scale=0.5),
               waits=[("pcon", pre_consts)], inc="sfa")
        sfa_t = C["sfa"]
        prg.op("scalar", lambda e: e.activation(out=grow[0:1, 2:3], in_=grow[0:1, 1:2],
                                                func=AF.Copy, scale=0.5, bias=0.5),
               waits=[("sfa", sfa_t)], inc="sfa")
        sfa_g = C["sfa"]
        prg.op("vector", lambda e: e.reduce_sum(out=grow[0:1, 6:7], in_=mnr[:], axis=AX.X),
               waits=[("pcon", pre_consts)], inc="sfv")
        sfv_mn = C["sfv"]
        prg.op("vector", lambda e: e.tensor_single_scalar(
            out=grow[0:1, 3:4], in_=grow[0:1, 6:7], scalar=EPS_ATT, op=ALU.is_ge),
            waits=[("sfv", sfv_mn), ("sfa", sfa_g)], inc="sfv")
        sfv_f = C["sfv"]
        prg.op("vector", lambda e: e.tensor_mul(out=grow[0:1, 4:5], in0=grow[0:1, 2:3],
                                                in1=grow[0:1, 3:4]),
               waits=[("sfv", sfv_f)], inc="sfv")
        sfv_ge = C["sfv"]
        prg.op("vector", lambda e: e.tensor_scalar(
            out=grow[0:1, 5:6], in0=grow[0:1, 2:3], scalar1=-1.0, scalar2=1.0,
            op0=ALU.mult, op1=ALU.add), waits=[("sfv", sfv_ge)], inc="sfv")
        sfv_g1 = C["sfv"]
        prg.op("tensor", lambda e: e.matmul(out=PS[:, 0:2], lhsT=onesf[0:1, :],
                                            rhs=grow[0:1, 4:6], start=True, stop=True),
               waits=[("sfv", sfv_g1), ("dqc", dq_memset)], inc="prc")
        prc_g = C["prc"]
        prg.op("vector", lambda e: e.tensor_copy(out=gcol[:, 0:2], in_=PS[:, 0:2]),
               waits=[("prc", prc_g)], inc="sfv")
        sfv_gc = C["sfv"]
        prg.op("vector", lambda e: e.tensor_scalar(
            out=mext[:, 0:2, 0:D],
            in0=memstg,
            scalar1=gcol[:, 0:1], scalar2=None, op0=ALU.mult),
            waits=[("sfv", sfv_gc), ("pmem", pre_mem)], inc="sfv")
        prg.op("vector", lambda e: e.tensor_copy(
            out=mext[:, 0:2, D:D + 1],
            in_=mncol[:, 0:2].unsqueeze(2)), waits=[("pmem", pre_mem)], inc="sfv")
        sfv_mext = C["sfv"]

        def wload(src_h, dst):
            dmaw = [("dqc", C["dqc"]), ("sfv", sfv_mext)]
            prg.dma("gpsimd", wst[:, 0:2, :],
                    src_h.rearrange("(t p) h -> p t h", p=P),
                    waits=dmaw, inc="pwst")
            pr = C["pwst"]
            prg.op("vector", lambda e: e.tensor_copy(
                out=wstb[:, 0:2, :].rearrange("p t h -> p (t h)"),
                in_=wst[:, 0:2, :].rearrange("p t h -> p (t h)")),
                waits=[("pwst", pr), ("prc", C["prc"])], inc="dqc")
            dq = C["dqc"]
            for j in range(8):
                for t in range(2):
                    prg.op("tensor", lambda e, j=j, t=t: e.transpose(
                        out=PSH[:, (2 * j + t) * P:(2 * j + t + 1) * P],
                        in_=wstb[:, t, j * P:(j + 1) * P], identity=idb[:]),
                        waits=[("dqc", dq)] if (j == 0 and t == 0) else (),
                        inc="prc")
            pr2 = C["prc"]
            prg.op("vector", lambda e, dst=dst: e.tensor_copy(
                out=dst[:, :, :].rearrange("p j d -> p (j d)"),
                in_=PSH[:, 0:2048]),
                waits=[("prc", pr2)], inc="dqc")
            return C["dqc"]

        dq_wq = wload(wq_h, wqT)
        dq_wk = wload(wk_h, wkT)
        dq_wv = wload(wv_h, wvT)

        def wo_half(li):
            dmaw = [("dqc", C["dqc"])]
            prg.dma("gpsimd", wst[:, 0, :].rearrange("p (a d) -> p a d", a=4),
                    wo_h.rearrange("(a p) d -> p a d", p=P)[:, 4 * li:4 * li + 4, :],
                    waits=dmaw, inc="pwst")
            pr = C["pwst"]
            prg.op("vector", lambda e: e.tensor_copy(
                out=wstb[:, 0, :], in_=wst[:, 0, :]),
                waits=[("pwst", pr), ("prc", C["prc"])], inc="dqc")
            dq = C["dqc"]
            flat = wstb[:, 0, :]
            for a in range(4):
                for t in range(2):
                    prg.op("tensor", lambda e, a=a, t=t: e.transpose(
                        out=PSH[:, (2 * a + t) * P:(2 * a + t + 1) * P],
                        in_=flat[:, a * 256 + t * P:a * 256 + (t + 1) * P],
                        identity=idb[:]),
                        waits=[("dqc", dq)] if (a == 0 and t == 0) else (),
                        inc="prc")
            pr2 = C["prc"]
            prg.op("vector", lambda e, li=li: e.tensor_copy(
                out=woT[:, 0:2, li * 512:(li + 1) * 512].rearrange("p t (a x) -> p t a x", a=4),
                in_=PSH[:, 0:1024].rearrange("p (a t x) -> p t a x", a=4, t=2, x=P)),
                waits=[("prc", pr2)], inc="dqc")
            return C["dqc"]

        dq_wo = wo_half(0)
        dq_wo = wo_half(1)

        prg.dma("gpsimd", qff[0:32, 1, :], s2_h,
                waits=[("sfv", sfv_mext), ("dqc", dq_wo)], inc="ps2")
        pre_s2 = C["ps2"]
        prg.op("vector", lambda e: e.tensor_scalar(
            out=qff[0:32, 1, :], in0=qff[0:32, 1, :], scalar1=0.5, scalar2=None,
            op0=ALU.mult), waits=[("ps2", pre_s2)], inc="dqc")
        dq_s2 = C["dqc"]
        prg.dma("gpsimd", s2h_h, qff[0:32, 1, :], waits=[("dqc", dq_s2)], inc="s2h")
        s2h_done = C["s2h"]

        dq_slot = [dq_s2, dq_s2]
        sbc_slot = [dq_b2, dq_b2]
        sbci = [0]

        def dq_w1_group(g):
            for qt in range(4):
                slot = qt % 2
                w = [("dqc", dq_slot[slot])]
                if g == 0 and qt == 0:
                    w.append(("prc", prc_b1))
                if g == 0 and qt == 1:
                    w.append(("s2h", s2h_done))
                pr_q = prg.dma("sync", qf[:, slot, :],
                               q1_h[g * P:(g + 1) * P, qt * 1024:(qt + 1) * 1024],
                               waits=w, inc="qf%d" % slot)
                for h2 in range(2):
                    si = sbci[0] % 2
                    sbci[0] += 1
                    pr_s = prg.dma("sync" if si == 0 else "scalar", sbc[:, si, :],
                                   s1_h[g:g + 1, qt * 1024 + h2 * 512:qt * 1024 + (h2 + 1) * 512].broadcast_to([P, 512]),
                                   waits=[("dqc", sbc_slot[si])], inc="sb%d" % si)
                    sbc_slot[si] = prg.op("vector", lambda e, g=g, qt=qt, h2=h2, slot=slot, si=si: e.scalar_tensor_tensor(
                        out=w1s[:, g, qt * 1024 + h2 * 512:qt * 1024 + (h2 + 1) * 512],
                        in0=qf[:, slot, h2 * 512:(h2 + 1) * 512], scalar=-128.0,
                        in1=sbc[:, si, :], op0=ALU.add, op1=ALU.mult),
                        waits=[("sb%d" % si, pr_s), ("qf%d" % slot, pr_q)], inc="dqc")
                dq_slot[slot] = C["dqc"]

        for g in range(8):
            dq_w1_group(g)
        dq_w1 = C["dqc"]

        splv = [0] * 33
        dqw2t = [0] * 32

        def dq_w2_tile(t):
            slot = t % 2
            w = [("dqc", dq_slot[slot])]
            pr_q = prg.dma("sync", qf[:, slot, :], q2_h[t * P:(t + 1) * P, :],
                           waits=w, inc="qf%d" % slot)
            wstt = []
            if t >= 8:
                wstt = [("sp%d" % ((t - 8) % 8), splv[t - 8]),
                        ("w2fq", (t - 8) // 4 + 1)]
            for h2 in range(2):
                si = sbci[0] % 2
                sbci[0] += 1
                pr_s = prg.dma("sync", sbc[:, si, :],
                               s2h_h[t:t + 1, h2 * 512:(h2 + 1) * 512].broadcast_to([P, 512]),
                               waits=[("dqc", sbc_slot[si]), ("s2h", s2h_done)], inc="sb%d" % si)
                sbc_slot[si] = prg.op("vector", lambda e, t=t, h2=h2, slot=slot, si=si: e.scalar_tensor_tensor(
                    out=w2r[:, t % 8, h2 * 512:(h2 + 1) * 512],
                    in0=qf[:, slot, h2 * 512:(h2 + 1) * 512], scalar=-128.0,
                    in1=sbc[:, si, :], op0=ALU.add, op1=ALU.mult),
                    waits=[("sb%d" % si, pr_s), ("qf%d" % slot, pr_q)] + wstt, inc="dqc")
            dq_slot[slot] = C["dqc"]
            dq_t = C["dqc"]
            dqw2t[t] = dq_t
            splv[t] = prg.dma("sync", w2s_h[t], w2r[:, t % 8, :],
                              waits=[("dqc", dq_t)], inc="sp%d" % (t % 8))

        dq_w2 = C["dqc"]

        st1v = [0] * NCH
        ln1v = [0] * NCH
        lnxv = [0] * NCH
        xsmv = [0] * NCH
        xdv = [0] * (NCH + 8)
        st2av = [0] * NCH
        st2ev = [0] * NCH
        ln2v = [0] * NCH
        st2v = [0] * NCH
        xbfv = [0] * NCH
        xtrv = [0] * NCH
        xntv = [0] * NCH
        bcav = [0] * NHB
        bcev = [0] * NHB
        xnnv = [0] * NHB
        qkv_ = [0] * NHB
        sgv = [0] * NHB
        vevv = [0] * NCH
        skrv = [0] * NCH
        smcv = [0] * NCH
        sktv = [0] * NCH
        sctv = [0] * NCH
        gstv = [0] * NCH
        stev = [0] * (NCH + 1)
        attv = [0] * NCH
        cmbv = [0] * NCH
        cbtv = [0] * NCH
        cbev = [0] * NCH
        wozv = [0] * NCH
        yqv = [0] * (NHB * 8 + 1)
        peyv = [0] * (NHB * 8 + 1)
        pezv = [0] * NCH
        zuv = [0] * NCH
        zevv = [0] * NCH
        otv = [0] * NCH
        w2lv = [0] * (NHB * 16 + 1)
        w2fv = [0] * (NHB * 8 + 1)

        ST = stat

        def s_ap(c, i):
            return ST[:, c % 8, i:i + 1]

        def x_load(c):
            if c >= NCH:
                return
            w = []
            if c >= 3:
                w = [("lnx", lnxv[c - 3]), ("st1", st1v[c - 3]),
                     ("xsm", xsmv[c - 3])]
            xdv[c] = prg.dma("sync", xbuf[:, c % 3, :], x_h[c * P:(c + 1) * P, :],
                             waits=w, inc="x%d" % (c % 3))

        def prep_dve_stats(c0):
            c1 = c0 + 1
            bnv = 0
            for c, base in ((c0, 0), (c1, 32)):
                w = [("x%d" % (c % 3), xdv[c])]
                if c >= 8:
                    w.append(("zev", zevv[c - 8]))
                prg.op("vector", lambda e, c=c, base=base: e.bn_stats(
                    out=dump[:, base:base + 6], in_=xbuf[:, c % 3, 0:512]),
                    waits=w, inc="sfv")
                bnv = prg.op("vector", lambda e, c=c, base=base: e.bn_stats(
                    out=dump[:, base + 6:base + 12], in_=xbuf[:, c % 3, 512:1024]),
                    inc="sfv")
            aggv = 0
            for c, base in ((c0, 0), (c1, 32)):
                aggv = prg.op("vector", lambda e, c=c, base=base: e.bn_aggr(
                    out=ST[:, c % 8, 0:2], in_=dump[:, base:base + 12]),
                    waits=[("sfv", bnv)] if base == 0 else (), inc="sfv")
            for c in (c0, c1):
                st1v[c] = prg.op("vector", lambda e, c=c: e.tensor_scalar(
                    out=s_ap(c, 15), in0=s_ap(c, 1), scalar1=EPS_LN,
                    scalar2=None, op0=ALU.add),
                    waits=[("sfv", aggv)] if c == c0 else (), inc="st1")

        def prep_scalar_ln1(c0):
            c1 = c0 + 1
            t1v = 0
            for c in (c0, c1):
                t1v = prg.op("scalar", lambda e, c=c: e.activation(
                    out=s_ap(c, 2), in_=s_ap(c, 15), func=AF.Ln),
                    waits=[("st1", st1v[c])], inc="sfa")
            for c in (c0, c1):
                prg.op("scalar", lambda e, c=c: e.activation(
                    out=s_ap(c, 3), in_=s_ap(c, 2), func=AF.Exp, scale=-0.5),
                    waits=[("sfa", t1v)] if c == c0 else (), inc="ln1")
                ln1v[c] = prg.op("scalar", lambda e, c=c: e.activation(
                    out=s_ap(c, 4), in_=s_ap(c, 2), func=AF.Exp, scale=0.5),
                    inc="ln1")

        def prep_dve_ln1(c0):
            for c in (c0, c0 + 1):
                prg.op("vector", lambda e, c=c: e.tensor_scalar(
                    out=s_ap(c, 5), in0=s_ap(c, 0), scalar1=s_ap(c, 3),
                    scalar2=-1.0, op0=ALU.mult, op1=ALU.mult),
                    waits=[("ln1", ln1v[c])], inc="st2")
                st2av[c] = prg.op("vector", lambda e, c=c: e.tensor_scalar(
                    out=s_ap(c, 6), in0=s_ap(c, 4), scalar1=1.0, scalar2=None,
                    op0=ALU.add), inc="st2")

        def prep_pool_xnf(c0):
            xfv = 0
            for c in (c0, c0 + 1):
                w = [("st2", st2av[c])]
                if c >= 2:
                    w.append(("xsm", xsmv[c - 2]))
                lnxv[c] = prg.op("vector", lambda e, c=c: e.tensor_scalar(
                    out=xnf[:, c % 2, :], in0=xbuf[:, c % 3, :],
                    scalar1=s_ap(c, 3), scalar2=s_ap(c, 5),
                    op0=ALU.mult, op1=ALU.add), waits=w, inc="lnx")
                xfv = lnxv[c]
            for c in (c0, c0 + 1):
                w = [("lnx", xfv)]
                if c >= 4:
                    w.append(("zev", zevv[c - 4]))
                xbfv[c] = prg.op("vector", lambda e, c=c: e.tensor_copy(
                    out=xnb[:, c % 4, :], in_=xnf[:, c % 2, :]),
                    waits=w, inc="xbf")

        def prep_dve_sums(c0):
            for c in (c0, c0 + 1):
                prg.op("vector", lambda e, c=c: e.reduce_sum(
                    out=s_ap(c, 7), in_=xnf[:, c % 2, :], axis=AX.X),
                    waits=[("lnx", lnxv[c])], inc="sfv")
                xsmv[c] = prg.op("vector", lambda e, c=c: e.scalar_tensor_tensor(
                    out=xbuf[:, c % 3, :], in0=xnf[:, c % 2, :], scalar=1.0,
                    in1=xnf[:, c % 2, :], op0=ALU.mult, op1=ALU.mult,
                    accum_out=s_ap(c, 8)), inc="xsm")

        def prep_dve_ln2(c0):
            mmv = 0
            for c in (c0, c0 + 1):
                prg.op("vector", lambda e, c=c: e.tensor_scalar(
                    out=s_ap(c, 9), in0=s_ap(c, 7), scalar1=1.0 / H,
                    scalar2=None, op0=ALU.mult),
                    waits=[("xsm", xsmv[c])])
                mmv = prg.op("vector", lambda e, c=c: e.tensor_scalar(
                    out=s_ap(c, 10), in0=s_ap(c, 7), scalar1=s_ap(c, 7),
                    scalar2=1.0 / (float(H) * H), op0=ALU.mult, op1=ALU.mult),
                    inc="sfv")
            vvv = 0
            for c in (c0, c0 + 1):
                vvv = prg.op("vector", lambda e, c=c: e.scalar_tensor_tensor(
                    out=s_ap(c, 11), in0=s_ap(c, 8), scalar=1.0 / H,
                    in1=s_ap(c, 10), op0=ALU.mult, op1=ALU.subtract),
                    waits=[("sfv", mmv)] if c == c0 else (), inc="sfv")
            for c in (c0, c0 + 1):
                st2ev[c] = prg.op("vector", lambda e, c=c: e.tensor_scalar(
                    out=s_ap(c, 15), in0=s_ap(c, 11), scalar1=EPS_LN,
                    scalar2=None, op0=ALU.add),
                    waits=[("sfv", vvv)] if c == c0 else (), inc="st2")

        def prep_scalar_ln2_late(c0):
            t2v = 0
            for c in (c0, c0 + 1):
                t2v = prg.op("scalar", lambda e, c=c: e.activation(
                    out=s_ap(c, 12), in_=s_ap(c, 15), func=AF.Ln),
                    waits=[("st2", st2ev[c])], inc="sfa")
            for c in (c0, c0 + 1):
                ln2v[c] = prg.op("scalar", lambda e, c=c: e.activation(
                    out=s_ap(c, 13), in_=s_ap(c, 12), func=AF.Exp, scale=-0.5),
                    waits=[("sfa", t2v)] if c == c0 else (), inc="ln1")

        def prep_dve_negm2(c0):
            for c in (c0, c0 + 1):
                st2v[c] = prg.op("vector", lambda e, c=c: e.tensor_scalar(
                    out=s_ap(c, 14), in0=s_ap(c, 9), scalar1=s_ap(c, 13),
                    scalar2=-1.0, op0=ALU.mult, op1=ALU.mult),
                    waits=[("ln1", ln2v[c])], inc="st2")

        def prep_pe_trans(c, hb_p, bnk):
            w = [("xbf", xbfv[c])]
            if hb_p >= 1:
                w.append(("cmbs", cmbv[2 * hb_p - 1]))
            for j in range(8):
                xtrv[c] = prg.op("tensor", lambda e, c=c, j=j, bnk=bnk: e.transpose(
                    out=PSH[:, bnk * 1024 + j * P:bnk * 1024 + (j + 1) * P],
                    in_=xnb[:, c % 4, j * P:(j + 1) * P], identity=idb[:]),
                    waits=w if j == 0 else (),
                    inc="xtr" if j == 7 else None)

        def prep_pe_bcast(hb_p):
            c0p, c1p = 2 * hb_p, 2 * hb_p + 1
            for cc, c in ((0, c0p), (1, c1p)):
                w = [("st2", st2v[c])]
                if cc == 0:
                    w.append(("xnt", xntv[c0p]))
                for i, col in ((0, 13), (1, 14)):
                    bcav[hb_p] = prg.op("tensor", lambda e, c=c, i=i, col=col, cc=cc: e.matmul(
                        out=PS[:, 6 * 512 + (cc * 2 + i) * P:6 * 512 + (cc * 2 + i + 1) * P],
                        lhsT=s_ap(c, col).broadcast_to([P, P]), rhs=idf[:],
                        start=True, stop=True),
                        waits=w if i == 0 else (),
                        inc="bca" if (cc == 1 and i == 1) else None)

        def prep_dve_evac_xnt(c, hb_p, bnk):
            cc = c % 2
            w = [("xtr", xtrv[c])]
            if hb_p >= 2:
                w.append(("qk", qkv_[hb_p - 2]))
            xntv[c] = prg.op("vector", lambda e, hb_p=hb_p, cc=cc, bnk=bnk: e.tensor_copy(
                out=xnT[:, hb_p % 2, :, cc * P:(cc + 1) * P],
                in_=PSH[:, bnk * 1024:(bnk + 1) * 1024].rearrange("p (j t) -> p j t", j=8)),
                waits=w, inc="xnt")

        def prep_dve_evac_bcast(hb_p):
            w = [("bca", bcav[hb_p])]
            if hb_p >= 2:
                w.append(("xnn", xnnv[hb_p - 2]))
            prg.op("vector", lambda e, hb_p=hb_p: e.tensor_copy(
                out=r2sb[:, hb_p % 2, 0, :].rearrange("p (c x) -> p c x", c=2),
                in_=PS[:, 6 * 512:6 * 512 + 512].rearrange("p (c i x) -> p c i x", c=2, i=2)[:, :, 0, :]),
                waits=w)
            bcev[hb_p] = prg.op("vector", lambda e, hb_p=hb_p: e.tensor_copy(
                out=r2sb[:, hb_p % 2, 1, :].rearrange("p (c x) -> p c x", c=2),
                in_=PS[:, 6 * 512:6 * 512 + 512].rearrange("p (c i x) -> p c i x", c=2, i=2)[:, :, 1, :]),
                inc="bce")

        def prep_pool_xnn(hb_p):
            c1 = 2 * hb_p + 1
            w = [("xnt", xntv[c1]), ("bce", bcev[hb_p])]
            if hb_p >= 2:
                w.append(("peyq", 8 * (hb_p - 2) + 8))
            r2rep = r2sb[:, hb_p % 2, 0, :].unsqueeze(1).broadcast_to([P, 8, 256])
            nmrep = r2sb[:, hb_p % 2, 1, :].unsqueeze(1).broadcast_to([P, 8, 256])
            dst = xnnT[:, hb_p % 2, :, :]
            src = xnT[:, hb_p % 2, :, :]
            prg.op("gpsimd", lambda e: e.tensor_tensor(out=dst, in0=src, in1=r2rep,
                                                       op=ALU.mult), waits=w)
            prg.op("gpsimd", lambda e: e.tensor_tensor(out=dst, in0=dst, in1=nmrep,
                                                       op=ALU.add))
            for j in range(8):
                xnnv[hb_p] = prg.op("gpsimd", lambda e, j=j, hb_p=hb_p: e.tensor_scalar(
                    out=xnnT[:, hb_p % 2, j, :], in0=xnnT[:, hb_p % 2, j, :],
                    scalar1=gbc[:, j:j + 1], scalar2=gbc[:, 8 + j:9 + j],
                    op0=ALU.mult, op1=ALU.add),
                    inc="xnn" if j == 7 else None)

        def p1(hb):
            c0, c1 = 2 * hb, 2 * hb + 1
            w0 = [("xnt", xntv[c1]), ("bce", bcev[hb])]
            if hb == 0:
                w0.append(("dqc", dq_wv))
            for t in range(2):
                for j in range(8):
                    qkv_[hb] = prg.op("tensor", lambda e, hb=hb, t=t, j=j: e.matmul(
                        out=PS[:, 6 * 512 + t * 256:6 * 512 + (t + 1) * 256],
                        lhsT=wqT[:, j, t * P:(t + 1) * P], rhs=xnT[:, hb % 2, j, :],
                        start=(j == 0), stop=(j == 7)),
                        waits=w0 if (t == 0 and j == 0) else (),
                        inc="qk" if (t == 1 and j == 7) else None)
            for t in range(2):
                for j in range(8):
                    qkv_[hb] = prg.op("tensor", lambda e, hb=hb, t=t, j=j: e.matmul(
                        out=PS[:, 7 * 512 + t * 256:7 * 512 + (t + 1) * 256],
                        lhsT=wkT[:, j, t * P:(t + 1) * P], rhs=xnT[:, hb % 2, j, :],
                        start=(j == 0), stop=(j == 7)),
                        inc="qk" if (t == 1 and j == 7) else None)
            wv_ = []
            if hb >= 1:
                wv_ = [("zu", zuv[c0 - 2]), ("vev", vevv[c1 - 2])]
            for cc in range(2):
                for j in range(8):
                    qkv_[hb] = prg.op("tensor", lambda e, hb=hb, cc=cc, j=j: e.matmul(
                        out=PS[:, 2 * 512 + cc * 256:2 * 512 + (cc + 1) * 256],
                        lhsT=xnT[:, hb % 2, j, cc * P:(cc + 1) * P], rhs=wvT[:, j, :],
                        start=(j == 0), stop=(j == 7)),
                        waits=wv_ if (cc == 0 and j == 0) else (),
                        inc="qk" if (cc == 1 and j == 7) else None)

        def sigma(hb):
            qdone = qkv_[hb] - 2
            kdone = qkv_[hb] - 1
            for idx, (bnk, dstT, wv) in enumerate(((6, sqT, qdone), (7, skT, kdone))):
                w = [("qk", wv)]
                if hb >= 2:
                    w.append(("attd", attv[2 * hb - 3]))
                prg.op("vector", lambda e, bnk=bnk, idx=idx: e.tensor_scalar(
                    out=elu[:, idx, :], in0=PS[:, bnk * 512:(bnk + 1) * 512],
                    scalar1=0.0, scalar2=None, op0=ALU.min),
                    waits=w, inc="sgm")
                mv = C["sgm"]
                prg.op("scalar", lambda e, idx=idx: e.activation(
                    out=elu[:, idx, :], in_=elu[:, idx, :], func=AF.Exp),
                    waits=[("sgm", mv)], inc="sge")
                ev = C["sge"]
                sgv[hb] = prg.op("vector", lambda e, bnk=bnk, idx=idx, hb=hb, dstT=dstT: e.scalar_tensor_tensor(
                    out=dstT[:, hb % 2, :, :].rearrange("p t x -> p (t x)"),
                    in0=PS[:, bnk * 512:(bnk + 1) * 512], scalar=1.0,
                    in1=elu[:, idx, :], op0=ALU.add, op1=ALU.max),
                    waits=[("sge", ev)], inc="sg")

        def v_evac(hb):
            c0 = 2 * hb
            for cc in range(2):
                c = c0 + cc
                w = [("qk", qkv_[hb])]
                if c >= 4:
                    w.append(("attd", attv[c - 4]))
                prg.op("vector", lambda e, cc=cc, c=c: e.tensor_copy(
                    out=vE[:, c % 4, 0:D],
                    in_=PS[:, 2 * 512 + cc * 256:2 * 512 + (cc + 1) * 256]),
                    waits=w)
                vevv[c] = prg.op("vector", lambda e, cc=cc, c=c: e.tensor_scalar(
                    out=vEg[:, c % 4, 0:D],
                    in0=PS[:, 2 * 512 + cc * 256:2 * 512 + (cc + 1) * 256],
                    scalar1=gcol[:, 1:2], scalar2=None, op0=ALU.mult), inc="vev")


        def attn_full(c, hb):
            cc = c % 2
            zb = 2 if cc == 0 else 4
            w = [("sg", sgv[hb])]
            if c >= 1:
                w.append(("cmbs", cmbv[c - 1]))
            for t in range(2):
                sktv[c] = prg.op("tensor", lambda e, t=t, hb=hb, cc=cc: e.transpose(
                    out=PSH[:, 6 * 1024 + 640 + t * P:6 * 1024 + 640 + (t + 1) * P],
                    in_=skT[:, hb % 2, t, cc * P:(cc + 1) * P], identity=idb[:]),
                    waits=w if t == 0 else (), inc="skt" if t == 1 else None)
            wd = [("skt", sktv[c])]
            if c >= 4:
                wd.append(("gst", gstv[c - 4]))
            skrv[c] = prg.op("vector", lambda e, c=c: e.tensor_copy(
                out=skR[:, c % 4, :], in_=PSH[:, 6 * 1024 + 640:6 * 1024 + 896]),
                waits=wd, inc="skr")
            prg.op("tensor", lambda e, hb=hb, cc=cc: e.matmul(
                out=PS[:, 7 * 512 + 320:7 * 512 + 448],
                lhsT=skT[:, hb % 2, 0, cc * P:(cc + 1) * P],
                rhs=sqT[:, hb % 2, 0, cc * P:(cc + 1) * P],
                start=True, stop=False),
                waits=[("smc", smcv[c - 1])] if c >= 1 else ())
            sctv[c] = prg.op("tensor", lambda e, hb=hb, cc=cc: e.matmul(
                out=PS[:, 7 * 512 + 320:7 * 512 + 448],
                lhsT=skT[:, hb % 2, 1, cc * P:(cc + 1) * P],
                rhs=sqT[:, hb % 2, 1, cc * P:(cc + 1) * P],
                start=False, stop=True), inc="sct")
            wd = [("sct", sctv[c])]
            if c >= 2:
                wd.append(("attd", attv[c - 1]))
            smcv[c] = prg.op("vector", lambda e, c=c: e.tensor_tensor(
                out=smx[:, c % 2, :], in0=PS[:, 7 * 512 + 320:7 * 512 + 448],
                in1=msk[:], op=ALU.mult), waits=wd, inc="smc")
            wg = [("vev", vevv[c]), ("skr", skrv[c])]
            if c >= 1:
                wg.append(("ste", stev[c]))
            if hb >= 1 and cc == 0:
                wg.append(("yq", yqv[8 * hb]))
            if hb == 0 and cc == 0:
                wg.append(("dqc", dq_wo))
            for t in range(2):
                gstv[c] = prg.op("tensor", lambda e, t=t, c=c: e.matmul(
                    out=PS[:, t * 256:(t + 1) * 256],
                    lhsT=skR[:, c % 4, t * P:(t + 1) * P], rhs=vE[:, c % 4, 0:D],
                    start=True, stop=True),
                    waits=wg if t == 0 else (), inc="gst" if t == 1 else None)
            for t in range(2):
                attv[c] = prg.op("tensor", lambda e, t=t, hb=hb, cc=cc: e.matmul(
                    out=PS[:, 7 * 512:7 * 512 + 257],
                    lhsT=sqT[:, hb % 2, t, cc * P:(cc + 1) * P],
                    rhs=mext[:, t, 0:257], start=(t == 0), stop=(t == 1)),
                    waits=[("sfv", sfv_mext)] if (t == 0 and c == 0) else (),
                    inc="attd" if t == 1 else None)
            wi = [("smc", smcv[c])]
            if c >= 1:
                wi.append(("ste", stev[c]))
            for t in range(2):
                prg.op("tensor", lambda e, t=t, hb=hb, cc=cc, c=c: e.matmul(
                    out=PS[:, 6 * 512:6 * 512 + 257],
                    lhsT=sqT[:, hb % 2, t, cc * P:(cc + 1) * P],
                    rhs=stE[:, c % 2, t, 0:257], start=(t == 0), stop=False),
                    waits=wi if t == 0 else ())
            attv[c] = prg.op("tensor", lambda e, c=c: e.matmul(
                out=PS[:, 6 * 512:6 * 512 + 257],
                lhsT=smx[:, c % 2, :], rhs=vEg[:, c % 4, 0:257],
                start=False, stop=True), inc="attd")
            pc, nx = c % 2, (c + 1) % 2
            prg.op("vector", lambda e, pc=pc, nx=nx: e.scalar_tensor_tensor(
                out=kvst[:, nx, 0, :], in0=PS[:, 0:256], scalar=gcol[:, 1:2],
                in1=kvst[:, pc, 0, :], op0=ALU.mult, op1=ALU.add),
                waits=[("gst", gstv[c])], inc="sfv")
            kvv = prg.op("vector", lambda e, pc=pc, nx=nx: e.scalar_tensor_tensor(
                out=kvst[:, nx, 1, :], in0=PS[:, 256:512], scalar=gcol[:, 1:2],
                in1=kvst[:, pc, 1, :], op0=ALU.mult, op1=ALU.add), inc="sfv")
            redv = 0
            for t in range(2):
                redv = prg.op("vector", lambda e, t=t, hb=hb, cc=cc: e.reduce_sum(
                    out=dump[:, 64 + t:65 + t],
                    in_=skT[:, hb % 2, t, cc * P:(cc + 1) * P], axis=AX.X),
                    inc="sfv")
            ksv = 0
            for t in range(2):
                ksv = prg.op("vector", lambda e, t=t, pc=pc, nx=nx: e.tensor_add(
                    out=kss[:, nx * 2 + t:nx * 2 + t + 1],
                    in0=kss[:, pc * 2 + t:pc * 2 + t + 1],
                    in1=dump[:, 64 + t:65 + t]),
                    waits=[("sfv", redv)] if t == 0 else (), inc="sfv")
            wc = [("attd", attv[c])]
            prg.op("vector", lambda e: e.tensor_scalar(
                out=dump[:, 66:67], in0=PS[:, 7 * 512 + 256:7 * 512 + 257],
                scalar1=EPS_ATT, scalar2=None, op0=ALU.max), waits=wc, inc="sfv")
            dav = prg.op("vector", lambda e: e.tensor_scalar(
                out=dump[:, 67:68], in0=PS[:, 6 * 512 + 256:6 * 512 + 257],
                scalar1=EPS_ATT, scalar2=None, op0=ALU.max), inc="sfv")
            rmv = prg.op("vector", lambda e: e.reciprocal(out=dump[:, 68:69], in_=dump[:, 66:67]),
                         waits=[("sfv", dav)], inc="sfv")
            rav = prg.op("vector", lambda e: e.reciprocal(out=dump[:, 69:70], in_=dump[:, 67:68]),
                         inc="sfv")
            t1v = prg.op("vector", lambda e, c=c: e.tensor_scalar(
                out=dump[:, 128:384], in0=PS[:, 7 * 512:7 * 512 + 256],
                scalar1=dump[:, 68:69], scalar2=None, op0=ALU.mult),
                waits=[("sfv", rmv)], inc="sfv")
            prg.op("vector", lambda e, nx=nx: e.tensor_copy(
                out=stE[:, nx, 0:2, 0:D],
                in_=kvst[:, nx, 0:2, :]), waits=[("sfv", kvv)])
            stev[c + 1] = prg.op("vector", lambda e, nx=nx: e.tensor_copy(
                out=stE[:, nx, 0:2, D:D + 1],
                in_=kss[:, nx * 2:nx * 2 + 2].unsqueeze(2)),
                waits=[("sfv", ksv)], inc="ste")
            wcm = []
            if c >= 2:
                wcm.append(("cbt", cbtv[c - 2]))
            cmbv[c] = prg.op("vector", lambda e, c=c: e.scalar_tensor_tensor(
                out=cmb[:, c % 2, :], in0=PS[:, 6 * 512:6 * 512 + 256],
                scalar=dump[:, 69:70], in1=dump[:, 128:384],
                op0=ALU.mult, op1=ALU.add), waits=wcm + [("sfv", t1v)], inc="cmbs")
            for t in range(2):
                cbtv[c] = prg.op("tensor", lambda e, t=t, c=c: e.transpose(
                    out=PSH[:, 1536 + t * P:1536 + (t + 1) * P],
                    in_=cmb[:, c % 2, t * P:(t + 1) * P], identity=idb[:]),
                    waits=[("cmbs", cmbv[c])] if t == 0 else (),
                    inc="cbt" if t == 1 else None)
            wd = [("cbt", cbtv[c])]
            if c >= 2:
                wd.append(("woz", wozv[c - 2]))
            cbev[c] = prg.op("vector", lambda e, c=c: e.tensor_copy(
                out=cbT[:, c % 2, :, :].rearrange("p t x -> p (t x)"),
                in_=PSH[:, 1536:1792]), waits=wd, inc="cbe")
            wz = [("cbe", cbev[c])]
            if c >= 2:
                wz.append(("zu", zuv[c - 2]))
            if c == 0:
                wz.append(("dqc", dq_b2))
            for t in range(2):
                for half in range(2):
                    b = zb + half
                    prg.op("tensor", lambda e, t=t, half=half, c=c, b=b: e.matmul(
                        out=PS[:, b * 512:(b + 1) * 512],
                        lhsT=cbT[:, c % 2, t, :],
                        rhs=woT[:, t, half * 512:(half + 1) * 512],
                        start=(t == 0), stop=False),
                        waits=wz if (t == 0 and half == 0) else ())
            for half in range(2):
                b = zb + half
                wozv[c] = prg.op("tensor", lambda e, half=half, b=b: e.matmul(
                    out=PS[:, b * 512:(b + 1) * 512],
                    lhsT=onesb[0:1, 0:P], rhs=b2r[0:1, half * 512:(half + 1) * 512],
                    start=False, stop=False),
                    inc="woz" if half == 1 else None)

        def w2_load(hb, gidx):
            if hb == 0:
                return
            G0 = hb * 16 + gidx
            t0 = 2 * gidx
            w = []
            if G0 >= 4:
                w.append(("w2fq", (2 * G0 - 7) // 4 + 1))
            if hb == 1:
                w.append(("sp%d" % (t0 % 8), splv[24 + t0 % 8]))
                w.append(("sp%d" % ((t0 + 1) % 8), splv[24 + (t0 + 1) % 8]))
            w2lv[G0] = ("wl%d" % (gidx % 4), prg.dma(
                "sync", w2r[:, (t0 % 8):(t0 % 8) + 2, :],
                w2s_h[t0:t0 + 2].rearrange("t p h -> p t h"),
                waits=w, inc="wl%d" % (gidx % 4)))

        def ffn(hb, do_prep):
            c0, c1 = 2 * hb, 2 * hb + 1
            hb_p = hb + 1

            def fc2_quad(qq, final):
                gq = hb * 8 + qq
                for fi in range(4):
                    ft = 4 * qq + fi
                    G0 = hb * 16 + ft // 2
                    if hb == 0:
                        wf = [("yq", yqv[hb * 8 + qq + 1]), ("dqc", dqw2t[ft])]
                    else:
                        wf = [("yq", yqv[hb * 8 + qq + 1]), w2lv[G0]]
                    for cc2 in range(2):
                        for half in range(2):
                            b = (2 if cc2 == 0 else 4) + half
                            inc = None
                            if final and fi == 3 and half == 1:
                                inc = "pez"
                            elif (not final) and fi == 3 and half == 1 and cc2 == 1:
                                inc = "w2fq"
                            r = prg.op("tensor", lambda e, ft=ft, cc2=cc2, half=half, b=b, final=final, fi=fi: e.matmul(
                                out=PS[:, b * 512:(b + 1) * 512],
                                lhsT=yT[:, ft % 8, cc2 * P:(cc2 + 1) * P],
                                rhs=w2r[:, ft % 8, half * 512:(half + 1) * 512],
                                start=False, stop=(final and fi == 3)),
                                waits=wf if (cc2 == 0 and half == 0) else (),
                                inc=inc)
                            if inc == "pez":
                                pezv[c0 + cc2] = r
                if final:
                    w2fv[gq] = prg.op("tensor", lambda e: e.nop(), inc="w2fq")

            for q in range(8):
                gq = hb * 8 + q
                wq_ = [("xnn", xnnv[hb]), ("cbe", cbev[c1])]
                if hb == 0 and q == 0:
                    wq_.append(("dqc", dq_w1))
                if gq >= 1:
                    wq_.append(("yq", gq))
                if q == 0:
                    wq_.append(("ste", stev[c1 + 1]))
                for fi in range(4):
                    ft = 4 * q + fi
                    b = 0 if fi < 2 else 1
                    off = (fi % 2) * 256
                    for j in range(8):
                        peyv[gq] = prg.op("tensor", lambda e, ft=ft, j=j, b=b, off=off, hb=hb: e.matmul(
                            out=PS[:, b * 512 + off:b * 512 + off + 256],
                            lhsT=w1s[:, j, ft * P:(ft + 1) * P],
                            rhs=xnnT[:, hb % 2, j, :], start=(j == 0), stop=(j == 7)),
                            waits=wq_ if (fi == 0 and j == 0) else (),
                            inc="peyq" if (fi == 3 and j == 7) else None)
                if do_prep:
                    if q == 0:
                        prep_pe_trans(2 * hb_p, hb_p, 6)
                        prep_dve_evac_xnt(2 * hb_p, hb_p, 6)
                    elif q == 1:
                        prep_pe_trans(2 * hb_p + 1, hb_p, 7)
                        prep_dve_evac_xnt(2 * hb_p + 1, hb_p, 7)
                for fi in range(4):
                    ft = 4 * q + fi
                    b = 0 if fi < 2 else 1
                    off = (fi % 2) * 256
                    yqv[gq + 1] = prg.op("scalar", lambda e, ft=ft, b=b, off=off: e.activation(
                        out=yT[:, ft % 8, :], in_=PS[:, b * 512 + off:b * 512 + off + 256],
                        func=AF.Silu, bias=b1c[:, ft:ft + 1]),
                        waits=[("peyq", peyv[gq])] if fi == 0 else (),
                        inc="yq" if fi == 3 else None)
                if 1 <= q <= 6:
                    w2_load(hb, 4 + 2 * (q - 1))
                    w2_load(hb, 4 + 2 * (q - 1) + 1)
                if q >= 1:
                    fc2_quad(q - 1, False)
                    if hb == 0 and q <= 4:
                        for t in range(16 + 4 * (q - 1), 20 + 4 * (q - 1)):
                            dq_w2_tile(t)
                if do_prep and q == 7:
                    prep_scalar_ln2_late(2 * hb_p)
                    prep_dve_negm2(2 * hb_p)
                    prep_pe_bcast(hb_p)
                    prep_dve_evac_bcast(hb_p)
            fc2_quad(7, True)

        def assembly_u(c):
            zb = 2 if c % 2 == 0 else 4
            w = [("pez", pezv[c])]
            if c >= 2:
                w.append(("zev", zevv[c - 2]))
            zuv[c] = prg.op("scalar", lambda e, c=c, zb=zb: e.activation(
                out=ubuf[:, c % 2, :], in_=PS[:, zb * 512:(zb + 2) * 512],
                func=AF.Identity, bias=s_ap(c, 0)), waits=w, inc="zu")

        def assembly_rest(c):
            w = [("zu", zuv[c])]
            if c >= 2:
                w.append(("ot%d" % (c % 2), otv[c - 2]))
            zevv[c] = prg.op("vector", lambda e, c=c: e.scalar_tensor_tensor(
                out=orow[:, c % 2, :], in0=xnb[:, c % 4, :], scalar=s_ap(c, 6),
                in1=ubuf[:, c % 2, :], op0=ALU.mult, op1=ALU.add),
                waits=w, inc="zev")
            otv[c] = prg.dma("sync", out_h[c * P:(c + 1) * P, :], orow[:, c % 2, :],
                             waits=[("zev", zevv[c])], inc="ot%d" % (c % 2))

        for c in range(3):
            x_load(c)
        prep_dve_stats(0)
        prep_scalar_ln1(0)
        prep_dve_ln1(0)
        prep_pool_xnf(0)
        prep_dve_sums(0)
        prep_dve_ln2(0)
        prep_scalar_ln2_late(0)
        prep_dve_negm2(0)
        prep_pe_trans(0, 0, 6)
        prep_dve_evac_xnt(0, 0, 6)
        prep_pe_trans(1, 0, 7)
        prep_dve_evac_xnt(1, 0, 7)
        prep_pe_bcast(0)
        prep_dve_evac_bcast(0)
        prep_pool_xnn(0)

        for hb in range(NHB):
            c0, c1 = 2 * hb, 2 * hb + 1
            x_load(c0 + 3)
            x_load(c0 + 4)
            for gidx in range(4):
                w2_load(hb, gidx)
            if hb >= 2:
                assembly_u(c0 - 2)
                assembly_u(c1 - 2)
            p1(hb)
            sigma(hb)
            if hb + 1 < NHB:
                prep_dve_stats(c0 + 2)
                prep_scalar_ln1(c0 + 2)
            v_evac(hb)
            attn_full(c0, hb)
            if hb >= 2:
                assembly_rest(c0 - 2)
                assembly_rest(c1 - 2)
            if hb + 1 < NHB:
                prep_dve_ln1(c0 + 2)
                prep_pool_xnf(c0 + 2)
                prep_dve_sums(c0 + 2)
                prep_dve_ln2(c0 + 2)
            attn_full(c1, hb)
            if hb == 0:
                for t in range(16):
                    dq_w2_tile(t)
            ffn(hb, do_prep=(hb + 1 < NHB))
            if hb == 0:
                assembly_u(0)
                assembly_u(1)
                assembly_rest(0)
                assembly_rest(1)
            if hb + 1 < NHB:
                prep_pool_xnn(hb + 1)
        assembly_u(NCH - 2)
        assembly_u(NCH - 1)
        assembly_rest(NCH - 2)
        assembly_rest(NCH - 1)

        if DEBUG2:
            prg.dma("sync", w1d_h, w1s[:, :, :].rearrange("p g f -> p (g f)"),
                    inc="pcon")
            prg.dma("sync", w2d_h, w2s_h, inc="pcon")
        if DEBUG:
            prg.dma("sync", dbg_h, dbg[:], waits=[("zev", C["zev"])], inc="pcon")
        prg.op("sync", lambda e: e.nop(),
               waits=[("ot0", C["ot0"]), ("ot1", C["ot1"]), ("pcon", C["pcon"])])

        prg.replay(block)

    return nc


_NC_CACHE = None


def kernel(**inputs):
    global _NC_CACHE
    xs = np.asarray(inputs["L0_attn_hidden_states"], dtype=np.float32)
    B = xs.shape[0]
    ident_b = np.eye(P, dtype=ml_dtypes.bfloat16)
    ident_f = np.eye(P, dtype=np.float32)
    mask = np.triu(np.ones((P, P), dtype=np.float32)).astype(ml_dtypes.bfloat16)

    common = {
        "wq": np.ascontiguousarray(np.asarray(inputs["L0_attn_w_q"], np.float32)),
        "wk": np.ascontiguousarray(np.asarray(inputs["L0_attn_w_k"], np.float32)),
        "wv": np.ascontiguousarray(np.asarray(inputs["L0_attn_w_v"], np.float32)),
        "wo": np.ascontiguousarray(np.asarray(inputs["L0_attn_w_o"], np.float32)),
        "gate": np.ascontiguousarray(np.asarray(inputs["L0_attn_gate"], np.float32)),
        "mem": np.ascontiguousarray(np.asarray(inputs["L0_attn_memory"], np.float32)),
        "mn": np.ascontiguousarray(np.asarray(inputs["L0_attn_memory_norm"], np.float32)),
        "gam": np.ascontiguousarray(np.asarray(inputs["L0_ffn_ln_gamma"], np.float32)),
        "bet": np.ascontiguousarray(np.asarray(inputs["L0_ffn_ln_beta"], np.float32)),
        "q1": np.ascontiguousarray(np.asarray(inputs["L0_ffn_fc1_q"], np.int32)),
        "s1": np.ascontiguousarray(np.asarray(inputs["L0_ffn_fc1_scales"], np.float32)),
        "b1": np.ascontiguousarray(np.asarray(inputs["L0_ffn_fc1_bias"], np.float32)),
        "q2": np.ascontiguousarray(np.asarray(inputs["L0_ffn_fc2_q"], np.int32)),
        "s2": np.ascontiguousarray(np.asarray(inputs["L0_ffn_fc2_scales"], np.float32)),
        "b2": np.ascontiguousarray(np.asarray(inputs["L0_ffn_fc2_bias"], np.float32)),
        "idb_d": ident_b, "idf_d": ident_f, "msk_d": mask,
    }
    in_maps = []
    for b in range(B):
        m = dict(common)
        m["x"] = np.ascontiguousarray(xs[b])
        in_maps.append(m)

    if _NC_CACHE is None:
        _NC_CACHE = build_nc()
    res = run_bass_kernel_spmd(_NC_CACHE, in_maps, core_ids=list(range(B)))
    out = np.stack([np.asarray(r["out"], np.float32) for r in res.results], axis=0)
    if DEBUG:
        global LAST_DBG
        LAST_DBG = {b: np.asarray(res.results[b]["dbg"], np.float32) for b in range(B)}
    if DEBUG2:
        global LAST_DBG2
        LAST_DBG2 = {b: (np.asarray(res.results[b]["w1dump"]),
                         np.asarray(res.results[b]["w2dump"])) for b in range(B)}
    return out


# version 26
# speedup vs baseline: 1.0013x; 1.0013x over previous
import numpy as np
import ml_dtypes
from contextlib import ExitStack

import concourse.bass as bass
import concourse.mybir as mybir
from concourse.bass_utils import run_bass_kernel_spmd

F32 = mybir.dt.float32
BF16 = mybir.dt.bfloat16
I32 = mybir.dt.int32
AF = mybir.ActivationFunctionType
ALU = mybir.AluOpType
AX = mybir.AxisListType

P = 128
S = 4096
H = 1024
D = 256
FF = 4096
NCH = S // P
NHB = NCH // 2
EPS_LN = 1e-5
EPS_ATT = 1e-6

ENGS = ("sync", "scalar", "vector", "tensor", "gpsimd")
DEBUG = False
DEBUG2 = False
LAST_DBG = {}
LAST_DBG2 = {}


class Prog:
    def __init__(self, nc, ctx):
        self.nc = nc
        self.ops = {e: [] for e in ENGS}
        self.sem = {}
        self.cnt = {}
        self.ctx = ctx

    def new_sem(self, name):
        self.sem[name] = self.ctx.enter_context(self.nc.semaphore(name))
        self.cnt[name] = 0

    def op(self, eng, fn, waits=(), inc=None, inc_amt=1):
        waits = tuple((s, v) for (s, v) in waits if v > 0)
        self.ops[eng].append((waits, fn, inc, inc_amt))
        if inc is not None:
            self.cnt[inc] += inc_amt
            return self.cnt[inc]
        return None

    def dma(self, eng, out, in_, waits=(), inc=None, **kw):
        def fn(e, out=out, in_=in_, kw=kw):
            return e.dma_start(out=out, in_=in_, **kw)
        return self.op(eng, fn, waits, inc, 16)

    def replay(self, block):
        for ename in ENGS:
            ops = self.ops[ename]

            def make(ops, sems):
                def body(eng):
                    last = {}
                    for waits, fn, inc, inc_amt in ops:
                        for sname, val in waits:
                            if last.get(sname, 0) < val:
                                eng.wait_ge(sems[sname], val)
                                last[sname] = val
                        inst = fn(eng)
                        if inc is not None:
                            inst.then_inc(sems[inc], inc_amt)
                return body
            getattr(block, ename)(make(ops, self.sem))


def build_nc():
    nc = bass.Bass("TRN2", target_bir_lowering=False, debug=False)

    x_h = nc.dram_tensor("x", [S, H], F32, kind="ExternalInput").ap()
    wq_h = nc.dram_tensor("wq", [D, H], F32, kind="ExternalInput").ap()
    wk_h = nc.dram_tensor("wk", [D, H], F32, kind="ExternalInput").ap()
    wv_h = nc.dram_tensor("wv", [D, H], F32, kind="ExternalInput").ap()
    wo_h = nc.dram_tensor("wo", [H, D], F32, kind="ExternalInput").ap()
    gate_h = nc.dram_tensor("gate", [1], F32, kind="ExternalInput").ap()
    mem_h = nc.dram_tensor("mem", [D, D], F32, kind="ExternalInput").ap()
    mn_h = nc.dram_tensor("mn", [D], F32, kind="ExternalInput").ap()
    gam_h = nc.dram_tensor("gam", [H], F32, kind="ExternalInput").ap()
    bet_h = nc.dram_tensor("bet", [H], F32, kind="ExternalInput").ap()
    q1_h = nc.dram_tensor("q1", [H, FF], I32, kind="ExternalInput").ap()
    s1_h = nc.dram_tensor("s1", [8, FF], F32, kind="ExternalInput").ap()
    b1_h = nc.dram_tensor("b1", [FF], F32, kind="ExternalInput").ap()
    q2_h = nc.dram_tensor("q2", [FF, H], I32, kind="ExternalInput").ap()
    s2_h = nc.dram_tensor("s2", [32, H], F32, kind="ExternalInput").ap()
    b2_h = nc.dram_tensor("b2", [H], F32, kind="ExternalInput").ap()
    idb_h = nc.dram_tensor("idb_d", [P, P], BF16, kind="ExternalInput").ap()
    idf_h = nc.dram_tensor("idf_d", [P, P], F32, kind="ExternalInput").ap()
    msk_h = nc.dram_tensor("msk_d", [P, P], BF16, kind="ExternalInput").ap()

    out_h = nc.dram_tensor("out", [S, H], F32, kind="ExternalOutput").ap()
    dbg_h = nc.dram_tensor("dbg", [P, 4096], F32, kind="ExternalOutput").ap() if DEBUG else None
    w1d_h = nc.dram_tensor("w1dump", [P, 8 * FF], BF16, kind="ExternalOutput").ap() if DEBUG2 else None
    w2d_h = nc.dram_tensor("w2dump", [32, P, H], BF16, kind="ExternalOutput").ap() if DEBUG2 else None
    w2s_h = nc.dram_tensor("w2scr", [32, P, H], BF16, kind="Internal").ap()
    s2h_h = nc.dram_tensor("s2half", [32, H], F32, kind="Internal").ap()

    with ExitStack() as ctx:
        def sb(name, shape, dtype):
            return ctx.enter_context(nc.sbuf_tensor(name, shape, dtype))

        w1s = sb("w1s", [P, 8, FF], BF16)
        w2r = sb("w2r", [P, 8, H], BF16)
        wqT = sb("wqT", [P, 8, D], BF16)
        wkT = sb("wkT", [P, 8, D], BF16)
        wvT = sb("wvT", [P, 8, D], BF16)
        woT = sb("woT", [P, 2, H], BF16)
        idb = sb("idb", [P, P], BF16)
        idf = sb("idf", [P, P], F32)
        msk = sb("msk", [P, P], BF16)
        xbuf = sb("xbuf", [P, 3, H], F32)
        xnf = sb("xnf", [P, 2, H], F32)
        ov1 = sb("ov1", [P, 2, H], F32)
        ov2 = sb("ov2", [P, 2, H], F32)
        ov3 = sb("ov3", [P, 2, H], F32)
        ov4 = sb("ov4", [P, 2, H], BF16)
        ubuf = ov1[:]
        orow = ov2[:]
        wst = ov3[:]
        _ov3b = ov3[:, 0:2, :].rearrange("p t h -> p (t h)").bitcast(BF16)
        yT = _ov3b[:, 0:2048].rearrange("p (f x) -> p f x", f=8)
        r2sb = _ov3b[:, 2048:3072].rearrange("p (a i x) -> p a i x", a=2, i=2)
        elu = _ov3b[:, 3072:4096].rearrange("p (i x) -> p i x", i=2)
        wstb = ov4[:]
        _ov4b = ov4[:, 0:2, :].rearrange("p t h -> p (t h)")
        skR = _ov4b[:, 0:1024].rearrange("p (a x) -> p a x", a=4)
        cmb = _ov4b[:, 1024:1536].rearrange("p (a x) -> p a x", a=2)
        cbT = _ov4b[:, 1536:2048].rearrange("p (a t x) -> p a t x", a=2, t=2)
        dump = sb("dump", [P, 384], F32)
        xnb = sb("xnb", [P, 4, H], BF16)
        xnT = sb("xnT", [P, 2, 8, D], BF16)
        xnnT = sb("xnnT", [P, 2, 8, D], BF16)
        sqT = sb("sqT", [P, 2, 2, D], BF16)
        skT = sb("skT", [P, 2, 2, D], BF16)
        vE = sb("vE", [P, 4, 258], BF16)
        vEg = sb("vEg", [P, 4, 258], BF16)
        smx = sb("smx", [P, 2, P], BF16)
        stat = sb("stat", [P, 8, 16], F32)
        kvst = sb("kvst", [P, 2, 2, D], F32)
        kss = sb("kss", [P, 4], F32)
        stE = sb("stE", [P, 2, 2, 258], BF16)
        mext = sb("mext", [P, 2, 258], BF16)
        mncol = sb("mncol", [P, 2], F32)
        b1c = sb("b1c", [P, 32], F32)
        sbc = sb("sbc", [P, 2, 512], F32)
        bstg = sbc[0:1, 0:2, :].rearrange("o t h -> o (t h)")
        b2r = sb("b2r", [1, H], BF16)
        onesb = sb("onesb", [1, 512], BF16)
        onesf = sb("onesf", [1, P], F32)
        halvesf = sb("halvesf", [1, P], F32)
        qf = sb("qf", [P, 2, 1024], I32)
        qff = qf[:, :, :].bitcast(F32)
        memstg = qff[:, 1, 0:2 * D].rearrange("p (t e) -> p t e", t=2)
        grow = sb("grow", [1, 16], F32)
        gcol = sb("gcol", [P, 4], F32)
        mnr = sb("mnr", [1, D], F32)
        gbc = sb("gbc", [P, 16], F32)
        dbg = w1s[:, 6:8, :].rearrange("p g f -> p (g f)").bitcast(F32) if DEBUG else None

        PS_t = ctx.enter_context(nc.psum_tensor("ps", [P, 8 * 512], F32))
        PS = PS_t[:]
        PSH = PS.bitcast(BF16)

        prg = Prog(nc, ctx)
        for s in ("x0", "x1", "x2", "ot0", "ot1", "wl0", "wl1", "wl2", "wl3",
                  "sp0", "sp1", "sp2", "sp3", "sp4", "sp5", "sp6", "sp7",
                  "qf0", "qf1", "sb0", "sb1", "pcon", "pmem", "pb1", "pgb",
                  "pb2", "ps2", "pwst", "w2fq", "dqc", "prc", "qf2", "qf3",
                  "st1", "ln1", "lnx", "st2", "xbf", "xtr", "xnt",
                  "bca", "bce", "xnn", "qk", "sgm", "sge", "sg",
                  "vev", "skt", "skr", "sct", "smc", "gst", "ste",
                  "attd", "cmbs", "cbt", "cbe", "yq", "peyq", "pez",
                  "zu", "zev", "woz", "sfv", "sfa", "xsm", "s2h"):
            prg.new_sem(s)
        C = prg.cnt

        block = ctx.enter_context(nc.Block())

        prg.dma("sync", idb[:], idb_h, inc="pcon")
        prg.dma("sync", idf[:], idf_h, inc="pcon")
        prg.dma("sync", msk[:], msk_h, inc="pcon")
        prg.dma("sync", grow[0:1, 0:1], gate_h.unsqueeze(0), inc="pcon")
        prg.dma("sync", mnr[:], mn_h.unsqueeze(0), inc="pcon")
        pre_consts = C["pcon"]
        prg.dma("sync", mncol[:, 0:1], mn_h[0:P].unsqueeze(1), inc="pmem")
        prg.dma("sync", mncol[:, 1:2], mn_h[P:D].unsqueeze(1), inc="pmem")
        prg.dma("sync", memstg,
                mem_h.rearrange("(t p) e -> p t e", p=P), inc="pmem")
        pre_mem = C["pmem"]
        prg.dma("gpsimd", qff[0:32, 0, 0:P], b1_h.rearrange("(a p) -> a p", p=P),
                inc="pb1")
        pre_b1 = C["pb1"]
        prg.op("tensor", lambda e: e.transpose(out=PS[:, 0:32], in_=qff[0:32, 0, 0:P],
                                               identity=idf[0:32, 0:32]),
               waits=[("pb1", pre_b1), ("pcon", pre_consts)], inc="prc")
        prc_b1 = C["prc"]
        prg.op("vector", lambda e: e.tensor_copy(out=b1c[:], in_=PS[:, 0:32]),
               waits=[("prc", prc_b1)], inc="dqc")
        dq_b1 = C["dqc"]
        prg.dma("sync", bstg[0:1, 0:H], b2_h.unsqueeze(0),
                waits=[("dqc", dq_b1)], inc="pb2")
        pre_b2 = C["pb2"]
        prg.op("vector", lambda e: e.tensor_scalar(
            out=b2r[0:1, 0:H], in0=bstg[0:1, 0:H], scalar1=0.5, scalar2=None,
            op0=ALU.mult), waits=[("pb2", pre_b2)], inc="dqc")
        dq_b2 = C["dqc"]
        for fn in (lambda e: e.memset(onesf[:], 1.0),
                   lambda e: e.memset(halvesf[:], 0.5),
                   lambda e: e.memset(onesb[:], 1.0),
                   lambda e: e.memset(kvst[:, :, :, :].rearrange("p a t x -> p (a t x)"), 0.0),
                   lambda e: e.memset(kss[:], 0.0),
                   lambda e: e.memset(stE[:, :, :, :].rearrange("p a t x -> p (a t x)"), 0.0),
                   lambda e: e.memset(vE[:, :, :].rearrange("p a x -> p (a x)"), 1.0),
                   lambda e: e.memset(vEg[:, :, :].rearrange("p a x -> p (a x)"), 1.0)):
            prg.op("vector", fn, inc="dqc")
        dq_memset = C["dqc"]

        prg.dma("gpsimd", qff[0:8, 0, 0:P], gam_h.rearrange("(g p) -> g p", p=P),
                waits=[("prc", prc_b1)], inc="pgb")
        prg.dma("gpsimd", qff[0:8, 0, P:2 * P], bet_h.rearrange("(g p) -> g p", p=P),
                inc="pgb")
        pre_gb = C["pgb"]
        prg.op("vector", lambda e: e.tensor_copy(out=wstb[0:8, 0, 0:2 * P],
                                                 in_=qff[0:8, 0, 0:2 * P]),
               waits=[("pgb", pre_gb)], inc="dqc")
        dq_gbc = C["dqc"]
        prg.op("tensor", lambda e: e.transpose(out=PSH[:, 0:8],
                                               in_=wstb[0:8, 0, 0:P],
                                               identity=idb[0:8, 0:8]),
               waits=[("dqc", dq_gbc), ("pcon", pre_consts)], inc="prc")
        prg.op("tensor", lambda e: e.transpose(out=PSH[:, 8:16],
                                               in_=wstb[0:8, 0, P:2 * P],
                                               identity=idb[0:8, 0:8]), inc="prc")
        prc_gb = C["prc"]
        prg.op("vector", lambda e: e.tensor_copy(out=gbc[:, 0:16], in_=PSH[:, 0:16]),
               waits=[("prc", prc_gb)], inc="dqc")
        dq_gb2 = C["dqc"]

        prg.op("scalar", lambda e: e.activation(out=grow[0:1, 1:2], in_=grow[0:1, 0:1],
                                                func=AF.Tanh, scale=0.5),
               waits=[("pcon", pre_consts)], inc="sfa")
        sfa_t = C["sfa"]
        prg.op("scalar", lambda e: e.activation(out=grow[0:1, 2:3], in_=grow[0:1, 1:2],
                                                func=AF.Copy, scale=0.5, bias=0.5),
               waits=[("sfa", sfa_t)], inc="sfa")
        sfa_g = C["sfa"]
        prg.op("vector", lambda e: e.reduce_sum(out=grow[0:1, 6:7], in_=mnr[:], axis=AX.X),
               waits=[("pcon", pre_consts)], inc="sfv")
        sfv_mn = C["sfv"]
        prg.op("vector", lambda e: e.tensor_single_scalar(
            out=grow[0:1, 3:4], in_=grow[0:1, 6:7], scalar=EPS_ATT, op=ALU.is_ge),
            waits=[("sfv", sfv_mn), ("sfa", sfa_g)], inc="sfv")
        sfv_f = C["sfv"]
        prg.op("vector", lambda e: e.tensor_mul(out=grow[0:1, 4:5], in0=grow[0:1, 2:3],
                                                in1=grow[0:1, 3:4]),
               waits=[("sfv", sfv_f)], inc="sfv")
        sfv_ge = C["sfv"]
        prg.op("vector", lambda e: e.tensor_scalar(
            out=grow[0:1, 5:6], in0=grow[0:1, 2:3], scalar1=-1.0, scalar2=1.0,
            op0=ALU.mult, op1=ALU.add), waits=[("sfv", sfv_ge)], inc="sfv")
        sfv_g1 = C["sfv"]
        prg.op("tensor", lambda e: e.matmul(out=PS[:, 0:2], lhsT=onesf[0:1, :],
                                            rhs=grow[0:1, 4:6], start=True, stop=True),
               waits=[("sfv", sfv_g1), ("dqc", dq_memset)], inc="prc")
        prc_g = C["prc"]
        prg.op("vector", lambda e: e.tensor_copy(out=gcol[:, 0:2], in_=PS[:, 0:2]),
               waits=[("prc", prc_g)], inc="sfv")
        sfv_gc = C["sfv"]
        prg.op("vector", lambda e: e.tensor_scalar(
            out=mext[:, 0:2, 0:D],
            in0=memstg,
            scalar1=gcol[:, 0:1], scalar2=None, op0=ALU.mult),
            waits=[("sfv", sfv_gc), ("pmem", pre_mem)], inc="sfv")
        prg.op("vector", lambda e: e.tensor_copy(
            out=mext[:, 0:2, D:D + 1],
            in_=mncol[:, 0:2].unsqueeze(2)), waits=[("pmem", pre_mem)], inc="sfv")
        sfv_mext = C["sfv"]

        def wload(src_h, dst):
            dmaw = [("dqc", C["dqc"]), ("sfv", sfv_mext)]
            prg.dma("gpsimd", wst[:, 0:2, :],
                    src_h.rearrange("(t p) h -> p t h", p=P),
                    waits=dmaw, inc="pwst")
            pr = C["pwst"]
            prg.op("vector", lambda e: e.tensor_copy(
                out=wstb[:, 0:2, :].rearrange("p t h -> p (t h)"),
                in_=wst[:, 0:2, :].rearrange("p t h -> p (t h)")),
                waits=[("pwst", pr), ("prc", C["prc"])], inc="dqc")
            dq = C["dqc"]
            for j in range(8):
                for t in range(2):
                    prg.op("tensor", lambda e, j=j, t=t: e.transpose(
                        out=PSH[:, (2 * j + t) * P:(2 * j + t + 1) * P],
                        in_=wstb[:, t, j * P:(j + 1) * P], identity=idb[:]),
                        waits=[("dqc", dq)] if (j == 0 and t == 0) else (),
                        inc="prc")
            pr2 = C["prc"]
            prg.op("vector", lambda e, dst=dst: e.tensor_copy(
                out=dst[:, :, :].rearrange("p j d -> p (j d)"),
                in_=PSH[:, 0:2048]),
                waits=[("prc", pr2)], inc="dqc")
            return C["dqc"]

        dq_wq = wload(wq_h, wqT)
        dq_wk = wload(wk_h, wkT)
        dq_wv = wload(wv_h, wvT)

        def wo_half(li):
            dmaw = [("dqc", C["dqc"])]
            prg.dma("gpsimd", wst[:, 0, :].rearrange("p (a d) -> p a d", a=4),
                    wo_h.rearrange("(a p) d -> p a d", p=P)[:, 4 * li:4 * li + 4, :],
                    waits=dmaw, inc="pwst")
            pr = C["pwst"]
            prg.op("vector", lambda e: e.tensor_copy(
                out=wstb[:, 0, :], in_=wst[:, 0, :]),
                waits=[("pwst", pr), ("prc", C["prc"])], inc="dqc")
            dq = C["dqc"]
            flat = wstb[:, 0, :]
            for a in range(4):
                for t in range(2):
                    prg.op("tensor", lambda e, a=a, t=t: e.transpose(
                        out=PSH[:, (2 * a + t) * P:(2 * a + t + 1) * P],
                        in_=flat[:, a * 256 + t * P:a * 256 + (t + 1) * P],
                        identity=idb[:]),
                        waits=[("dqc", dq)] if (a == 0 and t == 0) else (),
                        inc="prc")
            pr2 = C["prc"]
            prg.op("vector", lambda e, li=li: e.tensor_copy(
                out=woT[:, 0:2, li * 512:(li + 1) * 512].rearrange("p t (a x) -> p t a x", a=4),
                in_=PSH[:, 0:1024].rearrange("p (a t x) -> p t a x", a=4, t=2, x=P)),
                waits=[("prc", pr2)], inc="dqc")
            return C["dqc"]

        dq_wo = wo_half(0)
        dq_wo = wo_half(1)

        prg.dma("gpsimd", qff[0:32, 1, :], s2_h,
                waits=[("sfv", sfv_mext), ("dqc", dq_wo)], inc="ps2")
        pre_s2 = C["ps2"]
        prg.op("vector", lambda e: e.tensor_scalar(
            out=qff[0:32, 1, :], in0=qff[0:32, 1, :], scalar1=0.5, scalar2=None,
            op0=ALU.mult), waits=[("ps2", pre_s2)], inc="dqc")
        dq_s2 = C["dqc"]
        prg.dma("gpsimd", s2h_h, qff[0:32, 1, :], waits=[("dqc", dq_s2)], inc="s2h")
        s2h_done = C["s2h"]

        dq_half = [dq_s2] * 4
        sbc_slot = [dq_b2, dq_b2]
        sbci = [0]
        hfi = [0]

        def dq_w1_group(g):
            for qt in range(4):
                for h2 in range(2):
                    hs = hfi[0] % 4
                    hfi[0] += 1
                    slot, hh = hs // 2, hs % 2
                    w = [("dqc", dq_half[hs])]
                    if g == 0 and qt == 0 and h2 == 0:
                        w.append(("prc", prc_b1))
                    if g == 0 and qt == 1:
                        w.append(("s2h", s2h_done))
                    pr_q = prg.dma("sync", qf[:, slot, hh * 512:(hh + 1) * 512],
                                   q1_h[g * P:(g + 1) * P, qt * 1024 + h2 * 512:qt * 1024 + (h2 + 1) * 512],
                                   waits=w, inc="qf%d" % hs)
                    si = sbci[0] % 2
                    sbci[0] += 1
                    pr_s = prg.dma("sync" if si == 0 else "scalar", sbc[:, si, :],
                                   s1_h[g:g + 1, qt * 1024 + h2 * 512:qt * 1024 + (h2 + 1) * 512].broadcast_to([P, 512]),
                                   waits=[("dqc", sbc_slot[si])], inc="sb%d" % si)
                    sbc_slot[si] = prg.op("vector", lambda e, g=g, qt=qt, h2=h2, slot=slot, hh=hh, si=si: e.scalar_tensor_tensor(
                        out=w1s[:, g, qt * 1024 + h2 * 512:qt * 1024 + (h2 + 1) * 512],
                        in0=qf[:, slot, hh * 512:(hh + 1) * 512], scalar=-128.0,
                        in1=sbc[:, si, :], op0=ALU.add, op1=ALU.mult),
                        waits=[("sb%d" % si, pr_s), ("qf%d" % hs, pr_q)], inc="dqc")
                    dq_half[hs] = C["dqc"]

        for g in range(8):
            dq_w1_group(g)
        dq_w1 = C["dqc"]

        splv = [0] * 33
        dqw2t = [0] * 32

        def dq_w2_tile(t):
            wstt = []
            if t >= 8:
                wstt = [("sp%d" % ((t - 8) % 8), splv[t - 8]),
                        ("w2fq", (t - 8) // 4 + 1)]
            for h2 in range(2):
                hs = hfi[0] % 4
                hfi[0] += 1
                slot, hh = hs // 2, hs % 2
                pr_q = prg.dma("sync", qf[:, slot, hh * 512:(hh + 1) * 512],
                               q2_h[t * P:(t + 1) * P, h2 * 512:(h2 + 1) * 512],
                               waits=[("dqc", dq_half[hs])], inc="qf%d" % hs)
                si = sbci[0] % 2
                sbci[0] += 1
                pr_s = prg.dma("sync", sbc[:, si, :],
                               s2h_h[t:t + 1, h2 * 512:(h2 + 1) * 512].broadcast_to([P, 512]),
                               waits=[("dqc", sbc_slot[si]), ("s2h", s2h_done)], inc="sb%d" % si)
                sbc_slot[si] = prg.op("vector", lambda e, t=t, h2=h2, slot=slot, hh=hh, si=si: e.scalar_tensor_tensor(
                    out=w2r[:, t % 8, h2 * 512:(h2 + 1) * 512],
                    in0=qf[:, slot, hh * 512:(hh + 1) * 512], scalar=-128.0,
                    in1=sbc[:, si, :], op0=ALU.add, op1=ALU.mult),
                    waits=[("sb%d" % si, pr_s), ("qf%d" % hs, pr_q)] + wstt, inc="dqc")
                dq_half[hs] = C["dqc"]
            dq_t = C["dqc"]
            dqw2t[t] = dq_t
            splv[t] = prg.dma("sync", w2s_h[t], w2r[:, t % 8, :],
                              waits=[("dqc", dq_t)], inc="sp%d" % (t % 8))

        dq_w2 = C["dqc"]

        st1v = [0] * NCH
        ln1v = [0] * NCH
        lnxv = [0] * NCH
        xsmv = [0] * NCH
        xdv = [0] * (NCH + 8)
        st2av = [0] * NCH
        st2ev = [0] * NCH
        ln2v = [0] * NCH
        st2v = [0] * NCH
        xbfv = [0] * NCH
        xtrv = [0] * NCH
        xntv = [0] * NCH
        bcav = [0] * NHB
        bcev = [0] * NHB
        xnnv = [0] * NHB
        qkv_ = [0] * NHB
        sgv = [0] * NHB
        vevv = [0] * NCH
        skrv = [0] * NCH
        smcv = [0] * NCH
        sktv = [0] * NCH
        sctv = [0] * NCH
        gstv = [0] * NCH
        stev = [0] * (NCH + 1)
        attv = [0] * NCH
        cmbv = [0] * NCH
        cbtv = [0] * NCH
        cbev = [0] * NCH
        wozv = [0] * NCH
        yqv = [0] * (NHB * 8 + 1)
        peyv = [0] * (NHB * 8 + 1)
        pezv = [0] * NCH
        zuv = [0] * NCH
        zevv = [0] * NCH
        otv = [0] * NCH
        w2lv = [0] * (NHB * 16 + 1)
        w2fv = [0] * (NHB * 8 + 1)

        ST = stat

        def s_ap(c, i):
            return ST[:, c % 8, i:i + 1]

        def x_load(c):
            if c >= NCH:
                return
            w = []
            if c >= 3:
                w = [("lnx", lnxv[c - 3]), ("st1", st1v[c - 3]),
                     ("xsm", xsmv[c - 3])]
            xdv[c] = prg.dma("sync", xbuf[:, c % 3, :], x_h[c * P:(c + 1) * P, :],
                             waits=w, inc="x%d" % (c % 3))

        def prep_dve_stats(c0):
            c1 = c0 + 1
            bnv = 0
            for c, base in ((c0, 0), (c1, 32)):
                w = [("x%d" % (c % 3), xdv[c])]
                if c >= 8:
                    w.append(("zev", zevv[c - 8]))
                prg.op("vector", lambda e, c=c, base=base: e.bn_stats(
                    out=dump[:, base:base + 6], in_=xbuf[:, c % 3, 0:512]),
                    waits=w, inc="sfv")
                bnv = prg.op("vector", lambda e, c=c, base=base: e.bn_stats(
                    out=dump[:, base + 6:base + 12], in_=xbuf[:, c % 3, 512:1024]),
                    inc="sfv")
            aggv = 0
            for c, base in ((c0, 0), (c1, 32)):
                aggv = prg.op("vector", lambda e, c=c, base=base: e.bn_aggr(
                    out=ST[:, c % 8, 0:2], in_=dump[:, base:base + 12]),
                    waits=[("sfv", bnv)] if base == 0 else (), inc="sfv")
            for c in (c0, c1):
                st1v[c] = prg.op("vector", lambda e, c=c: e.tensor_scalar(
                    out=s_ap(c, 15), in0=s_ap(c, 1), scalar1=EPS_LN,
                    scalar2=None, op0=ALU.add),
                    waits=[("sfv", aggv)] if c == c0 else (), inc="st1")

        def prep_scalar_ln1(c0):
            c1 = c0 + 1
            t1v = 0
            for c in (c0, c1):
                t1v = prg.op("scalar", lambda e, c=c: e.activation(
                    out=s_ap(c, 2), in_=s_ap(c, 15), func=AF.Ln),
                    waits=[("st1", st1v[c])], inc="sfa")
            for c in (c0, c1):
                prg.op("scalar", lambda e, c=c: e.activation(
                    out=s_ap(c, 3), in_=s_ap(c, 2), func=AF.Exp, scale=-0.5),
                    waits=[("sfa", t1v)] if c == c0 else (), inc="ln1")
                ln1v[c] = prg.op("scalar", lambda e, c=c: e.activation(
                    out=s_ap(c, 4), in_=s_ap(c, 2), func=AF.Exp, scale=0.5),
                    inc="ln1")

        def prep_dve_ln1(c0):
            for c in (c0, c0 + 1):
                prg.op("vector", lambda e, c=c: e.tensor_scalar(
                    out=s_ap(c, 5), in0=s_ap(c, 0), scalar1=s_ap(c, 3),
                    scalar2=-1.0, op0=ALU.mult, op1=ALU.mult),
                    waits=[("ln1", ln1v[c])], inc="st2")
                st2av[c] = prg.op("vector", lambda e, c=c: e.tensor_scalar(
                    out=s_ap(c, 6), in0=s_ap(c, 4), scalar1=1.0, scalar2=None,
                    op0=ALU.add), inc="st2")

        def prep_pool_xnf(c0):
            xfv = 0
            for c in (c0, c0 + 1):
                w = [("st2", st2av[c])]
                if c >= 2:
                    w.append(("xsm", xsmv[c - 2]))
                lnxv[c] = prg.op("vector", lambda e, c=c: e.tensor_scalar(
                    out=xnf[:, c % 2, :], in0=xbuf[:, c % 3, :],
                    scalar1=s_ap(c, 3), scalar2=s_ap(c, 5),
                    op0=ALU.mult, op1=ALU.add), waits=w, inc="lnx")
                xfv = lnxv[c]
            for c in (c0, c0 + 1):
                w = [("lnx", xfv)]
                if c >= 4:
                    w.append(("zev", zevv[c - 4]))
                xbfv[c] = prg.op("vector", lambda e, c=c: e.tensor_copy(
                    out=xnb[:, c % 4, :], in_=xnf[:, c % 2, :]),
                    waits=w, inc="xbf")

        def prep_dve_sums(c0):
            for c in (c0, c0 + 1):
                prg.op("vector", lambda e, c=c: e.reduce_sum(
                    out=s_ap(c, 7), in_=xnf[:, c % 2, :], axis=AX.X),
                    waits=[("lnx", lnxv[c])], inc="sfv")
                xsmv[c] = prg.op("vector", lambda e, c=c: e.scalar_tensor_tensor(
                    out=xbuf[:, c % 3, :], in0=xnf[:, c % 2, :], scalar=1.0,
                    in1=xnf[:, c % 2, :], op0=ALU.mult, op1=ALU.mult,
                    accum_out=s_ap(c, 8)), inc="xsm")

        def prep_dve_ln2(c0):
            mmv = 0
            for c in (c0, c0 + 1):
                prg.op("vector", lambda e, c=c: e.tensor_scalar(
                    out=s_ap(c, 9), in0=s_ap(c, 7), scalar1=1.0 / H,
                    scalar2=None, op0=ALU.mult),
                    waits=[("xsm", xsmv[c])])
                mmv = prg.op("vector", lambda e, c=c: e.tensor_scalar(
                    out=s_ap(c, 10), in0=s_ap(c, 7), scalar1=s_ap(c, 7),
                    scalar2=1.0 / (float(H) * H), op0=ALU.mult, op1=ALU.mult),
                    inc="sfv")
            vvv = 0
            for c in (c0, c0 + 1):
                vvv = prg.op("vector", lambda e, c=c: e.scalar_tensor_tensor(
                    out=s_ap(c, 11), in0=s_ap(c, 8), scalar=1.0 / H,
                    in1=s_ap(c, 10), op0=ALU.mult, op1=ALU.subtract),
                    waits=[("sfv", mmv)] if c == c0 else (), inc="sfv")
            for c in (c0, c0 + 1):
                st2ev[c] = prg.op("vector", lambda e, c=c: e.tensor_scalar(
                    out=s_ap(c, 15), in0=s_ap(c, 11), scalar1=EPS_LN,
                    scalar2=None, op0=ALU.add),
                    waits=[("sfv", vvv)] if c == c0 else (), inc="st2")

        def prep_scalar_ln2_late(c0):
            t2v = 0
            for c in (c0, c0 + 1):
                t2v = prg.op("scalar", lambda e, c=c: e.activation(
                    out=s_ap(c, 12), in_=s_ap(c, 15), func=AF.Ln),
                    waits=[("st2", st2ev[c])], inc="sfa")
            for c in (c0, c0 + 1):
                ln2v[c] = prg.op("scalar", lambda e, c=c: e.activation(
                    out=s_ap(c, 13), in_=s_ap(c, 12), func=AF.Exp, scale=-0.5),
                    waits=[("sfa", t2v)] if c == c0 else (), inc="ln1")

        def prep_dve_negm2(c0):
            for c in (c0, c0 + 1):
                st2v[c] = prg.op("vector", lambda e, c=c: e.tensor_scalar(
                    out=s_ap(c, 14), in0=s_ap(c, 9), scalar1=s_ap(c, 13),
                    scalar2=-1.0, op0=ALU.mult, op1=ALU.mult),
                    waits=[("ln1", ln2v[c])], inc="st2")

        def prep_pe_trans(c, hb_p, bnk):
            w = [("xbf", xbfv[c])]
            if hb_p >= 1:
                w.append(("cmbs", cmbv[2 * hb_p - 1]))
            for j in range(8):
                xtrv[c] = prg.op("tensor", lambda e, c=c, j=j, bnk=bnk: e.transpose(
                    out=PSH[:, bnk * 1024 + j * P:bnk * 1024 + (j + 1) * P],
                    in_=xnb[:, c % 4, j * P:(j + 1) * P], identity=idb[:]),
                    waits=w if j == 0 else (),
                    inc="xtr" if j == 7 else None)

        def prep_pe_bcast(hb_p):
            c0p, c1p = 2 * hb_p, 2 * hb_p + 1
            for cc, c in ((0, c0p), (1, c1p)):
                w = [("st2", st2v[c])]
                if cc == 0:
                    w.append(("xnt", xntv[c0p]))
                for i, col in ((0, 13), (1, 14)):
                    bcav[hb_p] = prg.op("tensor", lambda e, c=c, i=i, col=col, cc=cc: e.matmul(
                        out=PS[:, 6 * 512 + (cc * 2 + i) * P:6 * 512 + (cc * 2 + i + 1) * P],
                        lhsT=s_ap(c, col).broadcast_to([P, P]), rhs=idf[:],
                        start=True, stop=True),
                        waits=w if i == 0 else (),
                        inc="bca" if (cc == 1 and i == 1) else None)

        def prep_dve_evac_xnt(c, hb_p, bnk):
            cc = c % 2
            w = [("xtr", xtrv[c])]
            if hb_p >= 2:
                w.append(("qk", qkv_[hb_p - 2]))
            xntv[c] = prg.op("vector", lambda e, hb_p=hb_p, cc=cc, bnk=bnk: e.tensor_copy(
                out=xnT[:, hb_p % 2, :, cc * P:(cc + 1) * P],
                in_=PSH[:, bnk * 1024:(bnk + 1) * 1024].rearrange("p (j t) -> p j t", j=8)),
                waits=w, inc="xnt")

        def prep_dve_evac_bcast(hb_p):
            w = [("bca", bcav[hb_p])]
            if hb_p >= 2:
                w.append(("xnn", xnnv[hb_p - 2]))
            prg.op("vector", lambda e, hb_p=hb_p: e.tensor_copy(
                out=r2sb[:, hb_p % 2, 0, :].rearrange("p (c x) -> p c x", c=2),
                in_=PS[:, 6 * 512:6 * 512 + 512].rearrange("p (c i x) -> p c i x", c=2, i=2)[:, :, 0, :]),
                waits=w)
            bcev[hb_p] = prg.op("vector", lambda e, hb_p=hb_p: e.tensor_copy(
                out=r2sb[:, hb_p % 2, 1, :].rearrange("p (c x) -> p c x", c=2),
                in_=PS[:, 6 * 512:6 * 512 + 512].rearrange("p (c i x) -> p c i x", c=2, i=2)[:, :, 1, :]),
                inc="bce")

        def prep_pool_xnn(hb_p):
            c1 = 2 * hb_p + 1
            w = [("xnt", xntv[c1]), ("bce", bcev[hb_p])]
            if hb_p >= 2:
                w.append(("peyq", 8 * (hb_p - 2) + 8))
            r2rep = r2sb[:, hb_p % 2, 0, :].unsqueeze(1).broadcast_to([P, 8, 256])
            nmrep = r2sb[:, hb_p % 2, 1, :].unsqueeze(1).broadcast_to([P, 8, 256])
            dst = xnnT[:, hb_p % 2, :, :]
            src = xnT[:, hb_p % 2, :, :]
            prg.op("gpsimd", lambda e: e.tensor_tensor(out=dst, in0=src, in1=r2rep,
                                                       op=ALU.mult), waits=w)
            prg.op("gpsimd", lambda e: e.tensor_tensor(out=dst, in0=dst, in1=nmrep,
                                                       op=ALU.add))
            for j in range(8):
                xnnv[hb_p] = prg.op("gpsimd", lambda e, j=j, hb_p=hb_p: e.tensor_scalar(
                    out=xnnT[:, hb_p % 2, j, :], in0=xnnT[:, hb_p % 2, j, :],
                    scalar1=gbc[:, j:j + 1], scalar2=gbc[:, 8 + j:9 + j],
                    op0=ALU.mult, op1=ALU.add),
                    inc="xnn" if j == 7 else None)

        def p1(hb):
            c0, c1 = 2 * hb, 2 * hb + 1
            w0 = [("xnt", xntv[c1]), ("bce", bcev[hb])]
            if hb == 0:
                w0.append(("dqc", dq_wv))
            for t in range(2):
                for j in range(8):
                    qkv_[hb] = prg.op("tensor", lambda e, hb=hb, t=t, j=j: e.matmul(
                        out=PS[:, 6 * 512 + t * 256:6 * 512 + (t + 1) * 256],
                        lhsT=wqT[:, j, t * P:(t + 1) * P], rhs=xnT[:, hb % 2, j, :],
                        start=(j == 0), stop=(j == 7)),
                        waits=w0 if (t == 0 and j == 0) else (),
                        inc="qk" if (t == 1 and j == 7) else None)
            for t in range(2):
                for j in range(8):
                    qkv_[hb] = prg.op("tensor", lambda e, hb=hb, t=t, j=j: e.matmul(
                        out=PS[:, 7 * 512 + t * 256:7 * 512 + (t + 1) * 256],
                        lhsT=wkT[:, j, t * P:(t + 1) * P], rhs=xnT[:, hb % 2, j, :],
                        start=(j == 0), stop=(j == 7)),
                        inc="qk" if (t == 1 and j == 7) else None)
            wv_ = []
            if hb >= 1:
                wv_ = [("zu", zuv[c0 - 2]), ("vev", vevv[c1 - 2])]
            for cc in range(2):
                for j in range(8):
                    qkv_[hb] = prg.op("tensor", lambda e, hb=hb, cc=cc, j=j: e.matmul(
                        out=PS[:, 2 * 512 + cc * 256:2 * 512 + (cc + 1) * 256],
                        lhsT=xnT[:, hb % 2, j, cc * P:(cc + 1) * P], rhs=wvT[:, j, :],
                        start=(j == 0), stop=(j == 7)),
                        waits=wv_ if (cc == 0 and j == 0) else (),
                        inc="qk" if (cc == 1 and j == 7) else None)

        def sigma(hb):
            qdone = qkv_[hb] - 2
            kdone = qkv_[hb] - 1
            for idx, (bnk, dstT, wv) in enumerate(((6, sqT, qdone), (7, skT, kdone))):
                w = [("qk", wv)]
                if hb >= 2:
                    w.append(("attd", attv[2 * hb - 3]))
                prg.op("vector", lambda e, bnk=bnk, idx=idx: e.tensor_scalar(
                    out=elu[:, idx, :], in0=PS[:, bnk * 512:(bnk + 1) * 512],
                    scalar1=0.0, scalar2=None, op0=ALU.min),
                    waits=w, inc="sgm")
                mv = C["sgm"]
                prg.op("scalar", lambda e, idx=idx: e.activation(
                    out=elu[:, idx, :], in_=elu[:, idx, :], func=AF.Exp),
                    waits=[("sgm", mv)], inc="sge")
                ev = C["sge"]
                sgv[hb] = prg.op("vector", lambda e, bnk=bnk, idx=idx, hb=hb, dstT=dstT: e.scalar_tensor_tensor(
                    out=dstT[:, hb % 2, :, :].rearrange("p t x -> p (t x)"),
                    in0=PS[:, bnk * 512:(bnk + 1) * 512], scalar=1.0,
                    in1=elu[:, idx, :], op0=ALU.add, op1=ALU.max),
                    waits=[("sge", ev)], inc="sg")

        def v_evac(hb):
            c0 = 2 * hb
            for cc in range(2):
                c = c0 + cc
                w = [("qk", qkv_[hb])]
                if c >= 4:
                    w.append(("attd", attv[c - 4]))
                prg.op("vector", lambda e, cc=cc, c=c: e.tensor_copy(
                    out=vE[:, c % 4, 0:D],
                    in_=PS[:, 2 * 512 + cc * 256:2 * 512 + (cc + 1) * 256]),
                    waits=w)
                vevv[c] = prg.op("vector", lambda e, cc=cc, c=c: e.tensor_scalar(
                    out=vEg[:, c % 4, 0:D],
                    in0=PS[:, 2 * 512 + cc * 256:2 * 512 + (cc + 1) * 256],
                    scalar1=gcol[:, 1:2], scalar2=None, op0=ALU.mult), inc="vev")


        def attn_full(c, hb):
            cc = c % 2
            zb = 2 if cc == 0 else 4
            w = [("sg", sgv[hb])]
            if c >= 1:
                w.append(("cmbs", cmbv[c - 1]))
            for t in range(2):
                sktv[c] = prg.op("tensor", lambda e, t=t, hb=hb, cc=cc: e.transpose(
                    out=PSH[:, 6 * 1024 + 640 + t * P:6 * 1024 + 640 + (t + 1) * P],
                    in_=skT[:, hb % 2, t, cc * P:(cc + 1) * P], identity=idb[:]),
                    waits=w if t == 0 else (), inc="skt" if t == 1 else None)
            wd = [("skt", sktv[c])]
            if c >= 4:
                wd.append(("gst", gstv[c - 4]))
            skrv[c] = prg.op("vector", lambda e, c=c: e.tensor_copy(
                out=skR[:, c % 4, :], in_=PSH[:, 6 * 1024 + 640:6 * 1024 + 896]),
                waits=wd, inc="skr")
            prg.op("tensor", lambda e, hb=hb, cc=cc: e.matmul(
                out=PS[:, 7 * 512 + 320:7 * 512 + 448],
                lhsT=skT[:, hb % 2, 0, cc * P:(cc + 1) * P],
                rhs=sqT[:, hb % 2, 0, cc * P:(cc + 1) * P],
                start=True, stop=False),
                waits=[("smc", smcv[c - 1])] if c >= 1 else ())
            sctv[c] = prg.op("tensor", lambda e, hb=hb, cc=cc: e.matmul(
                out=PS[:, 7 * 512 + 320:7 * 512 + 448],
                lhsT=skT[:, hb % 2, 1, cc * P:(cc + 1) * P],
                rhs=sqT[:, hb % 2, 1, cc * P:(cc + 1) * P],
                start=False, stop=True), inc="sct")
            wd = [("sct", sctv[c])]
            if c >= 2:
                wd.append(("attd", attv[c - 1]))
            smcv[c] = prg.op("vector", lambda e, c=c: e.tensor_tensor(
                out=smx[:, c % 2, :], in0=PS[:, 7 * 512 + 320:7 * 512 + 448],
                in1=msk[:], op=ALU.mult), waits=wd, inc="smc")
            wg = [("vev", vevv[c]), ("skr", skrv[c])]
            if c >= 1:
                wg.append(("ste", stev[c]))
            if hb >= 1 and cc == 0:
                wg.append(("yq", yqv[8 * hb]))
            if hb == 0 and cc == 0:
                wg.append(("dqc", dq_wo))
            for t in range(2):
                gstv[c] = prg.op("tensor", lambda e, t=t, c=c: e.matmul(
                    out=PS[:, t * 256:(t + 1) * 256],
                    lhsT=skR[:, c % 4, t * P:(t + 1) * P], rhs=vE[:, c % 4, 0:D],
                    start=True, stop=True),
                    waits=wg if t == 0 else (), inc="gst" if t == 1 else None)
            for t in range(2):
                attv[c] = prg.op("tensor", lambda e, t=t, hb=hb, cc=cc: e.matmul(
                    out=PS[:, 7 * 512:7 * 512 + 257],
                    lhsT=sqT[:, hb % 2, t, cc * P:(cc + 1) * P],
                    rhs=mext[:, t, 0:257], start=(t == 0), stop=(t == 1)),
                    waits=[("sfv", sfv_mext)] if (t == 0 and c == 0) else (),
                    inc="attd" if t == 1 else None)
            wi = [("smc", smcv[c])]
            if c >= 1:
                wi.append(("ste", stev[c]))
            for t in range(2):
                prg.op("tensor", lambda e, t=t, hb=hb, cc=cc, c=c: e.matmul(
                    out=PS[:, 6 * 512:6 * 512 + 257],
                    lhsT=sqT[:, hb % 2, t, cc * P:(cc + 1) * P],
                    rhs=stE[:, c % 2, t, 0:257], start=(t == 0), stop=False),
                    waits=wi if t == 0 else ())
            attv[c] = prg.op("tensor", lambda e, c=c: e.matmul(
                out=PS[:, 6 * 512:6 * 512 + 257],
                lhsT=smx[:, c % 2, :], rhs=vEg[:, c % 4, 0:257],
                start=False, stop=True), inc="attd")
            pc, nx = c % 2, (c + 1) % 2
            prg.op("vector", lambda e, pc=pc, nx=nx: e.scalar_tensor_tensor(
                out=kvst[:, nx, 0, :], in0=PS[:, 0:256], scalar=gcol[:, 1:2],
                in1=kvst[:, pc, 0, :], op0=ALU.mult, op1=ALU.add),
                waits=[("gst", gstv[c])], inc="sfv")
            kvv = prg.op("vector", lambda e, pc=pc, nx=nx: e.scalar_tensor_tensor(
                out=kvst[:, nx, 1, :], in0=PS[:, 256:512], scalar=gcol[:, 1:2],
                in1=kvst[:, pc, 1, :], op0=ALU.mult, op1=ALU.add), inc="sfv")
            redv = 0
            for t in range(2):
                redv = prg.op("vector", lambda e, t=t, hb=hb, cc=cc: e.reduce_sum(
                    out=dump[:, 64 + t:65 + t],
                    in_=skT[:, hb % 2, t, cc * P:(cc + 1) * P], axis=AX.X),
                    inc="sfv")
            ksv = 0
            for t in range(2):
                ksv = prg.op("vector", lambda e, t=t, pc=pc, nx=nx: e.tensor_add(
                    out=kss[:, nx * 2 + t:nx * 2 + t + 1],
                    in0=kss[:, pc * 2 + t:pc * 2 + t + 1],
                    in1=dump[:, 64 + t:65 + t]),
                    waits=[("sfv", redv)] if t == 0 else (), inc="sfv")
            wc = [("attd", attv[c])]
            prg.op("vector", lambda e: e.tensor_scalar(
                out=dump[:, 66:67], in0=PS[:, 7 * 512 + 256:7 * 512 + 257],
                scalar1=EPS_ATT, scalar2=None, op0=ALU.max), waits=wc, inc="sfv")
            dav = prg.op("vector", lambda e: e.tensor_scalar(
                out=dump[:, 67:68], in0=PS[:, 6 * 512 + 256:6 * 512 + 257],
                scalar1=EPS_ATT, scalar2=None, op0=ALU.max), inc="sfv")
            rmv = prg.op("vector", lambda e: e.reciprocal(out=dump[:, 68:69], in_=dump[:, 66:67]),
                         waits=[("sfv", dav)], inc="sfv")
            rav = prg.op("vector", lambda e: e.reciprocal(out=dump[:, 69:70], in_=dump[:, 67:68]),
                         inc="sfv")
            t1v = prg.op("vector", lambda e, c=c: e.tensor_scalar(
                out=dump[:, 128:384], in0=PS[:, 7 * 512:7 * 512 + 256],
                scalar1=dump[:, 68:69], scalar2=None, op0=ALU.mult),
                waits=[("sfv", rmv)], inc="sfv")
            prg.op("vector", lambda e, nx=nx: e.tensor_copy(
                out=stE[:, nx, 0:2, 0:D],
                in_=kvst[:, nx, 0:2, :]), waits=[("sfv", kvv)])
            stev[c + 1] = prg.op("vector", lambda e, nx=nx: e.tensor_copy(
                out=stE[:, nx, 0:2, D:D + 1],
                in_=kss[:, nx * 2:nx * 2 + 2].unsqueeze(2)),
                waits=[("sfv", ksv)], inc="ste")
            wcm = []
            if c >= 2:
                wcm.append(("cbt", cbtv[c - 2]))
            cmbv[c] = prg.op("vector", lambda e, c=c: e.scalar_tensor_tensor(
                out=cmb[:, c % 2, :], in0=PS[:, 6 * 512:6 * 512 + 256],
                scalar=dump[:, 69:70], in1=dump[:, 128:384],
                op0=ALU.mult, op1=ALU.add), waits=wcm + [("sfv", t1v)], inc="cmbs")
            for t in range(2):
                cbtv[c] = prg.op("tensor", lambda e, t=t, c=c: e.transpose(
                    out=PSH[:, 1536 + t * P:1536 + (t + 1) * P],
                    in_=cmb[:, c % 2, t * P:(t + 1) * P], identity=idb[:]),
                    waits=[("cmbs", cmbv[c])] if t == 0 else (),
                    inc="cbt" if t == 1 else None)
            wd = [("cbt", cbtv[c])]
            if c >= 2:
                wd.append(("woz", wozv[c - 2]))
            cbev[c] = prg.op("vector", lambda e, c=c: e.tensor_copy(
                out=cbT[:, c % 2, :, :].rearrange("p t x -> p (t x)"),
                in_=PSH[:, 1536:1792]), waits=wd, inc="cbe")
            wz = [("cbe", cbev[c])]
            if c >= 2:
                wz.append(("zu", zuv[c - 2]))
            if c == 0:
                wz.append(("dqc", dq_b2))
            for t in range(2):
                for half in range(2):
                    b = zb + half
                    prg.op("tensor", lambda e, t=t, half=half, c=c, b=b: e.matmul(
                        out=PS[:, b * 512:(b + 1) * 512],
                        lhsT=cbT[:, c % 2, t, :],
                        rhs=woT[:, t, half * 512:(half + 1) * 512],
                        start=(t == 0), stop=False),
                        waits=wz if (t == 0 and half == 0) else ())
            for half in range(2):
                b = zb + half
                wozv[c] = prg.op("tensor", lambda e, half=half, b=b: e.matmul(
                    out=PS[:, b * 512:(b + 1) * 512],
                    lhsT=onesb[0:1, 0:P], rhs=b2r[0:1, half * 512:(half + 1) * 512],
                    start=False, stop=False),
                    inc="woz" if half == 1 else None)

        def w2_load(hb, gidx):
            if hb == 0:
                return
            G0 = hb * 16 + gidx
            t0 = 2 * gidx
            w = []
            if G0 >= 4:
                w.append(("w2fq", (2 * G0 - 7) // 4 + 1))
            if hb == 1:
                w.append(("sp%d" % (t0 % 8), splv[24 + t0 % 8]))
                w.append(("sp%d" % ((t0 + 1) % 8), splv[24 + (t0 + 1) % 8]))
            w2lv[G0] = ("wl%d" % (gidx % 4), prg.dma(
                "sync", w2r[:, (t0 % 8):(t0 % 8) + 2, :],
                w2s_h[t0:t0 + 2].rearrange("t p h -> p t h"),
                waits=w, inc="wl%d" % (gidx % 4)))

        def ffn(hb, do_prep):
            c0, c1 = 2 * hb, 2 * hb + 1
            hb_p = hb + 1

            def fc2_quad(qq, final):
                gq = hb * 8 + qq
                for fi in range(4):
                    ft = 4 * qq + fi
                    G0 = hb * 16 + ft // 2
                    if hb == 0:
                        wf = [("yq", yqv[hb * 8 + qq + 1]), ("dqc", dqw2t[ft])]
                    else:
                        wf = [("yq", yqv[hb * 8 + qq + 1]), w2lv[G0]]
                    for cc2 in range(2):
                        for half in range(2):
                            b = (2 if cc2 == 0 else 4) + half
                            inc = None
                            if final and fi == 3 and half == 1:
                                inc = "pez"
                            elif (not final) and fi == 3 and half == 1 and cc2 == 1:
                                inc = "w2fq"
                            r = prg.op("tensor", lambda e, ft=ft, cc2=cc2, half=half, b=b, final=final, fi=fi: e.matmul(
                                out=PS[:, b * 512:(b + 1) * 512],
                                lhsT=yT[:, ft % 8, cc2 * P:(cc2 + 1) * P],
                                rhs=w2r[:, ft % 8, half * 512:(half + 1) * 512],
                                start=False, stop=(final and fi == 3)),
                                waits=wf if (cc2 == 0 and half == 0) else (),
                                inc=inc)
                            if inc == "pez":
                                pezv[c0 + cc2] = r
                if final:
                    w2fv[gq] = prg.op("tensor", lambda e: e.nop(), inc="w2fq")

            for q in range(8):
                gq = hb * 8 + q
                wq_ = [("xnn", xnnv[hb]), ("cbe", cbev[c1])]
                if hb == 0 and q == 0:
                    wq_.append(("dqc", dq_w1))
                if gq >= 1:
                    wq_.append(("yq", gq))
                if q == 0:
                    wq_.append(("ste", stev[c1 + 1]))
                for fi in range(4):
                    ft = 4 * q + fi
                    b = 0 if fi < 2 else 1
                    off = (fi % 2) * 256
                    for j in range(8):
                        peyv[gq] = prg.op("tensor", lambda e, ft=ft, j=j, b=b, off=off, hb=hb: e.matmul(
                            out=PS[:, b * 512 + off:b * 512 + off + 256],
                            lhsT=w1s[:, j, ft * P:(ft + 1) * P],
                            rhs=xnnT[:, hb % 2, j, :], start=(j == 0), stop=(j == 7)),
                            waits=wq_ if (fi == 0 and j == 0) else (),
                            inc="peyq" if (fi == 3 and j == 7) else None)
                if do_prep:
                    if q == 0:
                        prep_pe_trans(2 * hb_p, hb_p, 6)
                        prep_dve_evac_xnt(2 * hb_p, hb_p, 6)
                    elif q == 1:
                        prep_pe_trans(2 * hb_p + 1, hb_p, 7)
                        prep_dve_evac_xnt(2 * hb_p + 1, hb_p, 7)
                for fi in range(4):
                    ft = 4 * q + fi
                    b = 0 if fi < 2 else 1
                    off = (fi % 2) * 256
                    yqv[gq + 1] = prg.op("scalar", lambda e, ft=ft, b=b, off=off: e.activation(
                        out=yT[:, ft % 8, :], in_=PS[:, b * 512 + off:b * 512 + off + 256],
                        func=AF.Silu, bias=b1c[:, ft:ft + 1]),
                        waits=[("peyq", peyv[gq])] if fi == 0 else (),
                        inc="yq" if fi == 3 else None)
                if 1 <= q <= 6:
                    w2_load(hb, 4 + 2 * (q - 1))
                    w2_load(hb, 4 + 2 * (q - 1) + 1)
                if q >= 1:
                    fc2_quad(q - 1, False)
                    if hb == 0 and q <= 4:
                        for t in range(16 + 4 * (q - 1), 20 + 4 * (q - 1)):
                            dq_w2_tile(t)
                if do_prep and q == 7:
                    prep_scalar_ln2_late(2 * hb_p)
                    prep_dve_negm2(2 * hb_p)
                    prep_pe_bcast(hb_p)
                    prep_dve_evac_bcast(hb_p)
            fc2_quad(7, True)

        def assembly_u(c):
            zb = 2 if c % 2 == 0 else 4
            w = [("pez", pezv[c])]
            if c >= 2:
                w.append(("zev", zevv[c - 2]))
            zuv[c] = prg.op("scalar", lambda e, c=c, zb=zb: e.activation(
                out=ubuf[:, c % 2, :], in_=PS[:, zb * 512:(zb + 2) * 512],
                func=AF.Identity, bias=s_ap(c, 0)), waits=w, inc="zu")

        def assembly_rest(c):
            w = [("zu", zuv[c])]
            if c >= 2:
                w.append(("ot%d" % (c % 2), otv[c - 2]))
            zevv[c] = prg.op("vector", lambda e, c=c: e.scalar_tensor_tensor(
                out=orow[:, c % 2, :], in0=xnb[:, c % 4, :], scalar=s_ap(c, 6),
                in1=ubuf[:, c % 2, :], op0=ALU.mult, op1=ALU.add),
                waits=w, inc="zev")
            otv[c] = prg.dma("sync", out_h[c * P:(c + 1) * P, :], orow[:, c % 2, :],
                             waits=[("zev", zevv[c])], inc="ot%d" % (c % 2))

        for c in range(3):
            x_load(c)
        prep_dve_stats(0)
        prep_scalar_ln1(0)
        prep_dve_ln1(0)
        prep_pool_xnf(0)
        prep_dve_sums(0)
        prep_dve_ln2(0)
        prep_scalar_ln2_late(0)
        prep_dve_negm2(0)
        prep_pe_trans(0, 0, 6)
        prep_dve_evac_xnt(0, 0, 6)
        prep_pe_trans(1, 0, 7)
        prep_dve_evac_xnt(1, 0, 7)
        prep_pe_bcast(0)
        prep_dve_evac_bcast(0)
        prep_pool_xnn(0)

        for hb in range(NHB):
            c0, c1 = 2 * hb, 2 * hb + 1
            x_load(c0 + 3)
            x_load(c0 + 4)
            for gidx in range(4):
                w2_load(hb, gidx)
            if hb >= 2:
                assembly_u(c0 - 2)
                assembly_u(c1 - 2)
            p1(hb)
            sigma(hb)
            if hb + 1 < NHB:
                prep_dve_stats(c0 + 2)
                prep_scalar_ln1(c0 + 2)
            v_evac(hb)
            attn_full(c0, hb)
            if hb >= 2:
                assembly_rest(c0 - 2)
                assembly_rest(c1 - 2)
            if hb + 1 < NHB:
                prep_dve_ln1(c0 + 2)
                prep_pool_xnf(c0 + 2)
                prep_dve_sums(c0 + 2)
                prep_dve_ln2(c0 + 2)
            attn_full(c1, hb)
            if hb == 0:
                for t in range(16):
                    dq_w2_tile(t)
            ffn(hb, do_prep=(hb + 1 < NHB))
            if hb == 0:
                assembly_u(0)
                assembly_u(1)
                assembly_rest(0)
                assembly_rest(1)
            if hb + 1 < NHB:
                prep_pool_xnn(hb + 1)
        assembly_u(NCH - 2)
        assembly_u(NCH - 1)
        assembly_rest(NCH - 2)
        assembly_rest(NCH - 1)

        if DEBUG2:
            prg.dma("sync", w1d_h, w1s[:, :, :].rearrange("p g f -> p (g f)"),
                    inc="pcon")
            prg.dma("sync", w2d_h, w2s_h, inc="pcon")
        if DEBUG:
            prg.dma("sync", dbg_h, dbg[:], waits=[("zev", C["zev"])], inc="pcon")
        prg.op("sync", lambda e: e.nop(),
               waits=[("ot0", C["ot0"]), ("ot1", C["ot1"]), ("pcon", C["pcon"])])

        prg.replay(block)

    return nc


_NC_CACHE = None


def kernel(**inputs):
    global _NC_CACHE
    xs = np.asarray(inputs["L0_attn_hidden_states"], dtype=np.float32)
    B = xs.shape[0]
    ident_b = np.eye(P, dtype=ml_dtypes.bfloat16)
    ident_f = np.eye(P, dtype=np.float32)
    mask = np.triu(np.ones((P, P), dtype=np.float32)).astype(ml_dtypes.bfloat16)

    common = {
        "wq": np.ascontiguousarray(np.asarray(inputs["L0_attn_w_q"], np.float32)),
        "wk": np.ascontiguousarray(np.asarray(inputs["L0_attn_w_k"], np.float32)),
        "wv": np.ascontiguousarray(np.asarray(inputs["L0_attn_w_v"], np.float32)),
        "wo": np.ascontiguousarray(np.asarray(inputs["L0_attn_w_o"], np.float32)),
        "gate": np.ascontiguousarray(np.asarray(inputs["L0_attn_gate"], np.float32)),
        "mem": np.ascontiguousarray(np.asarray(inputs["L0_attn_memory"], np.float32)),
        "mn": np.ascontiguousarray(np.asarray(inputs["L0_attn_memory_norm"], np.float32)),
        "gam": np.ascontiguousarray(np.asarray(inputs["L0_ffn_ln_gamma"], np.float32)),
        "bet": np.ascontiguousarray(np.asarray(inputs["L0_ffn_ln_beta"], np.float32)),
        "q1": np.ascontiguousarray(np.asarray(inputs["L0_ffn_fc1_q"], np.int32)),
        "s1": np.ascontiguousarray(np.asarray(inputs["L0_ffn_fc1_scales"], np.float32)),
        "b1": np.ascontiguousarray(np.asarray(inputs["L0_ffn_fc1_bias"], np.float32)),
        "q2": np.ascontiguousarray(np.asarray(inputs["L0_ffn_fc2_q"], np.int32)),
        "s2": np.ascontiguousarray(np.asarray(inputs["L0_ffn_fc2_scales"], np.float32)),
        "b2": np.ascontiguousarray(np.asarray(inputs["L0_ffn_fc2_bias"], np.float32)),
        "idb_d": ident_b, "idf_d": ident_f, "msk_d": mask,
    }
    in_maps = []
    for b in range(B):
        m = dict(common)
        m["x"] = np.ascontiguousarray(xs[b])
        in_maps.append(m)

    if _NC_CACHE is None:
        _NC_CACHE = build_nc()
    res = run_bass_kernel_spmd(_NC_CACHE, in_maps, core_ids=list(range(B)))
    out = np.stack([np.asarray(r["out"], np.float32) for r in res.results], axis=0)
    if DEBUG:
        global LAST_DBG
        LAST_DBG = {b: np.asarray(res.results[b]["dbg"], np.float32) for b in range(B)}
    if DEBUG2:
        global LAST_DBG2
        LAST_DBG2 = {b: (np.asarray(res.results[b]["w1dump"]),
                         np.asarray(res.results[b]["w2dump"])) for b in range(B)}
    return out


# version 28
# speedup vs baseline: 1.0066x; 1.0053x over previous
import numpy as np
import ml_dtypes
from contextlib import ExitStack

import concourse.bass as bass
import concourse.mybir as mybir
from concourse.bass_utils import run_bass_kernel_spmd

F32 = mybir.dt.float32
BF16 = mybir.dt.bfloat16
I32 = mybir.dt.int32
AF = mybir.ActivationFunctionType
ALU = mybir.AluOpType
AX = mybir.AxisListType

P = 128
S = 4096
H = 1024
D = 256
FF = 4096
NCH = S // P
NHB = NCH // 2
EPS_LN = 1e-5
EPS_ATT = 1e-6

ENGS = ("sync", "scalar", "vector", "tensor", "gpsimd")
DEBUG = False
DEBUG2 = False
LAST_DBG = {}
LAST_DBG2 = {}


class Prog:
    def __init__(self, nc, ctx):
        self.nc = nc
        self.ops = {e: [] for e in ENGS}
        self.sem = {}
        self.cnt = {}
        self.ctx = ctx

    def new_sem(self, name):
        self.sem[name] = self.ctx.enter_context(self.nc.semaphore(name))
        self.cnt[name] = 0

    def op(self, eng, fn, waits=(), inc=None, inc_amt=1):
        waits = tuple((s, v) for (s, v) in waits if v > 0)
        self.ops[eng].append((waits, fn, inc, inc_amt))
        if inc is not None:
            self.cnt[inc] += inc_amt
            return self.cnt[inc]
        return None

    def dma(self, eng, out, in_, waits=(), inc=None, **kw):
        def fn(e, out=out, in_=in_, kw=kw):
            return e.dma_start(out=out, in_=in_, **kw)
        return self.op(eng, fn, waits, inc, 16)

    def replay(self, block):
        for ename in ENGS:
            ops = self.ops[ename]

            def make(ops, sems):
                def body(eng):
                    last = {}
                    for waits, fn, inc, inc_amt in ops:
                        for sname, val in waits:
                            if last.get(sname, 0) < val:
                                eng.wait_ge(sems[sname], val)
                                last[sname] = val
                        inst = fn(eng)
                        if inc is not None:
                            inst.then_inc(sems[inc], inc_amt)
                return body
            getattr(block, ename)(make(ops, self.sem))


def build_nc():
    nc = bass.Bass("TRN2", target_bir_lowering=False, debug=False)

    x_h = nc.dram_tensor("x", [S, H], F32, kind="ExternalInput").ap()
    wq_h = nc.dram_tensor("wq", [D, H], F32, kind="ExternalInput").ap()
    wk_h = nc.dram_tensor("wk", [D, H], F32, kind="ExternalInput").ap()
    wv_h = nc.dram_tensor("wv", [D, H], F32, kind="ExternalInput").ap()
    wo_h = nc.dram_tensor("wo", [H, D], F32, kind="ExternalInput").ap()
    gate_h = nc.dram_tensor("gate", [1], F32, kind="ExternalInput").ap()
    mem_h = nc.dram_tensor("mem", [D, D], F32, kind="ExternalInput").ap()
    mn_h = nc.dram_tensor("mn", [D], F32, kind="ExternalInput").ap()
    gam_h = nc.dram_tensor("gam", [H], F32, kind="ExternalInput").ap()
    bet_h = nc.dram_tensor("bet", [H], F32, kind="ExternalInput").ap()
    q1_h = nc.dram_tensor("q1", [H, FF], I32, kind="ExternalInput").ap()
    s1_h = nc.dram_tensor("s1", [8, FF], F32, kind="ExternalInput").ap()
    b1_h = nc.dram_tensor("b1", [FF], F32, kind="ExternalInput").ap()
    q2_h = nc.dram_tensor("q2", [FF, H], I32, kind="ExternalInput").ap()
    s2_h = nc.dram_tensor("s2", [32, H], F32, kind="ExternalInput").ap()
    b2_h = nc.dram_tensor("b2", [H], F32, kind="ExternalInput").ap()
    idb_h = nc.dram_tensor("idb_d", [P, P], BF16, kind="ExternalInput").ap()
    idf_h = nc.dram_tensor("idf_d", [P, P], F32, kind="ExternalInput").ap()
    msk_h = nc.dram_tensor("msk_d", [P, P], BF16, kind="ExternalInput").ap()

    out_h = nc.dram_tensor("out", [S, H], F32, kind="ExternalOutput").ap()
    dbg_h = nc.dram_tensor("dbg", [P, 4096], F32, kind="ExternalOutput").ap() if DEBUG else None
    w1d_h = nc.dram_tensor("w1dump", [P, 8 * FF], BF16, kind="ExternalOutput").ap() if DEBUG2 else None
    w2d_h = nc.dram_tensor("w2dump", [32, P, H], BF16, kind="ExternalOutput").ap() if DEBUG2 else None
    w2s_h = nc.dram_tensor("w2scr", [32, P, H], BF16, kind="Internal").ap()
    s2h_h = nc.dram_tensor("s2half", [32, H], F32, kind="Internal").ap()

    with ExitStack() as ctx:
        def sb(name, shape, dtype):
            return ctx.enter_context(nc.sbuf_tensor(name, shape, dtype))

        w1s = sb("w1s", [P, 8, FF], BF16)
        w2r = sb("w2r", [P, 8, H], BF16)
        wqT = sb("wqT", [P, 8, D], BF16)
        wkT = sb("wkT", [P, 8, D], BF16)
        wvT = sb("wvT", [P, 8, D], BF16)
        woT = sb("woT", [P, 2, H], BF16)
        idb = sb("idb", [P, P], BF16)
        idf = sb("idf", [P, P], F32)
        msk = sb("msk", [P, P], BF16)
        xbuf = sb("xbuf", [P, 3, H], F32)
        xnf = sb("xnf", [P, 2, H], F32)
        ov1 = sb("ov1", [P, 2, H], F32)
        ov2 = sb("ov2", [P, 2, H], F32)
        ov3 = sb("ov3", [P, 2, H], F32)
        ov4 = sb("ov4", [P, 2, H], BF16)
        ubuf = ov1[:]
        orow = ov2[:]
        wst = ov3[:]
        _ov3b = ov3[:, 0:2, :].rearrange("p t h -> p (t h)").bitcast(BF16)
        yT = _ov3b[:, 0:2048].rearrange("p (f x) -> p f x", f=8)
        r2sb = _ov3b[:, 2048:3072].rearrange("p (a i x) -> p a i x", a=2, i=2)
        elu = _ov3b[:, 3072:4096].rearrange("p (i x) -> p i x", i=2)
        wstb = ov4[:]
        _ov4b = ov4[:, 0:2, :].rearrange("p t h -> p (t h)")
        skR = _ov4b[:, 0:1024].rearrange("p (a x) -> p a x", a=4)
        cmb = _ov4b[:, 1024:1536].rearrange("p (a x) -> p a x", a=2)
        cbT = _ov4b[:, 1536:2048].rearrange("p (a t x) -> p a t x", a=2, t=2)
        dump = sb("dump", [P, 384], F32)
        xnb = sb("xnb", [P, 4, H], BF16)
        xnT = sb("xnT", [P, 2, 8, D], BF16)
        xnnT = sb("xnnT", [P, 2, 8, D], BF16)
        sqT = sb("sqT", [P, 2, 2, D], BF16)
        skT = sb("skT", [P, 2, 2, D], BF16)
        vE = sb("vE", [P, 4, 258], BF16)
        vEg = sb("vEg", [P, 4, 258], BF16)
        smx = sb("smx", [P, 2, P], BF16)
        stat = sb("stat", [P, 8, 16], F32)
        kvst = sb("kvst", [P, 2, 2, D], F32)
        kss = sb("kss", [P, 4], F32)
        stE = sb("stE", [P, 2, 2, 258], BF16)
        mext = sb("mext", [P, 2, 258], BF16)
        mncol = sb("mncol", [P, 2], F32)
        b1c = sb("b1c", [P, 32], F32)
        sbc = sb("sbc", [P, 2, 512], F32)
        bstg = sbc[0:1, 0:2, :].rearrange("o t h -> o (t h)")
        b2r = sb("b2r", [1, H], BF16)
        onesb = sb("onesb", [1, 512], BF16)
        onesf = sb("onesf", [1, P], F32)
        halvesf = sb("halvesf", [1, P], F32)
        qf = sb("qf", [P, 2, 1024], I32)
        qff = qf[:, :, :].bitcast(F32)
        memstg = qff[:, 1, 0:2 * D].rearrange("p (t e) -> p t e", t=2)
        grow = sb("grow", [1, 16], F32)
        gcol = sb("gcol", [P, 4], F32)
        mnr = sb("mnr", [1, D], F32)
        gbc = sb("gbc", [P, 16], F32)
        dbg = w1s[:, 6:8, :].rearrange("p g f -> p (g f)").bitcast(F32) if DEBUG else None

        PS_t = ctx.enter_context(nc.psum_tensor("ps", [P, 8 * 512], F32))
        PS = PS_t[:]
        PSH = PS.bitcast(BF16)

        prg = Prog(nc, ctx)
        for s in ("x0", "x1", "x2", "ot0", "ot1", "wl0", "wl1", "wl2", "wl3",
                  "sp0", "sp1", "sp2", "sp3", "sp4", "sp5", "sp6", "sp7",
                  "qf0", "qf1", "sb0", "sb1", "pcon", "pmem", "pb1", "pgb",
                  "pb2", "ps2", "pwst", "w2fq", "dqc", "prc",
                  "st1", "ln1", "lnx", "st2", "xbf", "xtr", "xnt",
                  "bca", "bce", "xnn", "qk", "sgm", "sge", "sg",
                  "vev", "skt", "skr", "sct", "smc", "gst", "ste",
                  "attd", "cmbs", "cbt", "cbe", "yq", "peyq", "pez",
                  "zu", "zev", "woz", "sfv", "sfa", "xsm", "s2h"):
            prg.new_sem(s)
        C = prg.cnt

        block = ctx.enter_context(nc.Block())

        prg.dma("sync", idb[:], idb_h, inc="pcon")
        prg.dma("sync", idf[:], idf_h, inc="pcon")
        prg.dma("sync", msk[:], msk_h, inc="pcon")
        prg.dma("sync", grow[0:1, 0:1], gate_h.unsqueeze(0), inc="pcon")
        prg.dma("sync", mnr[:], mn_h.unsqueeze(0), inc="pcon")
        pre_consts = C["pcon"]
        prg.dma("sync", mncol[:, 0:1], mn_h[0:P].unsqueeze(1), inc="pmem")
        prg.dma("sync", mncol[:, 1:2], mn_h[P:D].unsqueeze(1), inc="pmem")
        prg.dma("sync", memstg,
                mem_h.rearrange("(t p) e -> p t e", p=P), inc="pmem")
        pre_mem = C["pmem"]
        prg.dma("gpsimd", qff[0:32, 0, 0:P], b1_h.rearrange("(a p) -> a p", p=P),
                inc="pb1")
        pre_b1 = C["pb1"]
        prg.op("tensor", lambda e: e.transpose(out=PS[:, 0:32], in_=qff[0:32, 0, 0:P],
                                               identity=idf[0:32, 0:32]),
               waits=[("pb1", pre_b1), ("pcon", pre_consts)], inc="prc")
        prc_b1 = C["prc"]
        prg.op("vector", lambda e: e.tensor_copy(out=b1c[:], in_=PS[:, 0:32]),
               waits=[("prc", prc_b1)], inc="dqc")
        dq_b1 = C["dqc"]
        prg.dma("sync", bstg[0:1, 0:H], b2_h.unsqueeze(0),
                waits=[("dqc", dq_b1)], inc="pb2")
        pre_b2 = C["pb2"]
        prg.op("vector", lambda e: e.tensor_scalar(
            out=b2r[0:1, 0:H], in0=bstg[0:1, 0:H], scalar1=0.5, scalar2=None,
            op0=ALU.mult), waits=[("pb2", pre_b2)], inc="dqc")
        dq_b2 = C["dqc"]
        for fn in (lambda e: e.memset(onesf[:], 1.0),
                   lambda e: e.memset(halvesf[:], 0.5),
                   lambda e: e.memset(onesb[:], 1.0),
                   lambda e: e.memset(kvst[:, :, :, :].rearrange("p a t x -> p (a t x)"), 0.0),
                   lambda e: e.memset(kss[:], 0.0),
                   lambda e: e.memset(stE[:, :, :, :].rearrange("p a t x -> p (a t x)"), 0.0),
                   lambda e: e.memset(vE[:, :, :].rearrange("p a x -> p (a x)"), 1.0),
                   lambda e: e.memset(vEg[:, :, :].rearrange("p a x -> p (a x)"), 1.0)):
            prg.op("vector", fn, inc="dqc")
        dq_memset = C["dqc"]

        prg.dma("gpsimd", qff[0:8, 0, 0:P], gam_h.rearrange("(g p) -> g p", p=P),
                waits=[("prc", prc_b1)], inc="pgb")
        prg.dma("gpsimd", qff[0:8, 0, P:2 * P], bet_h.rearrange("(g p) -> g p", p=P),
                inc="pgb")
        pre_gb = C["pgb"]
        prg.op("vector", lambda e: e.tensor_copy(out=wstb[0:8, 0, 0:2 * P],
                                                 in_=qff[0:8, 0, 0:2 * P]),
               waits=[("pgb", pre_gb)], inc="dqc")
        dq_gbc = C["dqc"]
        prg.op("tensor", lambda e: e.transpose(out=PSH[:, 0:8],
                                               in_=wstb[0:8, 0, 0:P],
                                               identity=idb[0:8, 0:8]),
               waits=[("dqc", dq_gbc), ("pcon", pre_consts)], inc="prc")
        prg.op("tensor", lambda e: e.transpose(out=PSH[:, 8:16],
                                               in_=wstb[0:8, 0, P:2 * P],
                                               identity=idb[0:8, 0:8]), inc="prc")
        prc_gb = C["prc"]
        prg.op("vector", lambda e: e.tensor_copy(out=gbc[:, 0:16], in_=PSH[:, 0:16]),
               waits=[("prc", prc_gb)], inc="dqc")
        dq_gb2 = C["dqc"]

        prg.op("scalar", lambda e: e.activation(out=grow[0:1, 1:2], in_=grow[0:1, 0:1],
                                                func=AF.Tanh, scale=0.5),
               waits=[("pcon", pre_consts)], inc="sfa")
        sfa_t = C["sfa"]
        prg.op("scalar", lambda e: e.activation(out=grow[0:1, 2:3], in_=grow[0:1, 1:2],
                                                func=AF.Copy, scale=0.5, bias=0.5),
               waits=[("sfa", sfa_t)], inc="sfa")
        sfa_g = C["sfa"]
        prg.op("vector", lambda e: e.reduce_sum(out=grow[0:1, 6:7], in_=mnr[:], axis=AX.X),
               waits=[("pcon", pre_consts)], inc="sfv")
        sfv_mn = C["sfv"]
        prg.op("vector", lambda e: e.tensor_single_scalar(
            out=grow[0:1, 3:4], in_=grow[0:1, 6:7], scalar=EPS_ATT, op=ALU.is_ge),
            waits=[("sfv", sfv_mn), ("sfa", sfa_g)], inc="sfv")
        sfv_f = C["sfv"]
        prg.op("vector", lambda e: e.tensor_mul(out=grow[0:1, 4:5], in0=grow[0:1, 2:3],
                                                in1=grow[0:1, 3:4]),
               waits=[("sfv", sfv_f)], inc="sfv")
        sfv_ge = C["sfv"]
        prg.op("vector", lambda e: e.tensor_scalar(
            out=grow[0:1, 5:6], in0=grow[0:1, 2:3], scalar1=-1.0, scalar2=1.0,
            op0=ALU.mult, op1=ALU.add), waits=[("sfv", sfv_ge)], inc="sfv")
        sfv_g1 = C["sfv"]
        prg.op("tensor", lambda e: e.matmul(out=PS[:, 0:2], lhsT=onesf[0:1, :],
                                            rhs=grow[0:1, 4:6], start=True, stop=True),
               waits=[("sfv", sfv_g1), ("dqc", dq_memset)], inc="prc")
        prc_g = C["prc"]
        prg.op("vector", lambda e: e.tensor_copy(out=gcol[:, 0:2], in_=PS[:, 0:2]),
               waits=[("prc", prc_g)], inc="sfv")
        sfv_gc = C["sfv"]
        prg.op("vector", lambda e: e.tensor_scalar(
            out=mext[:, 0:2, 0:D],
            in0=memstg,
            scalar1=gcol[:, 0:1], scalar2=None, op0=ALU.mult),
            waits=[("sfv", sfv_gc), ("pmem", pre_mem)], inc="sfv")
        prg.op("vector", lambda e: e.tensor_copy(
            out=mext[:, 0:2, D:D + 1],
            in_=mncol[:, 0:2].unsqueeze(2)), waits=[("pmem", pre_mem)], inc="sfv")
        sfv_mext = C["sfv"]

        def wload(src_h, dst):
            dmaw = [("dqc", C["dqc"]), ("sfv", sfv_mext)]
            prg.dma("gpsimd", wst[:, 0:2, :],
                    src_h.rearrange("(t p) h -> p t h", p=P),
                    waits=dmaw, inc="pwst")
            pr = C["pwst"]
            prg.op("vector", lambda e: e.tensor_copy(
                out=wstb[:, 0:2, :].rearrange("p t h -> p (t h)"),
                in_=wst[:, 0:2, :].rearrange("p t h -> p (t h)")),
                waits=[("pwst", pr), ("prc", C["prc"])], inc="dqc")
            dq = C["dqc"]
            for j in range(8):
                for t in range(2):
                    prg.op("tensor", lambda e, j=j, t=t: e.transpose(
                        out=PSH[:, (2 * j + t) * P:(2 * j + t + 1) * P],
                        in_=wstb[:, t, j * P:(j + 1) * P], identity=idb[:]),
                        waits=[("dqc", dq)] if (j == 0 and t == 0) else (),
                        inc="prc")
            pr2 = C["prc"]
            prg.op("vector", lambda e, dst=dst: e.tensor_copy(
                out=dst[:, :, :].rearrange("p j d -> p (j d)"),
                in_=PSH[:, 0:2048]),
                waits=[("prc", pr2)], inc="dqc")
            return C["dqc"]

        dq_wq = wload(wq_h, wqT)
        dq_wk = wload(wk_h, wkT)
        dq_wv = wload(wv_h, wvT)

        def wo_half(li):
            dmaw = [("dqc", C["dqc"])]
            prg.dma("gpsimd", wst[:, 0, :].rearrange("p (a d) -> p a d", a=4),
                    wo_h.rearrange("(a p) d -> p a d", p=P)[:, 4 * li:4 * li + 4, :],
                    waits=dmaw, inc="pwst")
            pr = C["pwst"]
            prg.op("vector", lambda e: e.tensor_copy(
                out=wstb[:, 0, :], in_=wst[:, 0, :]),
                waits=[("pwst", pr), ("prc", C["prc"])], inc="dqc")
            dq = C["dqc"]
            flat = wstb[:, 0, :]
            for a in range(4):
                for t in range(2):
                    prg.op("tensor", lambda e, a=a, t=t: e.transpose(
                        out=PSH[:, (2 * a + t) * P:(2 * a + t + 1) * P],
                        in_=flat[:, a * 256 + t * P:a * 256 + (t + 1) * P],
                        identity=idb[:]),
                        waits=[("dqc", dq)] if (a == 0 and t == 0) else (),
                        inc="prc")
            pr2 = C["prc"]
            prg.op("vector", lambda e, li=li: e.tensor_copy(
                out=woT[:, 0:2, li * 512:(li + 1) * 512].rearrange("p t (a x) -> p t a x", a=4),
                in_=PSH[:, 0:1024].rearrange("p (a t x) -> p t a x", a=4, t=2, x=P)),
                waits=[("prc", pr2)], inc="dqc")
            return C["dqc"]

        dq_wo = wo_half(0)
        dq_wo = wo_half(1)

        prg.dma("gpsimd", qff[0:32, 1, :], s2_h,
                waits=[("sfv", sfv_mext), ("dqc", dq_wo)], inc="ps2")
        pre_s2 = C["ps2"]
        prg.op("vector", lambda e: e.tensor_scalar(
            out=qff[0:32, 1, :], in0=qff[0:32, 1, :], scalar1=0.5, scalar2=None,
            op0=ALU.mult), waits=[("ps2", pre_s2)], inc="dqc")
        dq_s2 = C["dqc"]
        prg.dma("gpsimd", s2h_h, qff[0:32, 1, :], waits=[("dqc", dq_s2)], inc="s2h")
        s2h_done = C["s2h"]

        dq_slot = [dq_s2, dq_s2]
        sbc_slot = [dq_b2, dq_b2]
        sbci = [0]

        dq_w1_qt = [0] * 4

        def dq_w1_col(qt):
            for g in range(8):
                slot = g % 2
                w = [("dqc", dq_slot[slot])]
                if g == 0 and qt == 0:
                    w.append(("prc", prc_b1))
                if g == 1 and qt == 0:
                    w.append(("s2h", s2h_done))
                pr_q = prg.dma("sync", qf[:, slot, :],
                               q1_h[g * P:(g + 1) * P, qt * 1024:(qt + 1) * 1024],
                               waits=w, inc="qf%d" % slot)
                for h2 in range(2):
                    si = sbci[0] % 2
                    sbci[0] += 1
                    pr_s = prg.dma("sync" if si == 0 else "scalar", sbc[:, si, :],
                                   s1_h[g:g + 1, qt * 1024 + h2 * 512:qt * 1024 + (h2 + 1) * 512].broadcast_to([P, 512]),
                                   waits=[("dqc", sbc_slot[si])], inc="sb%d" % si)
                    sbc_slot[si] = prg.op("vector", lambda e, g=g, qt=qt, h2=h2, slot=slot, si=si: e.scalar_tensor_tensor(
                        out=w1s[:, g, qt * 1024 + h2 * 512:qt * 1024 + (h2 + 1) * 512],
                        in0=qf[:, slot, h2 * 512:(h2 + 1) * 512], scalar=-128.0,
                        in1=sbc[:, si, :], op0=ALU.add, op1=ALU.mult),
                        waits=[("sb%d" % si, pr_s), ("qf%d" % slot, pr_q)], inc="dqc")
                dq_slot[slot] = C["dqc"]
            dq_w1_qt[qt] = C["dqc"]

        for qt in range(4):
            dq_w1_col(qt)
        dq_w1 = C["dqc"]

        splv = [0] * 33
        dqw2t = [0] * 32

        def dq_w2_tile(t):
            slot = t % 2
            w = [("dqc", dq_slot[slot])]
            pr_q = prg.dma("sync", qf[:, slot, :], q2_h[t * P:(t + 1) * P, :],
                           waits=w, inc="qf%d" % slot)
            wstt = []
            if t >= 8:
                wstt = [("sp%d" % ((t - 8) % 8), splv[t - 8]),
                        ("w2fq", (t - 8) // 4 + 1)]
            for h2 in range(2):
                si = sbci[0] % 2
                sbci[0] += 1
                pr_s = prg.dma("sync", sbc[:, si, :],
                               s2h_h[t:t + 1, h2 * 512:(h2 + 1) * 512].broadcast_to([P, 512]),
                               waits=[("dqc", sbc_slot[si]), ("s2h", s2h_done)], inc="sb%d" % si)
                sbc_slot[si] = prg.op("vector", lambda e, t=t, h2=h2, slot=slot, si=si: e.scalar_tensor_tensor(
                    out=w2r[:, t % 8, h2 * 512:(h2 + 1) * 512],
                    in0=qf[:, slot, h2 * 512:(h2 + 1) * 512], scalar=-128.0,
                    in1=sbc[:, si, :], op0=ALU.add, op1=ALU.mult),
                    waits=[("sb%d" % si, pr_s), ("qf%d" % slot, pr_q)] + wstt, inc="dqc")
            dq_slot[slot] = C["dqc"]
            dq_t = C["dqc"]
            dqw2t[t] = dq_t
            splv[t] = prg.dma("sync", w2s_h[t], w2r[:, t % 8, :],
                              waits=[("dqc", dq_t)], inc="sp%d" % (t % 8))

        dq_w2 = C["dqc"]

        st1v = [0] * NCH
        ln1v = [0] * NCH
        lnxv = [0] * NCH
        xsmv = [0] * NCH
        xdv = [0] * (NCH + 8)
        st2av = [0] * NCH
        st2ev = [0] * NCH
        ln2v = [0] * NCH
        st2v = [0] * NCH
        xbfv = [0] * NCH
        xtrv = [0] * NCH
        xntv = [0] * NCH
        bcav = [0] * NHB
        bcev = [0] * NHB
        xnnv = [0] * NHB
        qkv_ = [0] * NHB
        sgv = [0] * NHB
        vevv = [0] * NCH
        skrv = [0] * NCH
        smcv = [0] * NCH
        sktv = [0] * NCH
        sctv = [0] * NCH
        gstv = [0] * NCH
        stev = [0] * (NCH + 1)
        attv = [0] * NCH
        cmbv = [0] * NCH
        cbtv = [0] * NCH
        cbev = [0] * NCH
        wozv = [0] * NCH
        yqv = [0] * (NHB * 8 + 1)
        peyv = [0] * (NHB * 8 + 1)
        pezv = [0] * NCH
        zuv = [0] * NCH
        zevv = [0] * NCH
        otv = [0] * NCH
        w2lv = [0] * (NHB * 16 + 1)
        w2fv = [0] * (NHB * 8 + 1)

        ST = stat

        def s_ap(c, i):
            return ST[:, c % 8, i:i + 1]

        def x_load(c):
            if c >= NCH:
                return
            w = []
            if c >= 3:
                w = [("lnx", lnxv[c - 3]), ("st1", st1v[c - 3]),
                     ("xsm", xsmv[c - 3])]
            xdv[c] = prg.dma("sync", xbuf[:, c % 3, :], x_h[c * P:(c + 1) * P, :],
                             waits=w, inc="x%d" % (c % 3))

        def prep_dve_stats(c0):
            c1 = c0 + 1
            bnv = 0
            for c, base in ((c0, 0), (c1, 32)):
                w = [("x%d" % (c % 3), xdv[c])]
                if c >= 8:
                    w.append(("zev", zevv[c - 8]))
                prg.op("vector", lambda e, c=c, base=base: e.bn_stats(
                    out=dump[:, base:base + 6], in_=xbuf[:, c % 3, 0:512]),
                    waits=w, inc="sfv")
                bnv = prg.op("vector", lambda e, c=c, base=base: e.bn_stats(
                    out=dump[:, base + 6:base + 12], in_=xbuf[:, c % 3, 512:1024]),
                    inc="sfv")
            aggv = 0
            for c, base in ((c0, 0), (c1, 32)):
                aggv = prg.op("vector", lambda e, c=c, base=base: e.bn_aggr(
                    out=ST[:, c % 8, 0:2], in_=dump[:, base:base + 12]),
                    waits=[("sfv", bnv)] if base == 0 else (), inc="sfv")
            for c in (c0, c1):
                st1v[c] = prg.op("vector", lambda e, c=c: e.tensor_scalar(
                    out=s_ap(c, 15), in0=s_ap(c, 1), scalar1=EPS_LN,
                    scalar2=None, op0=ALU.add),
                    waits=[("sfv", aggv)] if c == c0 else (), inc="st1")

        def prep_scalar_ln1(c0):
            c1 = c0 + 1
            t1v = 0
            for c in (c0, c1):
                t1v = prg.op("scalar", lambda e, c=c: e.activation(
                    out=s_ap(c, 2), in_=s_ap(c, 15), func=AF.Ln),
                    waits=[("st1", st1v[c])], inc="sfa")
            for c in (c0, c1):
                prg.op("scalar", lambda e, c=c: e.activation(
                    out=s_ap(c, 3), in_=s_ap(c, 2), func=AF.Exp, scale=-0.5),
                    waits=[("sfa", t1v)] if c == c0 else (), inc="ln1")
                ln1v[c] = prg.op("scalar", lambda e, c=c: e.activation(
                    out=s_ap(c, 4), in_=s_ap(c, 2), func=AF.Exp, scale=0.5),
                    inc="ln1")

        def prep_dve_ln1(c0):
            for c in (c0, c0 + 1):
                prg.op("vector", lambda e, c=c: e.tensor_scalar(
                    out=s_ap(c, 5), in0=s_ap(c, 0), scalar1=s_ap(c, 3),
                    scalar2=-1.0, op0=ALU.mult, op1=ALU.mult),
                    waits=[("ln1", ln1v[c])], inc="st2")
                st2av[c] = prg.op("vector", lambda e, c=c: e.tensor_scalar(
                    out=s_ap(c, 6), in0=s_ap(c, 4), scalar1=1.0, scalar2=None,
                    op0=ALU.add), inc="st2")

        def prep_pool_xnf(c0):
            xfv = 0
            for c in (c0, c0 + 1):
                w = [("st2", st2av[c])]
                if c >= 2:
                    w.append(("xsm", xsmv[c - 2]))
                lnxv[c] = prg.op("vector", lambda e, c=c: e.tensor_scalar(
                    out=xnf[:, c % 2, :], in0=xbuf[:, c % 3, :],
                    scalar1=s_ap(c, 3), scalar2=s_ap(c, 5),
                    op0=ALU.mult, op1=ALU.add), waits=w, inc="lnx")
                xfv = lnxv[c]
            for c in (c0, c0 + 1):
                w = [("lnx", xfv)]
                if c >= 4:
                    w.append(("zev", zevv[c - 4]))
                xbfv[c] = prg.op("vector", lambda e, c=c: e.tensor_copy(
                    out=xnb[:, c % 4, :], in_=xnf[:, c % 2, :]),
                    waits=w, inc="xbf")

        def prep_dve_sums(c0):
            for c in (c0, c0 + 1):
                prg.op("vector", lambda e, c=c: e.reduce_sum(
                    out=s_ap(c, 7), in_=xnf[:, c % 2, :], axis=AX.X),
                    waits=[("lnx", lnxv[c])], inc="sfv")
                xsmv[c] = prg.op("vector", lambda e, c=c: e.scalar_tensor_tensor(
                    out=xbuf[:, c % 3, :], in0=xnf[:, c % 2, :], scalar=1.0,
                    in1=xnf[:, c % 2, :], op0=ALU.mult, op1=ALU.mult,
                    accum_out=s_ap(c, 8)), inc="xsm")

        def prep_dve_ln2(c0):
            mmv = 0
            for c in (c0, c0 + 1):
                prg.op("vector", lambda e, c=c: e.tensor_scalar(
                    out=s_ap(c, 9), in0=s_ap(c, 7), scalar1=1.0 / H,
                    scalar2=None, op0=ALU.mult),
                    waits=[("xsm", xsmv[c])])
                mmv = prg.op("vector", lambda e, c=c: e.tensor_scalar(
                    out=s_ap(c, 10), in0=s_ap(c, 7), scalar1=s_ap(c, 7),
                    scalar2=1.0 / (float(H) * H), op0=ALU.mult, op1=ALU.mult),
                    inc="sfv")
            vvv = 0
            for c in (c0, c0 + 1):
                vvv = prg.op("vector", lambda e, c=c: e.scalar_tensor_tensor(
                    out=s_ap(c, 11), in0=s_ap(c, 8), scalar=1.0 / H,
                    in1=s_ap(c, 10), op0=ALU.mult, op1=ALU.subtract),
                    waits=[("sfv", mmv)] if c == c0 else (), inc="sfv")
            for c in (c0, c0 + 1):
                st2ev[c] = prg.op("vector", lambda e, c=c: e.tensor_scalar(
                    out=s_ap(c, 15), in0=s_ap(c, 11), scalar1=EPS_LN,
                    scalar2=None, op0=ALU.add),
                    waits=[("sfv", vvv)] if c == c0 else (), inc="st2")

        def prep_scalar_ln2_late(c0):
            t2v = 0
            for c in (c0, c0 + 1):
                t2v = prg.op("scalar", lambda e, c=c: e.activation(
                    out=s_ap(c, 12), in_=s_ap(c, 15), func=AF.Ln),
                    waits=[("st2", st2ev[c])], inc="sfa")
            for c in (c0, c0 + 1):
                ln2v[c] = prg.op("scalar", lambda e, c=c: e.activation(
                    out=s_ap(c, 13), in_=s_ap(c, 12), func=AF.Exp, scale=-0.5),
                    waits=[("sfa", t2v)] if c == c0 else (), inc="ln1")

        def prep_dve_negm2(c0):
            for c in (c0, c0 + 1):
                st2v[c] = prg.op("vector", lambda e, c=c: e.tensor_scalar(
                    out=s_ap(c, 14), in0=s_ap(c, 9), scalar1=s_ap(c, 13),
                    scalar2=-1.0, op0=ALU.mult, op1=ALU.mult),
                    waits=[("ln1", ln2v[c])], inc="st2")

        def prep_pe_trans(c, hb_p, bnk):
            w = [("xbf", xbfv[c])]
            if hb_p >= 1:
                w.append(("cmbs", cmbv[2 * hb_p - 1]))
            for j in range(8):
                xtrv[c] = prg.op("tensor", lambda e, c=c, j=j, bnk=bnk: e.transpose(
                    out=PSH[:, bnk * 1024 + j * P:bnk * 1024 + (j + 1) * P],
                    in_=xnb[:, c % 4, j * P:(j + 1) * P], identity=idb[:]),
                    waits=w if j == 0 else (),
                    inc="xtr" if j == 7 else None)

        def prep_pe_bcast(hb_p):
            c0p, c1p = 2 * hb_p, 2 * hb_p + 1
            for cc, c in ((0, c0p), (1, c1p)):
                w = [("st2", st2v[c])]
                if cc == 0:
                    w.append(("xnt", xntv[c0p]))
                for i, col in ((0, 13), (1, 14)):
                    bcav[hb_p] = prg.op("tensor", lambda e, c=c, i=i, col=col, cc=cc: e.matmul(
                        out=PS[:, 6 * 512 + (cc * 2 + i) * P:6 * 512 + (cc * 2 + i + 1) * P],
                        lhsT=s_ap(c, col).broadcast_to([P, P]), rhs=idf[:],
                        start=True, stop=True),
                        waits=w if i == 0 else (),
                        inc="bca" if (cc == 1 and i == 1) else None)

        def prep_dve_evac_xnt(c, hb_p, bnk):
            cc = c % 2
            w = [("xtr", xtrv[c])]
            if hb_p >= 2:
                w.append(("qk", qkv_[hb_p - 2]))
            xntv[c] = prg.op("vector", lambda e, hb_p=hb_p, cc=cc, bnk=bnk: e.tensor_copy(
                out=xnT[:, hb_p % 2, :, cc * P:(cc + 1) * P],
                in_=PSH[:, bnk * 1024:(bnk + 1) * 1024].rearrange("p (j t) -> p j t", j=8)),
                waits=w, inc="xnt")

        def prep_dve_evac_bcast(hb_p):
            w = [("bca", bcav[hb_p])]
            if hb_p >= 2:
                w.append(("xnn", xnnv[hb_p - 2]))
            prg.op("vector", lambda e, hb_p=hb_p: e.tensor_copy(
                out=r2sb[:, hb_p % 2, 0, :].rearrange("p (c x) -> p c x", c=2),
                in_=PS[:, 6 * 512:6 * 512 + 512].rearrange("p (c i x) -> p c i x", c=2, i=2)[:, :, 0, :]),
                waits=w)
            bcev[hb_p] = prg.op("vector", lambda e, hb_p=hb_p: e.tensor_copy(
                out=r2sb[:, hb_p % 2, 1, :].rearrange("p (c x) -> p c x", c=2),
                in_=PS[:, 6 * 512:6 * 512 + 512].rearrange("p (c i x) -> p c i x", c=2, i=2)[:, :, 1, :]),
                inc="bce")

        def prep_pool_xnn(hb_p):
            c1 = 2 * hb_p + 1
            w = [("xnt", xntv[c1]), ("bce", bcev[hb_p])]
            if hb_p >= 2:
                w.append(("peyq", 8 * (hb_p - 2) + 8))
            r2rep = r2sb[:, hb_p % 2, 0, :].unsqueeze(1).broadcast_to([P, 8, 256])
            nmrep = r2sb[:, hb_p % 2, 1, :].unsqueeze(1).broadcast_to([P, 8, 256])
            dst = xnnT[:, hb_p % 2, :, :]
            src = xnT[:, hb_p % 2, :, :]
            prg.op("gpsimd", lambda e: e.tensor_tensor(out=dst, in0=src, in1=r2rep,
                                                       op=ALU.mult), waits=w)
            prg.op("gpsimd", lambda e: e.tensor_tensor(out=dst, in0=dst, in1=nmrep,
                                                       op=ALU.add))
            for j in range(8):
                xnnv[hb_p] = prg.op("gpsimd", lambda e, j=j, hb_p=hb_p: e.tensor_scalar(
                    out=xnnT[:, hb_p % 2, j, :], in0=xnnT[:, hb_p % 2, j, :],
                    scalar1=gbc[:, j:j + 1], scalar2=gbc[:, 8 + j:9 + j],
                    op0=ALU.mult, op1=ALU.add),
                    inc="xnn" if j == 7 else None)

        def p1(hb):
            c0, c1 = 2 * hb, 2 * hb + 1
            w0 = [("xnt", xntv[c1]), ("bce", bcev[hb])]
            if hb == 0:
                w0.append(("dqc", dq_wv))
            for t in range(2):
                for j in range(8):
                    qkv_[hb] = prg.op("tensor", lambda e, hb=hb, t=t, j=j: e.matmul(
                        out=PS[:, 6 * 512 + t * 256:6 * 512 + (t + 1) * 256],
                        lhsT=wqT[:, j, t * P:(t + 1) * P], rhs=xnT[:, hb % 2, j, :],
                        start=(j == 0), stop=(j == 7)),
                        waits=w0 if (t == 0 and j == 0) else (),
                        inc="qk" if (t == 1 and j == 7) else None)
            for t in range(2):
                for j in range(8):
                    qkv_[hb] = prg.op("tensor", lambda e, hb=hb, t=t, j=j: e.matmul(
                        out=PS[:, 7 * 512 + t * 256:7 * 512 + (t + 1) * 256],
                        lhsT=wkT[:, j, t * P:(t + 1) * P], rhs=xnT[:, hb % 2, j, :],
                        start=(j == 0), stop=(j == 7)),
                        inc="qk" if (t == 1 and j == 7) else None)
            wv_ = []
            if hb >= 1:
                wv_ = [("zu", zuv[c0 - 2]), ("vev", vevv[c1 - 2])]
            for cc in range(2):
                for j in range(8):
                    qkv_[hb] = prg.op("tensor", lambda e, hb=hb, cc=cc, j=j: e.matmul(
                        out=PS[:, 2 * 512 + cc * 256:2 * 512 + (cc + 1) * 256],
                        lhsT=xnT[:, hb % 2, j, cc * P:(cc + 1) * P], rhs=wvT[:, j, :],
                        start=(j == 0), stop=(j == 7)),
                        waits=wv_ if (cc == 0 and j == 0) else (),
                        inc="qk" if (cc == 1 and j == 7) else None)

        def sigma(hb):
            qdone = qkv_[hb] - 2
            kdone = qkv_[hb] - 1
            for idx, (bnk, dstT, wv) in enumerate(((6, sqT, qdone), (7, skT, kdone))):
                w = [("qk", wv)]
                if hb >= 2:
                    w.append(("attd", attv[2 * hb - 3]))
                prg.op("vector", lambda e, bnk=bnk, idx=idx: e.tensor_scalar(
                    out=elu[:, idx, :], in0=PS[:, bnk * 512:(bnk + 1) * 512],
                    scalar1=0.0, scalar2=None, op0=ALU.min),
                    waits=w, inc="sgm")
                mv = C["sgm"]
                prg.op("scalar", lambda e, idx=idx: e.activation(
                    out=elu[:, idx, :], in_=elu[:, idx, :], func=AF.Exp),
                    waits=[("sgm", mv)], inc="sge")
                ev = C["sge"]
                sgv[hb] = prg.op("vector", lambda e, bnk=bnk, idx=idx, hb=hb, dstT=dstT: e.scalar_tensor_tensor(
                    out=dstT[:, hb % 2, :, :].rearrange("p t x -> p (t x)"),
                    in0=PS[:, bnk * 512:(bnk + 1) * 512], scalar=1.0,
                    in1=elu[:, idx, :], op0=ALU.add, op1=ALU.max),
                    waits=[("sge", ev)], inc="sg")

        def v_evac(hb):
            c0 = 2 * hb
            for cc in range(2):
                c = c0 + cc
                w = [("qk", qkv_[hb])]
                if c >= 4:
                    w.append(("attd", attv[c - 4]))
                prg.op("vector", lambda e, cc=cc, c=c: e.tensor_copy(
                    out=vE[:, c % 4, 0:D],
                    in_=PS[:, 2 * 512 + cc * 256:2 * 512 + (cc + 1) * 256]),
                    waits=w)
                vevv[c] = prg.op("vector", lambda e, cc=cc, c=c: e.tensor_scalar(
                    out=vEg[:, c % 4, 0:D],
                    in0=PS[:, 2 * 512 + cc * 256:2 * 512 + (cc + 1) * 256],
                    scalar1=gcol[:, 1:2], scalar2=None, op0=ALU.mult), inc="vev")


        def attn_full(c, hb):
            cc = c % 2
            zb = 2 if cc == 0 else 4
            w = [("sg", sgv[hb])]
            if c >= 1:
                w.append(("cmbs", cmbv[c - 1]))
            for t in range(2):
                sktv[c] = prg.op("tensor", lambda e, t=t, hb=hb, cc=cc: e.transpose(
                    out=PSH[:, 6 * 1024 + 640 + t * P:6 * 1024 + 640 + (t + 1) * P],
                    in_=skT[:, hb % 2, t, cc * P:(cc + 1) * P], identity=idb[:]),
                    waits=w if t == 0 else (), inc="skt" if t == 1 else None)
            wd = [("skt", sktv[c])]
            if c >= 4:
                wd.append(("gst", gstv[c - 4]))
            skrv[c] = prg.op("vector", lambda e, c=c: e.tensor_copy(
                out=skR[:, c % 4, :], in_=PSH[:, 6 * 1024 + 640:6 * 1024 + 896]),
                waits=wd, inc="skr")
            prg.op("tensor", lambda e, hb=hb, cc=cc: e.matmul(
                out=PS[:, 7 * 512 + 320:7 * 512 + 448],
                lhsT=skT[:, hb % 2, 0, cc * P:(cc + 1) * P],
                rhs=sqT[:, hb % 2, 0, cc * P:(cc + 1) * P],
                start=True, stop=False),
                waits=[("smc", smcv[c - 1])] if c >= 1 else ())
            sctv[c] = prg.op("tensor", lambda e, hb=hb, cc=cc: e.matmul(
                out=PS[:, 7 * 512 + 320:7 * 512 + 448],
                lhsT=skT[:, hb % 2, 1, cc * P:(cc + 1) * P],
                rhs=sqT[:, hb % 2, 1, cc * P:(cc + 1) * P],
                start=False, stop=True), inc="sct")
            wd = [("sct", sctv[c])]
            if c >= 2:
                wd.append(("attd", attv[c - 1]))
            smcv[c] = prg.op("vector", lambda e, c=c: e.tensor_tensor(
                out=smx[:, c % 2, :], in0=PS[:, 7 * 512 + 320:7 * 512 + 448],
                in1=msk[:], op=ALU.mult), waits=wd, inc="smc")
            wg = [("vev", vevv[c]), ("skr", skrv[c])]
            if c >= 1:
                wg.append(("ste", stev[c]))
            if hb >= 1 and cc == 0:
                wg.append(("yq", yqv[8 * hb]))
            if hb == 0 and cc == 0:
                wg.append(("dqc", dq_wo))
            for t in range(2):
                gstv[c] = prg.op("tensor", lambda e, t=t, c=c: e.matmul(
                    out=PS[:, t * 256:(t + 1) * 256],
                    lhsT=skR[:, c % 4, t * P:(t + 1) * P], rhs=vE[:, c % 4, 0:D],
                    start=True, stop=True),
                    waits=wg if t == 0 else (), inc="gst" if t == 1 else None)
            for t in range(2):
                attv[c] = prg.op("tensor", lambda e, t=t, hb=hb, cc=cc: e.matmul(
                    out=PS[:, 7 * 512:7 * 512 + 257],
                    lhsT=sqT[:, hb % 2, t, cc * P:(cc + 1) * P],
                    rhs=mext[:, t, 0:257], start=(t == 0), stop=(t == 1)),
                    waits=[("sfv", sfv_mext)] if (t == 0 and c == 0) else (),
                    inc="attd" if t == 1 else None)
            wi = [("smc", smcv[c])]
            if c >= 1:
                wi.append(("ste", stev[c]))
            for t in range(2):
                prg.op("tensor", lambda e, t=t, hb=hb, cc=cc, c=c: e.matmul(
                    out=PS[:, 6 * 512:6 * 512 + 257],
                    lhsT=sqT[:, hb % 2, t, cc * P:(cc + 1) * P],
                    rhs=stE[:, c % 2, t, 0:257], start=(t == 0), stop=False),
                    waits=wi if t == 0 else ())
            attv[c] = prg.op("tensor", lambda e, c=c: e.matmul(
                out=PS[:, 6 * 512:6 * 512 + 257],
                lhsT=smx[:, c % 2, :], rhs=vEg[:, c % 4, 0:257],
                start=False, stop=True), inc="attd")
            pc, nx = c % 2, (c + 1) % 2
            prg.op("vector", lambda e, pc=pc, nx=nx: e.scalar_tensor_tensor(
                out=kvst[:, nx, 0, :], in0=PS[:, 0:256], scalar=gcol[:, 1:2],
                in1=kvst[:, pc, 0, :], op0=ALU.mult, op1=ALU.add),
                waits=[("gst", gstv[c])], inc="sfv")
            kvv = prg.op("vector", lambda e, pc=pc, nx=nx: e.scalar_tensor_tensor(
                out=kvst[:, nx, 1, :], in0=PS[:, 256:512], scalar=gcol[:, 1:2],
                in1=kvst[:, pc, 1, :], op0=ALU.mult, op1=ALU.add), inc="sfv")
            redv = 0
            for t in range(2):
                redv = prg.op("vector", lambda e, t=t, hb=hb, cc=cc: e.reduce_sum(
                    out=dump[:, 64 + t:65 + t],
                    in_=skT[:, hb % 2, t, cc * P:(cc + 1) * P], axis=AX.X),
                    inc="sfv")
            ksv = 0
            for t in range(2):
                ksv = prg.op("vector", lambda e, t=t, pc=pc, nx=nx: e.tensor_add(
                    out=kss[:, nx * 2 + t:nx * 2 + t + 1],
                    in0=kss[:, pc * 2 + t:pc * 2 + t + 1],
                    in1=dump[:, 64 + t:65 + t]),
                    waits=[("sfv", redv)] if t == 0 else (), inc="sfv")
            wc = [("attd", attv[c])]
            prg.op("vector", lambda e: e.tensor_scalar(
                out=dump[:, 66:67], in0=PS[:, 7 * 512 + 256:7 * 512 + 257],
                scalar1=EPS_ATT, scalar2=None, op0=ALU.max), waits=wc, inc="sfv")
            dav = prg.op("vector", lambda e: e.tensor_scalar(
                out=dump[:, 67:68], in0=PS[:, 6 * 512 + 256:6 * 512 + 257],
                scalar1=EPS_ATT, scalar2=None, op0=ALU.max), inc="sfv")
            rmv = prg.op("vector", lambda e: e.reciprocal(out=dump[:, 68:69], in_=dump[:, 66:67]),
                         waits=[("sfv", dav)], inc="sfv")
            rav = prg.op("vector", lambda e: e.reciprocal(out=dump[:, 69:70], in_=dump[:, 67:68]),
                         inc="sfv")
            t1v = prg.op("vector", lambda e, c=c: e.tensor_scalar(
                out=dump[:, 128:384], in0=PS[:, 7 * 512:7 * 512 + 256],
                scalar1=dump[:, 68:69], scalar2=None, op0=ALU.mult),
                waits=[("sfv", rmv)], inc="sfv")
            prg.op("vector", lambda e, nx=nx: e.tensor_copy(
                out=stE[:, nx, 0:2, 0:D],
                in_=kvst[:, nx, 0:2, :]), waits=[("sfv", kvv)])
            stev[c + 1] = prg.op("vector", lambda e, nx=nx: e.tensor_copy(
                out=stE[:, nx, 0:2, D:D + 1],
                in_=kss[:, nx * 2:nx * 2 + 2].unsqueeze(2)),
                waits=[("sfv", ksv)], inc="ste")
            wcm = []
            if c >= 2:
                wcm.append(("cbt", cbtv[c - 2]))
            cmbv[c] = prg.op("vector", lambda e, c=c: e.scalar_tensor_tensor(
                out=cmb[:, c % 2, :], in0=PS[:, 6 * 512:6 * 512 + 256],
                scalar=dump[:, 69:70], in1=dump[:, 128:384],
                op0=ALU.mult, op1=ALU.add), waits=wcm + [("sfv", t1v)], inc="cmbs")
            for t in range(2):
                cbtv[c] = prg.op("tensor", lambda e, t=t, c=c: e.transpose(
                    out=PSH[:, 1536 + t * P:1536 + (t + 1) * P],
                    in_=cmb[:, c % 2, t * P:(t + 1) * P], identity=idb[:]),
                    waits=[("cmbs", cmbv[c])] if t == 0 else (),
                    inc="cbt" if t == 1 else None)
            wd = [("cbt", cbtv[c])]
            if c >= 2:
                wd.append(("woz", wozv[c - 2]))
            cbev[c] = prg.op("vector", lambda e, c=c: e.tensor_copy(
                out=cbT[:, c % 2, :, :].rearrange("p t x -> p (t x)"),
                in_=PSH[:, 1536:1792]), waits=wd, inc="cbe")
            wz = [("cbe", cbev[c])]
            if c >= 2:
                wz.append(("zu", zuv[c - 2]))
            if c == 0:
                wz.append(("dqc", dq_b2))
            for t in range(2):
                for half in range(2):
                    b = zb + half
                    prg.op("tensor", lambda e, t=t, half=half, c=c, b=b: e.matmul(
                        out=PS[:, b * 512:(b + 1) * 512],
                        lhsT=cbT[:, c % 2, t, :],
                        rhs=woT[:, t, half * 512:(half + 1) * 512],
                        start=(t == 0), stop=False),
                        waits=wz if (t == 0 and half == 0) else ())
            for half in range(2):
                b = zb + half
                wozv[c] = prg.op("tensor", lambda e, half=half, b=b: e.matmul(
                    out=PS[:, b * 512:(b + 1) * 512],
                    lhsT=onesb[0:1, 0:P], rhs=b2r[0:1, half * 512:(half + 1) * 512],
                    start=False, stop=False),
                    inc="woz" if half == 1 else None)

        def w2_load(hb, gidx):
            if hb == 0:
                return
            G0 = hb * 16 + gidx
            t0 = 2 * gidx
            w = []
            if G0 >= 4:
                w.append(("w2fq", (2 * G0 - 7) // 4 + 1))
            if hb == 1:
                w.append(("sp%d" % (t0 % 8), splv[24 + t0 % 8]))
                w.append(("sp%d" % ((t0 + 1) % 8), splv[24 + (t0 + 1) % 8]))
            w2lv[G0] = ("wl%d" % (gidx % 4), prg.dma(
                "sync", w2r[:, (t0 % 8):(t0 % 8) + 2, :],
                w2s_h[t0:t0 + 2].rearrange("t p h -> p t h"),
                waits=w, inc="wl%d" % (gidx % 4)))

        def ffn(hb, do_prep):
            c0, c1 = 2 * hb, 2 * hb + 1
            hb_p = hb + 1

            def fc2_quad(qq, final):
                gq = hb * 8 + qq
                for fi in range(4):
                    ft = 4 * qq + fi
                    G0 = hb * 16 + ft // 2
                    if hb == 0:
                        wf = [("yq", yqv[hb * 8 + qq + 1]), ("dqc", dqw2t[ft])]
                    else:
                        wf = [("yq", yqv[hb * 8 + qq + 1]), w2lv[G0]]
                    for cc2 in range(2):
                        for half in range(2):
                            b = (2 if cc2 == 0 else 4) + half
                            inc = None
                            if final and fi == 3 and half == 1:
                                inc = "pez"
                            elif (not final) and fi == 3 and half == 1 and cc2 == 1:
                                inc = "w2fq"
                            r = prg.op("tensor", lambda e, ft=ft, cc2=cc2, half=half, b=b, final=final, fi=fi: e.matmul(
                                out=PS[:, b * 512:(b + 1) * 512],
                                lhsT=yT[:, ft % 8, cc2 * P:(cc2 + 1) * P],
                                rhs=w2r[:, ft % 8, half * 512:(half + 1) * 512],
                                start=False, stop=(final and fi == 3)),
                                waits=wf if (cc2 == 0 and half == 0) else (),
                                inc=inc)
                            if inc == "pez":
                                pezv[c0 + cc2] = r
                if final:
                    w2fv[gq] = prg.op("tensor", lambda e: e.nop(), inc="w2fq")

            for q in range(8):
                gq = hb * 8 + q
                wq_ = [("xnn", xnnv[hb]), ("cbe", cbev[c1])]
                if hb == 0 and q >= 1:
                    wq_.append(("dqc", dq_w1_qt[q // 2]))
                if hb == 0 and q == 0:
                    wq_.append(("dqc", dq_w1_qt[0]))
                if gq >= 1:
                    wq_.append(("yq", gq))
                if q == 0:
                    wq_.append(("ste", stev[c1 + 1]))
                for fi in range(4):
                    ft = 4 * q + fi
                    b = 0 if fi < 2 else 1
                    off = (fi % 2) * 256
                    for j in range(8):
                        peyv[gq] = prg.op("tensor", lambda e, ft=ft, j=j, b=b, off=off, hb=hb: e.matmul(
                            out=PS[:, b * 512 + off:b * 512 + off + 256],
                            lhsT=w1s[:, j, ft * P:(ft + 1) * P],
                            rhs=xnnT[:, hb % 2, j, :], start=(j == 0), stop=(j == 7)),
                            waits=wq_ if (fi == 0 and j == 0) else (),
                            inc="peyq" if (fi == 3 and j == 7) else None)
                if do_prep:
                    if q == 0:
                        prep_pe_trans(2 * hb_p, hb_p, 6)
                        prep_dve_evac_xnt(2 * hb_p, hb_p, 6)
                    elif q == 1:
                        prep_pe_trans(2 * hb_p + 1, hb_p, 7)
                        prep_dve_evac_xnt(2 * hb_p + 1, hb_p, 7)
                for fi in range(4):
                    ft = 4 * q + fi
                    b = 0 if fi < 2 else 1
                    off = (fi % 2) * 256
                    yqv[gq + 1] = prg.op("scalar", lambda e, ft=ft, b=b, off=off: e.activation(
                        out=yT[:, ft % 8, :], in_=PS[:, b * 512 + off:b * 512 + off + 256],
                        func=AF.Silu, bias=b1c[:, ft:ft + 1]),
                        waits=[("peyq", peyv[gq])] if fi == 0 else (),
                        inc="yq" if fi == 3 else None)
                if 1 <= q <= 6:
                    w2_load(hb, 4 + 2 * (q - 1))
                    w2_load(hb, 4 + 2 * (q - 1) + 1)
                if q >= 1:
                    fc2_quad(q - 1, False)
                    if hb == 0 and q <= 4:
                        for t in range(16 + 4 * (q - 1), 20 + 4 * (q - 1)):
                            dq_w2_tile(t)
                if do_prep and q == 7:
                    prep_scalar_ln2_late(2 * hb_p)
                    prep_dve_negm2(2 * hb_p)
                    prep_pe_bcast(hb_p)
                    prep_dve_evac_bcast(hb_p)
            fc2_quad(7, True)

        def assembly_u(c):
            zb = 2 if c % 2 == 0 else 4
            w = [("pez", pezv[c])]
            if c >= 2:
                w.append(("zev", zevv[c - 2]))
            zuv[c] = prg.op("scalar", lambda e, c=c, zb=zb: e.activation(
                out=ubuf[:, c % 2, :], in_=PS[:, zb * 512:(zb + 2) * 512],
                func=AF.Identity, bias=s_ap(c, 0)), waits=w, inc="zu")

        def assembly_rest(c):
            w = [("zu", zuv[c])]
            if c >= 2:
                w.append(("ot%d" % (c % 2), otv[c - 2]))
            zevv[c] = prg.op("vector", lambda e, c=c: e.scalar_tensor_tensor(
                out=orow[:, c % 2, :], in0=xnb[:, c % 4, :], scalar=s_ap(c, 6),
                in1=ubuf[:, c % 2, :], op0=ALU.mult, op1=ALU.add),
                waits=w, inc="zev")
            otv[c] = prg.dma("sync", out_h[c * P:(c + 1) * P, :], orow[:, c % 2, :],
                             waits=[("zev", zevv[c])], inc="ot%d" % (c % 2))

        for c in range(3):
            x_load(c)
        prep_dve_stats(0)
        prep_scalar_ln1(0)
        prep_dve_ln1(0)
        prep_pool_xnf(0)
        prep_dve_sums(0)
        prep_dve_ln2(0)
        prep_scalar_ln2_late(0)
        prep_dve_negm2(0)
        prep_pe_trans(0, 0, 6)
        prep_dve_evac_xnt(0, 0, 6)
        prep_pe_trans(1, 0, 7)
        prep_dve_evac_xnt(1, 0, 7)
        prep_pe_bcast(0)
        prep_dve_evac_bcast(0)
        prep_pool_xnn(0)

        for hb in range(NHB):
            c0, c1 = 2 * hb, 2 * hb + 1
            x_load(c0 + 3)
            x_load(c0 + 4)
            for gidx in range(4):
                w2_load(hb, gidx)
            if hb >= 2:
                assembly_u(c0 - 2)
                assembly_u(c1 - 2)
            p1(hb)
            sigma(hb)
            if hb + 1 < NHB:
                prep_dve_stats(c0 + 2)
                prep_scalar_ln1(c0 + 2)
            v_evac(hb)
            attn_full(c0, hb)
            if hb >= 2:
                assembly_rest(c0 - 2)
                assembly_rest(c1 - 2)
            if hb + 1 < NHB:
                prep_dve_ln1(c0 + 2)
                prep_pool_xnf(c0 + 2)
                prep_dve_sums(c0 + 2)
                prep_dve_ln2(c0 + 2)
            attn_full(c1, hb)
            if hb == 0:
                for t in range(16):
                    dq_w2_tile(t)
            ffn(hb, do_prep=(hb + 1 < NHB))
            if hb == 0:
                assembly_u(0)
                assembly_u(1)
                assembly_rest(0)
                assembly_rest(1)
            if hb + 1 < NHB:
                prep_pool_xnn(hb + 1)
        assembly_u(NCH - 2)
        assembly_u(NCH - 1)
        assembly_rest(NCH - 2)
        assembly_rest(NCH - 1)

        if DEBUG2:
            prg.dma("sync", w1d_h, w1s[:, :, :].rearrange("p g f -> p (g f)"),
                    inc="pcon")
            prg.dma("sync", w2d_h, w2s_h, inc="pcon")
        if DEBUG:
            prg.dma("sync", dbg_h, dbg[:], waits=[("zev", C["zev"])], inc="pcon")
        prg.op("sync", lambda e: e.nop(),
               waits=[("ot0", C["ot0"]), ("ot1", C["ot1"]), ("pcon", C["pcon"])])

        prg.replay(block)

    return nc


_NC_CACHE = None


def kernel(**inputs):
    global _NC_CACHE
    xs = np.asarray(inputs["L0_attn_hidden_states"], dtype=np.float32)
    B = xs.shape[0]
    ident_b = np.eye(P, dtype=ml_dtypes.bfloat16)
    ident_f = np.eye(P, dtype=np.float32)
    mask = np.triu(np.ones((P, P), dtype=np.float32)).astype(ml_dtypes.bfloat16)

    common = {
        "wq": np.ascontiguousarray(np.asarray(inputs["L0_attn_w_q"], np.float32)),
        "wk": np.ascontiguousarray(np.asarray(inputs["L0_attn_w_k"], np.float32)),
        "wv": np.ascontiguousarray(np.asarray(inputs["L0_attn_w_v"], np.float32)),
        "wo": np.ascontiguousarray(np.asarray(inputs["L0_attn_w_o"], np.float32)),
        "gate": np.ascontiguousarray(np.asarray(inputs["L0_attn_gate"], np.float32)),
        "mem": np.ascontiguousarray(np.asarray(inputs["L0_attn_memory"], np.float32)),
        "mn": np.ascontiguousarray(np.asarray(inputs["L0_attn_memory_norm"], np.float32)),
        "gam": np.ascontiguousarray(np.asarray(inputs["L0_ffn_ln_gamma"], np.float32)),
        "bet": np.ascontiguousarray(np.asarray(inputs["L0_ffn_ln_beta"], np.float32)),
        "q1": np.ascontiguousarray(np.asarray(inputs["L0_ffn_fc1_q"], np.int32)),
        "s1": np.ascontiguousarray(np.asarray(inputs["L0_ffn_fc1_scales"], np.float32)),
        "b1": np.ascontiguousarray(np.asarray(inputs["L0_ffn_fc1_bias"], np.float32)),
        "q2": np.ascontiguousarray(np.asarray(inputs["L0_ffn_fc2_q"], np.int32)),
        "s2": np.ascontiguousarray(np.asarray(inputs["L0_ffn_fc2_scales"], np.float32)),
        "b2": np.ascontiguousarray(np.asarray(inputs["L0_ffn_fc2_bias"], np.float32)),
        "idb_d": ident_b, "idf_d": ident_f, "msk_d": mask,
    }
    in_maps = []
    for b in range(B):
        m = dict(common)
        m["x"] = np.ascontiguousarray(xs[b])
        in_maps.append(m)

    if _NC_CACHE is None:
        _NC_CACHE = build_nc()
    res = run_bass_kernel_spmd(_NC_CACHE, in_maps, core_ids=list(range(B)))
    out = np.stack([np.asarray(r["out"], np.float32) for r in res.results], axis=0)
    if DEBUG:
        global LAST_DBG
        LAST_DBG = {b: np.asarray(res.results[b]["dbg"], np.float32) for b in range(B)}
    if DEBUG2:
        global LAST_DBG2
        LAST_DBG2 = {b: (np.asarray(res.results[b]["w1dump"]),
                         np.asarray(res.results[b]["w2dump"])) for b in range(B)}
    return out


# version 29
# speedup vs baseline: 1.0206x; 1.0139x over previous
import numpy as np
import ml_dtypes
from contextlib import ExitStack

import concourse.bass as bass
import concourse.mybir as mybir
from concourse.bass_utils import run_bass_kernel_spmd

F32 = mybir.dt.float32
BF16 = mybir.dt.bfloat16
I32 = mybir.dt.int32
AF = mybir.ActivationFunctionType
ALU = mybir.AluOpType
AX = mybir.AxisListType

P = 128
S = 4096
H = 1024
D = 256
FF = 4096
NCH = S // P
NHB = NCH // 2
EPS_LN = 1e-5
EPS_ATT = 1e-6

ENGS = ("sync", "scalar", "vector", "tensor", "gpsimd")
DEBUG = False
DEBUG2 = False
LAST_DBG = {}
LAST_DBG2 = {}


class Prog:
    def __init__(self, nc, ctx):
        self.nc = nc
        self.ops = {e: [] for e in ENGS}
        self.sem = {}
        self.cnt = {}
        self.ctx = ctx

    def new_sem(self, name):
        self.sem[name] = self.ctx.enter_context(self.nc.semaphore(name))
        self.cnt[name] = 0

    def op(self, eng, fn, waits=(), inc=None, inc_amt=1):
        waits = tuple((s, v) for (s, v) in waits if v > 0)
        self.ops[eng].append((waits, fn, inc, inc_amt))
        if inc is not None:
            self.cnt[inc] += inc_amt
            return self.cnt[inc]
        return None

    def dma(self, eng, out, in_, waits=(), inc=None, **kw):
        def fn(e, out=out, in_=in_, kw=kw):
            return e.dma_start(out=out, in_=in_, **kw)
        return self.op(eng, fn, waits, inc, 16)

    def replay(self, block):
        for ename in ENGS:
            ops = self.ops[ename]

            def make(ops, sems):
                def body(eng):
                    last = {}
                    for waits, fn, inc, inc_amt in ops:
                        for sname, val in waits:
                            if last.get(sname, 0) < val:
                                eng.wait_ge(sems[sname], val)
                                last[sname] = val
                        inst = fn(eng)
                        if inc is not None:
                            inst.then_inc(sems[inc], inc_amt)
                return body
            getattr(block, ename)(make(ops, self.sem))


def build_nc():
    nc = bass.Bass("TRN2", target_bir_lowering=False, debug=False)

    x_h = nc.dram_tensor("x", [S, H], F32, kind="ExternalInput").ap()
    wq_h = nc.dram_tensor("wq", [D, H], F32, kind="ExternalInput").ap()
    wk_h = nc.dram_tensor("wk", [D, H], F32, kind="ExternalInput").ap()
    wv_h = nc.dram_tensor("wv", [D, H], F32, kind="ExternalInput").ap()
    wo_h = nc.dram_tensor("wo", [H, D], F32, kind="ExternalInput").ap()
    gate_h = nc.dram_tensor("gate", [1], F32, kind="ExternalInput").ap()
    mem_h = nc.dram_tensor("mem", [D, D], F32, kind="ExternalInput").ap()
    mn_h = nc.dram_tensor("mn", [D], F32, kind="ExternalInput").ap()
    gam_h = nc.dram_tensor("gam", [H], F32, kind="ExternalInput").ap()
    bet_h = nc.dram_tensor("bet", [H], F32, kind="ExternalInput").ap()
    q1_h = nc.dram_tensor("q1", [H, FF], I32, kind="ExternalInput").ap()
    s1_h = nc.dram_tensor("s1", [8, FF], F32, kind="ExternalInput").ap()
    b1_h = nc.dram_tensor("b1", [FF], F32, kind="ExternalInput").ap()
    q2_h = nc.dram_tensor("q2", [FF, H], I32, kind="ExternalInput").ap()
    s2_h = nc.dram_tensor("s2", [32, H], F32, kind="ExternalInput").ap()
    b2_h = nc.dram_tensor("b2", [H], F32, kind="ExternalInput").ap()
    idb_h = nc.dram_tensor("idb_d", [P, P], BF16, kind="ExternalInput").ap()
    idf_h = nc.dram_tensor("idf_d", [P, P], F32, kind="ExternalInput").ap()
    msk_h = nc.dram_tensor("msk_d", [P, P], BF16, kind="ExternalInput").ap()

    out_h = nc.dram_tensor("out", [S, H], F32, kind="ExternalOutput").ap()
    dbg_h = nc.dram_tensor("dbg", [P, 4096], F32, kind="ExternalOutput").ap() if DEBUG else None
    w1d_h = nc.dram_tensor("w1dump", [P, 8 * FF], BF16, kind="ExternalOutput").ap() if DEBUG2 else None
    w2d_h = nc.dram_tensor("w2dump", [32, P, H], BF16, kind="ExternalOutput").ap() if DEBUG2 else None
    w2s_h = nc.dram_tensor("w2scr", [32, P, H], BF16, kind="Internal").ap()
    s2h_h = nc.dram_tensor("s2half", [32, H], F32, kind="Internal").ap()

    with ExitStack() as ctx:
        def sb(name, shape, dtype):
            return ctx.enter_context(nc.sbuf_tensor(name, shape, dtype))

        w1s = sb("w1s", [P, 8, FF], BF16)
        w2r = sb("w2r", [P, 8, H], BF16)
        wqT = sb("wqT", [P, 8, D], BF16)
        wkT = sb("wkT", [P, 8, D], BF16)
        wvT = sb("wvT", [P, 8, D], BF16)
        woT = sb("woT", [P, 2, H], BF16)
        idb = sb("idb", [P, P], BF16)
        idf = sb("idf", [P, P], F32)
        msk = sb("msk", [P, P], BF16)
        xbuf = sb("xbuf", [P, 3, H], F32)
        xnf = sb("xnf", [P, 2, H], F32)
        ov1 = sb("ov1", [P, 2, H], F32)
        ov2 = sb("ov2", [P, 2, H], F32)
        ov3 = sb("ov3", [P, 2, H], F32)
        ov4 = sb("ov4", [P, 2, H], BF16)
        ubuf = ov1[:]
        orow = ov2[:]
        wst = ov3[:]
        _ov3b = ov3[:, 0:2, :].rearrange("p t h -> p (t h)").bitcast(BF16)
        yT = _ov3b[:, 0:2048].rearrange("p (f x) -> p f x", f=8)
        r2sb = _ov3b[:, 2048:3072].rearrange("p (a i x) -> p a i x", a=2, i=2)
        elu = _ov3b[:, 3072:4096].rearrange("p (i x) -> p i x", i=2)
        wstb = ov4[:]
        _ov4b = ov4[:, 0:2, :].rearrange("p t h -> p (t h)")
        skR = _ov4b[:, 0:1024].rearrange("p (a x) -> p a x", a=4)
        cmb = _ov4b[:, 1024:1536].rearrange("p (a x) -> p a x", a=2)
        cbT = _ov4b[:, 1536:2048].rearrange("p (a t x) -> p a t x", a=2, t=2)
        dump = sb("dump", [P, 384], F32)
        xnb = sb("xnb", [P, 4, H], BF16)
        xnT = sb("xnT", [P, 2, 8, D], BF16)
        xnnT = sb("xnnT", [P, 2, 8, D], BF16)
        sqT = sb("sqT", [P, 2, 2, D], BF16)
        skT = sb("skT", [P, 2, 2, D], BF16)
        vE = sb("vE", [P, 4, 258], BF16)
        vEg = sb("vEg", [P, 4, 258], BF16)
        smx = sb("smx", [P, 2, P], BF16)
        stat = sb("stat", [P, 8, 16], F32)
        kvst = sb("kvst", [P, 2, 2, D], F32)
        kss = sb("kss", [P, 4], F32)
        stE = sb("stE", [P, 2, 2, 258], BF16)
        mext = sb("mext", [P, 2, 258], BF16)
        mncol = sb("mncol", [P, 2], F32)
        b1c = sb("b1c", [P, 32], F32)
        sbc = sb("sbc", [P, 2, 512], F32)
        bstg = sbc[0:1, 0:2, :].rearrange("o t h -> o (t h)")
        b2r = sb("b2r", [1, H], BF16)
        onesb = sb("onesb", [1, 512], BF16)
        onesf = sb("onesf", [1, P], F32)
        halvesf = sb("halvesf", [1, P], F32)
        qf = sb("qf", [P, 2, 1024], I32)
        qff = qf[:, :, :].bitcast(F32)
        memstg = qff[:, 1, 0:2 * D].rearrange("p (t e) -> p t e", t=2)
        grow = sb("grow", [1, 16], F32)
        gcol = sb("gcol", [P, 4], F32)
        mnr = sb("mnr", [1, D], F32)
        gbc = sb("gbc", [P, 16], F32)
        dbg = w1s[:, 6:8, :].rearrange("p g f -> p (g f)").bitcast(F32) if DEBUG else None

        PS_t = ctx.enter_context(nc.psum_tensor("ps", [P, 8 * 512], F32))
        PS = PS_t[:]
        PSH = PS.bitcast(BF16)

        prg = Prog(nc, ctx)
        for s in ("x0", "x1", "x2", "ot0", "ot1", "wl0", "wl1", "wl2", "wl3",
                  "sp0", "sp1", "sp2", "sp3", "sp4", "sp5", "sp6", "sp7",
                  "qf0", "qf1", "sb0", "sb1", "pcon", "pmem", "pb1", "pgb",
                  "pb2", "ps2", "pwst", "w2fq", "dqc", "prc", "gse",
                  "st1", "ln1", "lnx", "st2", "xbf", "xtr", "xnt",
                  "bca", "bce", "xnn", "qk", "sgm", "sge", "sg",
                  "vev", "skt", "skr", "sct", "smc", "gst", "ste",
                  "attd", "cmbs", "cbt", "cbe", "yq", "peyq", "pez",
                  "zu", "zev", "woz", "sfv", "sfa", "xsm", "s2h"):
            prg.new_sem(s)
        C = prg.cnt

        block = ctx.enter_context(nc.Block())

        prg.dma("sync", idb[:], idb_h, inc="pcon")
        prg.dma("sync", idf[:], idf_h, inc="pcon")
        prg.dma("sync", msk[:], msk_h, inc="pcon")
        prg.dma("sync", grow[0:1, 0:1], gate_h.unsqueeze(0), inc="pcon")
        prg.dma("sync", mnr[:], mn_h.unsqueeze(0), inc="pcon")
        pre_consts = C["pcon"]
        prg.dma("sync", mncol[:, 0:1], mn_h[0:P].unsqueeze(1), inc="pmem")
        prg.dma("sync", mncol[:, 1:2], mn_h[P:D].unsqueeze(1), inc="pmem")
        prg.dma("sync", memstg,
                mem_h.rearrange("(t p) e -> p t e", p=P), inc="pmem")
        pre_mem = C["pmem"]
        prg.dma("gpsimd", qff[0:32, 0, 0:P], b1_h.rearrange("(a p) -> a p", p=P),
                inc="pb1")
        pre_b1 = C["pb1"]
        prg.op("tensor", lambda e: e.transpose(out=PS[:, 0:32], in_=qff[0:32, 0, 0:P],
                                               identity=idf[0:32, 0:32]),
               waits=[("pb1", pre_b1), ("pcon", pre_consts)], inc="prc")
        prc_b1 = C["prc"]
        prg.op("vector", lambda e: e.tensor_copy(out=b1c[:], in_=PS[:, 0:32]),
               waits=[("prc", prc_b1)], inc="dqc")
        dq_b1 = C["dqc"]
        prg.dma("sync", bstg[0:1, 0:H], b2_h.unsqueeze(0),
                waits=[("dqc", dq_b1)], inc="pb2")
        pre_b2 = C["pb2"]
        prg.op("vector", lambda e: e.tensor_scalar(
            out=b2r[0:1, 0:H], in0=bstg[0:1, 0:H], scalar1=0.5, scalar2=None,
            op0=ALU.mult), waits=[("pb2", pre_b2)], inc="dqc")
        dq_b2 = C["dqc"]
        for fn in (lambda e: e.memset(onesf[:], 1.0),
                   lambda e: e.memset(halvesf[:], 0.5),
                   lambda e: e.memset(onesb[:], 1.0),
                   lambda e: e.memset(kvst[:, :, :, :].rearrange("p a t x -> p (a t x)"), 0.0),
                   lambda e: e.memset(kss[:], 0.0),
                   lambda e: e.memset(stE[:, :, :, :].rearrange("p a t x -> p (a t x)"), 0.0),
                   lambda e: e.memset(vE[:, :, :].rearrange("p a x -> p (a x)"), 1.0),
                   lambda e: e.memset(vEg[:, :, :].rearrange("p a x -> p (a x)"), 1.0)):
            prg.op("vector", fn, inc="dqc")
        dq_memset = C["dqc"]

        prg.dma("gpsimd", qff[0:8, 0, 0:P], gam_h.rearrange("(g p) -> g p", p=P),
                waits=[("prc", prc_b1)], inc="pgb")
        prg.dma("gpsimd", qff[0:8, 0, P:2 * P], bet_h.rearrange("(g p) -> g p", p=P),
                inc="pgb")
        pre_gb = C["pgb"]
        prg.op("vector", lambda e: e.tensor_copy(out=wstb[0:8, 0, 0:2 * P],
                                                 in_=qff[0:8, 0, 0:2 * P]),
               waits=[("pgb", pre_gb)], inc="dqc")
        dq_gbc = C["dqc"]
        prg.op("tensor", lambda e: e.transpose(out=PSH[:, 0:8],
                                               in_=wstb[0:8, 0, 0:P],
                                               identity=idb[0:8, 0:8]),
               waits=[("dqc", dq_gbc), ("pcon", pre_consts)], inc="prc")
        prg.op("tensor", lambda e: e.transpose(out=PSH[:, 8:16],
                                               in_=wstb[0:8, 0, P:2 * P],
                                               identity=idb[0:8, 0:8]), inc="prc")
        prc_gb = C["prc"]
        prg.op("vector", lambda e: e.tensor_copy(out=gbc[:, 0:16], in_=PSH[:, 0:16]),
               waits=[("prc", prc_gb)], inc="dqc")
        dq_gb2 = C["dqc"]

        prg.op("scalar", lambda e: e.activation(out=grow[0:1, 1:2], in_=grow[0:1, 0:1],
                                                func=AF.Tanh, scale=0.5),
               waits=[("pcon", pre_consts)], inc="sfa")
        sfa_t = C["sfa"]
        prg.op("scalar", lambda e: e.activation(out=grow[0:1, 2:3], in_=grow[0:1, 1:2],
                                                func=AF.Copy, scale=0.5, bias=0.5),
               waits=[("sfa", sfa_t)], inc="sfa")
        sfa_g = C["sfa"]
        prg.op("vector", lambda e: e.reduce_sum(out=grow[0:1, 6:7], in_=mnr[:], axis=AX.X),
               waits=[("pcon", pre_consts)], inc="sfv")
        sfv_mn = C["sfv"]
        prg.op("vector", lambda e: e.tensor_single_scalar(
            out=grow[0:1, 3:4], in_=grow[0:1, 6:7], scalar=EPS_ATT, op=ALU.is_ge),
            waits=[("sfv", sfv_mn), ("sfa", sfa_g)], inc="sfv")
        sfv_f = C["sfv"]
        prg.op("vector", lambda e: e.tensor_mul(out=grow[0:1, 4:5], in0=grow[0:1, 2:3],
                                                in1=grow[0:1, 3:4]),
               waits=[("sfv", sfv_f)], inc="sfv")
        sfv_ge = C["sfv"]
        prg.op("vector", lambda e: e.tensor_scalar(
            out=grow[0:1, 5:6], in0=grow[0:1, 2:3], scalar1=-1.0, scalar2=1.0,
            op0=ALU.mult, op1=ALU.add), waits=[("sfv", sfv_ge)], inc="sfv")
        sfv_g1 = C["sfv"]
        prg.op("tensor", lambda e: e.matmul(out=PS[:, 0:2], lhsT=onesf[0:1, :],
                                            rhs=grow[0:1, 4:6], start=True, stop=True),
               waits=[("sfv", sfv_g1), ("dqc", dq_memset)], inc="prc")
        prc_g = C["prc"]
        prg.op("vector", lambda e: e.tensor_copy(out=gcol[:, 0:2], in_=PS[:, 0:2]),
               waits=[("prc", prc_g)], inc="sfv")
        sfv_gc = C["sfv"]
        prg.op("vector", lambda e: e.tensor_scalar(
            out=mext[:, 0:2, 0:D],
            in0=memstg,
            scalar1=gcol[:, 0:1], scalar2=None, op0=ALU.mult),
            waits=[("sfv", sfv_gc), ("pmem", pre_mem)], inc="sfv")
        prg.op("vector", lambda e: e.tensor_copy(
            out=mext[:, 0:2, D:D + 1],
            in_=mncol[:, 0:2].unsqueeze(2)), waits=[("pmem", pre_mem)], inc="sfv")
        sfv_mext = C["sfv"]

        def wload(src_h, dst):
            dmaw = [("dqc", C["dqc"]), ("sfv", sfv_mext)]
            prg.dma("gpsimd", wst[:, 0:2, :],
                    src_h.rearrange("(t p) h -> p t h", p=P),
                    waits=dmaw, inc="pwst")
            pr = C["pwst"]
            prg.op("vector", lambda e: e.tensor_copy(
                out=wstb[:, 0:2, :].rearrange("p t h -> p (t h)"),
                in_=wst[:, 0:2, :].rearrange("p t h -> p (t h)")),
                waits=[("pwst", pr), ("prc", C["prc"])], inc="dqc")
            dq = C["dqc"]
            for j in range(8):
                for t in range(2):
                    prg.op("tensor", lambda e, j=j, t=t: e.transpose(
                        out=PSH[:, (2 * j + t) * P:(2 * j + t + 1) * P],
                        in_=wstb[:, t, j * P:(j + 1) * P], identity=idb[:]),
                        waits=[("dqc", dq)] if (j == 0 and t == 0) else (),
                        inc="prc")
            pr2 = C["prc"]
            prg.op("vector", lambda e, dst=dst: e.tensor_copy(
                out=dst[:, :, :].rearrange("p j d -> p (j d)"),
                in_=PSH[:, 0:2048]),
                waits=[("prc", pr2)], inc="dqc")
            return C["dqc"]

        dq_wq = wload(wq_h, wqT)
        dq_wk = wload(wk_h, wkT)
        dq_wv = wload(wv_h, wvT)

        def wo_half(li):
            dmaw = [("dqc", C["dqc"])]
            prg.dma("gpsimd", wst[:, 0, :].rearrange("p (a d) -> p a d", a=4),
                    wo_h.rearrange("(a p) d -> p a d", p=P)[:, 4 * li:4 * li + 4, :],
                    waits=dmaw, inc="pwst")
            pr = C["pwst"]
            prg.op("vector", lambda e: e.tensor_copy(
                out=wstb[:, 0, :], in_=wst[:, 0, :]),
                waits=[("pwst", pr), ("prc", C["prc"])], inc="dqc")
            dq = C["dqc"]
            flat = wstb[:, 0, :]
            for a in range(4):
                for t in range(2):
                    prg.op("tensor", lambda e, a=a, t=t: e.transpose(
                        out=PSH[:, (2 * a + t) * P:(2 * a + t + 1) * P],
                        in_=flat[:, a * 256 + t * P:a * 256 + (t + 1) * P],
                        identity=idb[:]),
                        waits=[("dqc", dq)] if (a == 0 and t == 0) else (),
                        inc="prc")
            pr2 = C["prc"]
            prg.op("vector", lambda e, li=li: e.tensor_copy(
                out=woT[:, 0:2, li * 512:(li + 1) * 512].rearrange("p t (a x) -> p t a x", a=4),
                in_=PSH[:, 0:1024].rearrange("p (a t x) -> p t a x", a=4, t=2, x=P)),
                waits=[("prc", pr2)], inc="dqc")
            return C["dqc"]

        dq_wo = wo_half(0)
        dq_wo = wo_half(1)

        prg.dma("gpsimd", qff[0:32, 1, :], s2_h,
                waits=[("sfv", sfv_mext), ("dqc", dq_wo)], inc="ps2")
        pre_s2 = C["ps2"]
        prg.op("vector", lambda e: e.tensor_scalar(
            out=qff[0:32, 1, :], in0=qff[0:32, 1, :], scalar1=0.5, scalar2=None,
            op0=ALU.mult), waits=[("ps2", pre_s2)], inc="dqc")
        dq_s2 = C["dqc"]
        prg.dma("gpsimd", s2h_h, qff[0:32, 1, :], waits=[("dqc", dq_s2)], inc="s2h")
        s2h_done = C["s2h"]

        dq_slot = [dq_s2, dq_s2]
        sbc_slot = [dq_b2, dq_b2]
        sbci = [0]

        def dq_w1_group(g):
            for qt in range(4):
                slot = qt % 2
                w = [("dqc", dq_slot[slot])]
                if g == 0 and qt == 0:
                    w.append(("prc", prc_b1))
                if g == 0 and qt == 1:
                    w.append(("s2h", s2h_done))
                pr_q = prg.dma("sync", qf[:, slot, :],
                               q1_h[g * P:(g + 1) * P, qt * 1024:(qt + 1) * 1024],
                               waits=w, inc="qf%d" % slot)
                for h2 in range(2):
                    si = sbci[0] % 2
                    sbci[0] += 1
                    pr_s = prg.dma("sync" if si == 0 else "scalar", sbc[:, si, :],
                                   s1_h[g:g + 1, qt * 1024 + h2 * 512:qt * 1024 + (h2 + 1) * 512].broadcast_to([P, 512]),
                                   waits=[("dqc", sbc_slot[si])], inc="sb%d" % si)
                    sbc_slot[si] = prg.op("vector", lambda e, g=g, qt=qt, h2=h2, slot=slot, si=si: e.scalar_tensor_tensor(
                        out=w1s[:, g, qt * 1024 + h2 * 512:qt * 1024 + (h2 + 1) * 512],
                        in0=qf[:, slot, h2 * 512:(h2 + 1) * 512], scalar=-128.0,
                        in1=sbc[:, si, :], op0=ALU.add, op1=ALU.mult),
                        waits=[("sb%d" % si, pr_s), ("qf%d" % slot, pr_q)], inc="dqc")
                dq_slot[slot] = C["dqc"]

        for g in range(8):
            dq_w1_group(g)
        dq_w1 = C["dqc"]

        splv = [0] * 33
        dqw2t = [0] * 32

        def dq_w2_tile(t):
            slot = t % 2
            w = [("dqc", dq_slot[slot])]
            pr_q = prg.dma("sync", qf[:, slot, :], q2_h[t * P:(t + 1) * P, :],
                           waits=w, inc="qf%d" % slot)
            wstt = []
            if t >= 8:
                wstt = [("sp%d" % ((t - 8) % 8), splv[t - 8]),
                        ("w2fq", (t - 8) // 4 + 1)]
            for h2 in range(2):
                si = sbci[0] % 2
                sbci[0] += 1
                pr_s = prg.dma("sync", sbc[:, si, :],
                               s2h_h[t:t + 1, h2 * 512:(h2 + 1) * 512].broadcast_to([P, 512]),
                               waits=[("dqc", sbc_slot[si]), ("s2h", s2h_done)], inc="sb%d" % si)
                sbc_slot[si] = prg.op("vector", lambda e, t=t, h2=h2, slot=slot, si=si: e.scalar_tensor_tensor(
                    out=w2r[:, t % 8, h2 * 512:(h2 + 1) * 512],
                    in0=qf[:, slot, h2 * 512:(h2 + 1) * 512], scalar=-128.0,
                    in1=sbc[:, si, :], op0=ALU.add, op1=ALU.mult),
                    waits=[("sb%d" % si, pr_s), ("qf%d" % slot, pr_q)] + wstt, inc="dqc")
            dq_slot[slot] = C["dqc"]
            dq_t = C["dqc"]
            dqw2t[t] = dq_t
            splv[t] = prg.dma("sync", w2s_h[t], w2r[:, t % 8, :],
                              waits=[("dqc", dq_t)], inc="sp%d" % (t % 8))

        dq_w2 = C["dqc"]

        st1v = [0] * NCH
        ln1v = [0] * NCH
        lnxv = [0] * NCH
        xsmv = [0] * NCH
        xdv = [0] * (NCH + 8)
        st2av = [0] * NCH
        st2ev = [0] * NCH
        ln2v = [0] * NCH
        st2v = [0] * NCH
        xbfv = [0] * NCH
        xtrv = [0] * NCH
        xntv = [0] * NCH
        bcav = [0] * NHB
        bcev = [0] * NHB
        xnnv = [0] * NHB
        qkv_ = [0] * NHB
        sgv = [0] * NHB
        vevv = [0] * NCH
        skrv = [0] * NCH
        smcv = [0] * NCH
        sktv = [0] * NCH
        sctv = [0] * NCH
        gstv = [0] * NCH
        stev = [0] * (NCH + 1)
        attv = [0] * NCH
        cmbv = [0] * NCH
        cbtv = [0] * NCH
        cbev = [0] * NCH
        wozv = [0] * NCH
        yqv = [0] * (NHB * 8 + 1)
        peyv = [0] * (NHB * 8 + 1)
        pezv = [0] * NCH
        zuv = [0] * NCH
        zevv = [0] * NCH
        otv = [0] * NCH
        w2lv = [0] * (NHB * 16 + 1)
        w2fv = [0] * (NHB * 8 + 1)

        ST = stat

        def s_ap(c, i):
            return ST[:, c % 8, i:i + 1]

        def x_load(c):
            if c >= NCH:
                return
            w = []
            if c >= 3:
                w = [("lnx", lnxv[c - 3]), ("st1", st1v[c - 3]),
                     ("xsm", xsmv[c - 3])]
            xdv[c] = prg.dma("sync", xbuf[:, c % 3, :], x_h[c * P:(c + 1) * P, :],
                             waits=w, inc="x%d" % (c % 3))

        def prep_dve_stats(c0):
            c1 = c0 + 1
            bnv = 0
            for c, base in ((c0, 0), (c1, 32)):
                w = [("x%d" % (c % 3), xdv[c])]
                if c >= 8:
                    w.append(("zev", zevv[c - 8]))
                prg.op("vector", lambda e, c=c, base=base: e.bn_stats(
                    out=dump[:, base:base + 6], in_=xbuf[:, c % 3, 0:512]),
                    waits=w, inc="sfv")
                bnv = prg.op("vector", lambda e, c=c, base=base: e.bn_stats(
                    out=dump[:, base + 6:base + 12], in_=xbuf[:, c % 3, 512:1024]),
                    inc="sfv")
            aggv = 0
            for c, base in ((c0, 0), (c1, 32)):
                aggv = prg.op("vector", lambda e, c=c, base=base: e.bn_aggr(
                    out=ST[:, c % 8, 0:2], in_=dump[:, base:base + 12]),
                    waits=[("sfv", bnv)] if base == 0 else (), inc="sfv")
            for c in (c0, c1):
                st1v[c] = prg.op("vector", lambda e, c=c: e.tensor_scalar(
                    out=s_ap(c, 15), in0=s_ap(c, 1), scalar1=EPS_LN,
                    scalar2=None, op0=ALU.add),
                    waits=[("sfv", aggv)] if c == c0 else (), inc="st1")

        def prep_scalar_ln1(c0):
            c1 = c0 + 1
            t1v = 0
            for c in (c0, c1):
                t1v = prg.op("scalar", lambda e, c=c: e.activation(
                    out=s_ap(c, 2), in_=s_ap(c, 15), func=AF.Ln),
                    waits=[("st1", st1v[c])], inc="sfa")
            for c in (c0, c1):
                prg.op("scalar", lambda e, c=c: e.activation(
                    out=s_ap(c, 3), in_=s_ap(c, 2), func=AF.Exp, scale=-0.5),
                    waits=[("sfa", t1v)] if c == c0 else (), inc="ln1")
                ln1v[c] = prg.op("scalar", lambda e, c=c: e.activation(
                    out=s_ap(c, 4), in_=s_ap(c, 2), func=AF.Exp, scale=0.5),
                    inc="ln1")

        def prep_dve_ln1(c0):
            for c in (c0, c0 + 1):
                prg.op("vector", lambda e, c=c: e.tensor_scalar(
                    out=s_ap(c, 5), in0=s_ap(c, 0), scalar1=s_ap(c, 3),
                    scalar2=-1.0, op0=ALU.mult, op1=ALU.mult),
                    waits=[("ln1", ln1v[c])], inc="st2")
                st2av[c] = prg.op("vector", lambda e, c=c: e.tensor_scalar(
                    out=s_ap(c, 6), in0=s_ap(c, 4), scalar1=1.0, scalar2=None,
                    op0=ALU.add), inc="st2")

        def prep_pool_xnf(c0):
            xfv = 0
            for c in (c0, c0 + 1):
                w = [("st2", st2av[c])]
                if c >= 2:
                    w.append(("xsm", xsmv[c - 2]))
                lnxv[c] = prg.op("vector", lambda e, c=c: e.tensor_scalar(
                    out=xnf[:, c % 2, :], in0=xbuf[:, c % 3, :],
                    scalar1=s_ap(c, 3), scalar2=s_ap(c, 5),
                    op0=ALU.mult, op1=ALU.add), waits=w, inc="lnx")
                xfv = lnxv[c]
            for c in (c0, c0 + 1):
                w = [("lnx", xfv)]
                if c >= 4:
                    w.append(("zev", zevv[c - 4]))
                xbfv[c] = prg.op("vector", lambda e, c=c: e.tensor_copy(
                    out=xnb[:, c % 4, :], in_=xnf[:, c % 2, :]),
                    waits=w, inc="xbf")

        def prep_dve_sums(c0):
            for c in (c0, c0 + 1):
                prg.op("vector", lambda e, c=c: e.reduce_sum(
                    out=s_ap(c, 7), in_=xnf[:, c % 2, :], axis=AX.X),
                    waits=[("lnx", lnxv[c])], inc="sfv")
                xsmv[c] = prg.op("vector", lambda e, c=c: e.scalar_tensor_tensor(
                    out=xbuf[:, c % 3, :], in0=xnf[:, c % 2, :], scalar=1.0,
                    in1=xnf[:, c % 2, :], op0=ALU.mult, op1=ALU.mult,
                    accum_out=s_ap(c, 8)), inc="xsm")

        def prep_dve_ln2(c0):
            mmv = 0
            for c in (c0, c0 + 1):
                prg.op("vector", lambda e, c=c: e.tensor_scalar(
                    out=s_ap(c, 9), in0=s_ap(c, 7), scalar1=1.0 / H,
                    scalar2=None, op0=ALU.mult),
                    waits=[("xsm", xsmv[c])])
                mmv = prg.op("vector", lambda e, c=c: e.tensor_scalar(
                    out=s_ap(c, 10), in0=s_ap(c, 7), scalar1=s_ap(c, 7),
                    scalar2=1.0 / (float(H) * H), op0=ALU.mult, op1=ALU.mult),
                    inc="sfv")
            vvv = 0
            for c in (c0, c0 + 1):
                vvv = prg.op("vector", lambda e, c=c: e.scalar_tensor_tensor(
                    out=s_ap(c, 11), in0=s_ap(c, 8), scalar=1.0 / H,
                    in1=s_ap(c, 10), op0=ALU.mult, op1=ALU.subtract),
                    waits=[("sfv", mmv)] if c == c0 else (), inc="sfv")
            for c in (c0, c0 + 1):
                st2ev[c] = prg.op("vector", lambda e, c=c: e.tensor_scalar(
                    out=s_ap(c, 15), in0=s_ap(c, 11), scalar1=EPS_LN,
                    scalar2=None, op0=ALU.add),
                    waits=[("sfv", vvv)] if c == c0 else (), inc="st2")

        def prep_scalar_ln2_late(c0):
            t2v = 0
            for c in (c0, c0 + 1):
                t2v = prg.op("scalar", lambda e, c=c: e.activation(
                    out=s_ap(c, 12), in_=s_ap(c, 15), func=AF.Ln),
                    waits=[("st2", st2ev[c])], inc="sfa")
            for c in (c0, c0 + 1):
                ln2v[c] = prg.op("scalar", lambda e, c=c: e.activation(
                    out=s_ap(c, 13), in_=s_ap(c, 12), func=AF.Exp, scale=-0.5),
                    waits=[("sfa", t2v)] if c == c0 else (), inc="ln1")

        def prep_dve_negm2(c0):
            for c in (c0, c0 + 1):
                st2v[c] = prg.op("vector", lambda e, c=c: e.tensor_scalar(
                    out=s_ap(c, 14), in0=s_ap(c, 9), scalar1=s_ap(c, 13),
                    scalar2=-1.0, op0=ALU.mult, op1=ALU.mult),
                    waits=[("ln1", ln2v[c])], inc="st2")

        def prep_pe_trans(c, hb_p, bnk):
            w = [("xbf", xbfv[c])]
            if hb_p >= 1:
                w.append(("cmbs", cmbv[2 * hb_p - 1]))
            for j in range(8):
                xtrv[c] = prg.op("tensor", lambda e, c=c, j=j, bnk=bnk: e.transpose(
                    out=PSH[:, bnk * 1024 + j * P:bnk * 1024 + (j + 1) * P],
                    in_=xnb[:, c % 4, j * P:(j + 1) * P], identity=idb[:]),
                    waits=w if j == 0 else (),
                    inc="xtr" if j == 7 else None)

        def prep_pe_bcast(hb_p):
            c0p, c1p = 2 * hb_p, 2 * hb_p + 1
            for cc, c in ((0, c0p), (1, c1p)):
                w = [("st2", st2v[c])]
                if cc == 0:
                    w.append(("xnt", xntv[c0p]))
                for i, col in ((0, 13), (1, 14)):
                    bcav[hb_p] = prg.op("tensor", lambda e, c=c, i=i, col=col, cc=cc: e.matmul(
                        out=PS[:, 6 * 512 + (cc * 2 + i) * P:6 * 512 + (cc * 2 + i + 1) * P],
                        lhsT=s_ap(c, col).broadcast_to([P, P]), rhs=idf[:],
                        start=True, stop=True),
                        waits=w if i == 0 else (),
                        inc="bca" if (cc == 1 and i == 1) else None)

        def prep_dve_evac_xnt(c, hb_p, bnk):
            cc = c % 2
            w = [("xtr", xtrv[c])]
            if hb_p >= 2:
                w.append(("qk", qkv_[hb_p - 2]))
            xntv[c] = prg.op("vector", lambda e, hb_p=hb_p, cc=cc, bnk=bnk: e.tensor_copy(
                out=xnT[:, hb_p % 2, :, cc * P:(cc + 1) * P],
                in_=PSH[:, bnk * 1024:(bnk + 1) * 1024].rearrange("p (j t) -> p j t", j=8)),
                waits=w, inc="xnt")

        def prep_dve_evac_bcast(hb_p):
            w = [("bca", bcav[hb_p])]
            if hb_p >= 2:
                w.append(("xnn", xnnv[hb_p - 2]))
            prg.op("vector", lambda e, hb_p=hb_p: e.tensor_copy(
                out=r2sb[:, hb_p % 2, 0, :].rearrange("p (c x) -> p c x", c=2),
                in_=PS[:, 6 * 512:6 * 512 + 512].rearrange("p (c i x) -> p c i x", c=2, i=2)[:, :, 0, :]),
                waits=w)
            bcev[hb_p] = prg.op("vector", lambda e, hb_p=hb_p: e.tensor_copy(
                out=r2sb[:, hb_p % 2, 1, :].rearrange("p (c x) -> p c x", c=2),
                in_=PS[:, 6 * 512:6 * 512 + 512].rearrange("p (c i x) -> p c i x", c=2, i=2)[:, :, 1, :]),
                inc="bce")

        def prep_pool_xnn(hb_p):
            c1 = 2 * hb_p + 1
            w = [("xnt", xntv[c1]), ("bce", bcev[hb_p])]
            if hb_p >= 2:
                w.append(("peyq", 8 * (hb_p - 2) + 8))
            r2rep = r2sb[:, hb_p % 2, 0, :].unsqueeze(1).broadcast_to([P, 8, 256])
            nmrep = r2sb[:, hb_p % 2, 1, :].unsqueeze(1).broadcast_to([P, 8, 256])
            dst = xnnT[:, hb_p % 2, :, :]
            src = xnT[:, hb_p % 2, :, :]
            prg.op("gpsimd", lambda e: e.tensor_tensor(out=dst, in0=src, in1=r2rep,
                                                       op=ALU.mult), waits=w)
            prg.op("gpsimd", lambda e: e.tensor_tensor(out=dst, in0=dst, in1=nmrep,
                                                       op=ALU.add))
            for j in range(8):
                xnnv[hb_p] = prg.op("gpsimd", lambda e, j=j, hb_p=hb_p: e.tensor_scalar(
                    out=xnnT[:, hb_p % 2, j, :], in0=xnnT[:, hb_p % 2, j, :],
                    scalar1=gbc[:, j:j + 1], scalar2=gbc[:, 8 + j:9 + j],
                    op0=ALU.mult, op1=ALU.add),
                    inc="xnn" if j == 7 else None)

        def p1(hb):
            c0, c1 = 2 * hb, 2 * hb + 1
            w0 = [("xnt", xntv[c1]), ("bce", bcev[hb])]
            if hb == 0:
                w0.append(("dqc", dq_wv))
            for t in range(2):
                for j in range(8):
                    qkv_[hb] = prg.op("tensor", lambda e, hb=hb, t=t, j=j: e.matmul(
                        out=PS[:, 6 * 512 + t * 256:6 * 512 + (t + 1) * 256],
                        lhsT=wqT[:, j, t * P:(t + 1) * P], rhs=xnT[:, hb % 2, j, :],
                        start=(j == 0), stop=(j == 7)),
                        waits=w0 if (t == 0 and j == 0) else (),
                        inc="qk" if (t == 1 and j == 7) else None)
            for t in range(2):
                for j in range(8):
                    qkv_[hb] = prg.op("tensor", lambda e, hb=hb, t=t, j=j: e.matmul(
                        out=PS[:, 7 * 512 + t * 256:7 * 512 + (t + 1) * 256],
                        lhsT=wkT[:, j, t * P:(t + 1) * P], rhs=xnT[:, hb % 2, j, :],
                        start=(j == 0), stop=(j == 7)),
                        inc="qk" if (t == 1 and j == 7) else None)
            wv_ = []
            if hb >= 1:
                wv_ = [("zu", zuv[c0 - 2]), ("vev", vevv[c1 - 2])]
            for cc in range(2):
                for j in range(8):
                    qkv_[hb] = prg.op("tensor", lambda e, hb=hb, cc=cc, j=j: e.matmul(
                        out=PS[:, 2 * 512 + cc * 256:2 * 512 + (cc + 1) * 256],
                        lhsT=xnT[:, hb % 2, j, cc * P:(cc + 1) * P], rhs=wvT[:, j, :],
                        start=(j == 0), stop=(j == 7)),
                        waits=wv_ if (cc == 0 and j == 0) else (),
                        inc="qk" if (cc == 1 and j == 7) else None)

        def sigma(hb):
            qdone = qkv_[hb] - 2
            kdone = qkv_[hb] - 1
            for idx, (bnk, dstT, wv) in enumerate(((6, sqT, qdone), (7, skT, kdone))):
                w = [("qk", wv)]
                if hb >= 2:
                    w.append(("attd", attv[2 * hb - 3]))
                prg.op("vector", lambda e, bnk=bnk, idx=idx: e.tensor_scalar(
                    out=elu[:, idx, :], in0=PS[:, bnk * 512:(bnk + 1) * 512],
                    scalar1=0.0, scalar2=None, op0=ALU.min),
                    waits=w, inc="sgm")
                mv = C["sgm"]
                prg.op("scalar", lambda e, idx=idx: e.activation(
                    out=elu[:, idx, :], in_=elu[:, idx, :], func=AF.Exp),
                    waits=[("sgm", mv)], inc="sge")
                ev = C["sge"]
                sgv[hb] = prg.op("vector", lambda e, bnk=bnk, idx=idx, hb=hb, dstT=dstT: e.scalar_tensor_tensor(
                    out=dstT[:, hb % 2, :, :].rearrange("p t x -> p (t x)"),
                    in0=PS[:, bnk * 512:(bnk + 1) * 512], scalar=1.0,
                    in1=elu[:, idx, :], op0=ALU.add, op1=ALU.max),
                    waits=[("sge", ev)], inc="sg")

        def v_evac(hb):
            c0 = 2 * hb
            for cc in range(2):
                c = c0 + cc
                w = [("qk", qkv_[hb])]
                if c >= 4:
                    w.append(("attd", attv[c - 4]))
                prg.op("vector", lambda e, cc=cc, c=c: e.tensor_copy(
                    out=vE[:, c % 4, 0:D],
                    in_=PS[:, 2 * 512 + cc * 256:2 * 512 + (cc + 1) * 256]),
                    waits=w)
                vevv[c] = prg.op("vector", lambda e, cc=cc, c=c: e.tensor_scalar(
                    out=vEg[:, c % 4, 0:D],
                    in0=PS[:, 2 * 512 + cc * 256:2 * 512 + (cc + 1) * 256],
                    scalar1=gcol[:, 1:2], scalar2=None, op0=ALU.mult), inc="vev")


        def attn_full(c, hb):
            cc = c % 2
            zb = 2 if cc == 0 else 4
            w = [("sg", sgv[hb])]
            if c >= 1:
                w.append(("cmbs", cmbv[c - 1]))
            for t in range(2):
                sktv[c] = prg.op("tensor", lambda e, t=t, hb=hb, cc=cc: e.transpose(
                    out=PSH[:, 6 * 1024 + 640 + t * P:6 * 1024 + 640 + (t + 1) * P],
                    in_=skT[:, hb % 2, t, cc * P:(cc + 1) * P], identity=idb[:]),
                    waits=w if t == 0 else (), inc="skt" if t == 1 else None)
            wd = [("skt", sktv[c])]
            if c >= 4:
                wd.append(("gst", gstv[c - 4]))
            skrv[c] = prg.op("vector", lambda e, c=c: e.tensor_copy(
                out=skR[:, c % 4, :], in_=PSH[:, 6 * 1024 + 640:6 * 1024 + 896]),
                waits=wd, inc="skr")
            prg.op("tensor", lambda e, hb=hb, cc=cc: e.matmul(
                out=PS[:, 7 * 512 + 320:7 * 512 + 448],
                lhsT=skT[:, hb % 2, 0, cc * P:(cc + 1) * P],
                rhs=sqT[:, hb % 2, 0, cc * P:(cc + 1) * P],
                start=True, stop=False),
                waits=[("smc", smcv[c - 1])] if c >= 1 else ())
            sctv[c] = prg.op("tensor", lambda e, hb=hb, cc=cc: e.matmul(
                out=PS[:, 7 * 512 + 320:7 * 512 + 448],
                lhsT=skT[:, hb % 2, 1, cc * P:(cc + 1) * P],
                rhs=sqT[:, hb % 2, 1, cc * P:(cc + 1) * P],
                start=False, stop=True), inc="sct")
            wd = [("sct", sctv[c])]
            if c >= 2:
                wd.append(("attd", attv[c - 1]))
            smcv[c] = prg.op("vector", lambda e, c=c: e.tensor_tensor(
                out=smx[:, c % 2, :], in0=PS[:, 7 * 512 + 320:7 * 512 + 448],
                in1=msk[:], op=ALU.mult), waits=wd, inc="smc")
            wg = [("vev", vevv[c]), ("skr", skrv[c])]
            if c >= 1:
                wg.append(("ste", stev[c]))
            if hb >= 1 and cc == 0:
                wg.append(("yq", yqv[8 * hb]))
            if hb == 0 and cc == 0:
                wg.append(("dqc", dq_wo))
            for t in range(2):
                gstv[c] = prg.op("tensor", lambda e, t=t, c=c: e.matmul(
                    out=PS[:, t * 256:(t + 1) * 256],
                    lhsT=skR[:, c % 4, t * P:(t + 1) * P], rhs=vE[:, c % 4, 0:D],
                    start=True, stop=True),
                    waits=wg if t == 0 else (), inc="gst" if t == 1 else None)
            for t in range(2):
                attv[c] = prg.op("tensor", lambda e, t=t, hb=hb, cc=cc: e.matmul(
                    out=PS[:, 7 * 512:7 * 512 + 257],
                    lhsT=sqT[:, hb % 2, t, cc * P:(cc + 1) * P],
                    rhs=mext[:, t, 0:257], start=(t == 0), stop=(t == 1)),
                    waits=[("sfv", sfv_mext)] if (t == 0 and c == 0) else (),
                    inc="attd" if t == 1 else None)
            wi = [("smc", smcv[c])]
            if c >= 1:
                wi.append(("ste", stev[c]))
            for t in range(2):
                prg.op("tensor", lambda e, t=t, hb=hb, cc=cc, c=c: e.matmul(
                    out=PS[:, 6 * 512:6 * 512 + 257],
                    lhsT=sqT[:, hb % 2, t, cc * P:(cc + 1) * P],
                    rhs=stE[:, c % 2, t, 0:257], start=(t == 0), stop=False),
                    waits=wi if t == 0 else ())
            attv[c] = prg.op("tensor", lambda e, c=c: e.matmul(
                out=PS[:, 6 * 512:6 * 512 + 257],
                lhsT=smx[:, c % 2, :], rhs=vEg[:, c % 4, 0:257],
                start=False, stop=True), inc="attd")
            pc, nx = c % 2, (c + 1) % 2
            prg.op("vector", lambda e, pc=pc, nx=nx: e.scalar_tensor_tensor(
                out=kvst[:, nx, 0, :], in0=PS[:, 0:256], scalar=gcol[:, 1:2],
                in1=kvst[:, pc, 0, :], op0=ALU.mult, op1=ALU.add),
                waits=[("gst", gstv[c])], inc="sfv")
            kvv = prg.op("vector", lambda e, pc=pc, nx=nx: e.scalar_tensor_tensor(
                out=kvst[:, nx, 1, :], in0=PS[:, 256:512], scalar=gcol[:, 1:2],
                in1=kvst[:, pc, 1, :], op0=ALU.mult, op1=ALU.add), inc="sfv")
            redv = 0
            for t in range(2):
                redv = prg.op("vector", lambda e, t=t, hb=hb, cc=cc: e.reduce_sum(
                    out=dump[:, 64 + t:65 + t],
                    in_=skT[:, hb % 2, t, cc * P:(cc + 1) * P], axis=AX.X),
                    inc="sfv")
            ksv = 0
            for t in range(2):
                wk2 = ()
                if t == 0:
                    wk2 = [("sfv", redv)]
                    if c == 0:
                        wk2.append(("dqc", dq_memset))
                ksv = prg.op("gpsimd", lambda e, t=t, pc=pc, nx=nx: e.tensor_add(
                    out=kss[:, nx * 2 + t:nx * 2 + t + 1],
                    in0=kss[:, pc * 2 + t:pc * 2 + t + 1],
                    in1=dump[:, 64 + t:65 + t]),
                    waits=wk2, inc="gse")
            wc = [("attd", attv[c])]
            prg.op("vector", lambda e: e.tensor_scalar(
                out=dump[:, 66:67], in0=PS[:, 7 * 512 + 256:7 * 512 + 257],
                scalar1=EPS_ATT, scalar2=None, op0=ALU.max), waits=wc, inc="sfv")
            dav = prg.op("vector", lambda e: e.tensor_scalar(
                out=dump[:, 67:68], in0=PS[:, 6 * 512 + 256:6 * 512 + 257],
                scalar1=EPS_ATT, scalar2=None, op0=ALU.max), inc="sfv")
            rmv = prg.op("vector", lambda e: e.reciprocal(out=dump[:, 68:69], in_=dump[:, 66:67]),
                         waits=[("sfv", dav)], inc="sfv")
            rav = prg.op("vector", lambda e: e.reciprocal(out=dump[:, 69:70], in_=dump[:, 67:68]),
                         inc="sfv")
            t1v = prg.op("vector", lambda e, c=c: e.tensor_scalar(
                out=dump[:, 128:384], in0=PS[:, 7 * 512:7 * 512 + 256],
                scalar1=dump[:, 68:69], scalar2=None, op0=ALU.mult),
                waits=[("sfv", rmv)], inc="sfv")
            prg.op("gpsimd", lambda e, nx=nx: e.tensor_copy(
                out=stE[:, nx, 0:2, 0:D],
                in_=kvst[:, nx, 0:2, :]), waits=[("sfv", kvv)], inc="gse")
            stev[c + 1] = prg.op("gpsimd", lambda e, nx=nx: e.tensor_copy(
                out=stE[:, nx, 0:2, D:D + 1],
                in_=kss[:, nx * 2:nx * 2 + 2].unsqueeze(2)),
                waits=[("gse", ksv)], inc="ste")
            wcm = []
            if c >= 2:
                wcm.append(("cbt", cbtv[c - 2]))
            cmbv[c] = prg.op("vector", lambda e, c=c: e.scalar_tensor_tensor(
                out=cmb[:, c % 2, :], in0=PS[:, 6 * 512:6 * 512 + 256],
                scalar=dump[:, 69:70], in1=dump[:, 128:384],
                op0=ALU.mult, op1=ALU.add), waits=wcm + [("sfv", t1v)], inc="cmbs")
            for t in range(2):
                cbtv[c] = prg.op("tensor", lambda e, t=t, c=c: e.transpose(
                    out=PSH[:, 1536 + t * P:1536 + (t + 1) * P],
                    in_=cmb[:, c % 2, t * P:(t + 1) * P], identity=idb[:]),
                    waits=[("cmbs", cmbv[c])] if t == 0 else (),
                    inc="cbt" if t == 1 else None)
            wd = [("cbt", cbtv[c])]
            if c >= 2:
                wd.append(("woz", wozv[c - 2]))
            cbev[c] = prg.op("vector", lambda e, c=c: e.tensor_copy(
                out=cbT[:, c % 2, :, :].rearrange("p t x -> p (t x)"),
                in_=PSH[:, 1536:1792]), waits=wd, inc="cbe")
            wz = [("cbe", cbev[c])]
            if c >= 2:
                wz.append(("zu", zuv[c - 2]))
            if c == 0:
                wz.append(("dqc", dq_b2))
            for t in range(2):
                for half in range(2):
                    b = zb + half
                    prg.op("tensor", lambda e, t=t, half=half, c=c, b=b: e.matmul(
                        out=PS[:, b * 512:(b + 1) * 512],
                        lhsT=cbT[:, c % 2, t, :],
                        rhs=woT[:, t, half * 512:(half + 1) * 512],
                        start=(t == 0), stop=False),
                        waits=wz if (t == 0 and half == 0) else ())
            for half in range(2):
                b = zb + half
                wozv[c] = prg.op("tensor", lambda e, half=half, b=b: e.matmul(
                    out=PS[:, b * 512:(b + 1) * 512],
                    lhsT=onesb[0:1, 0:P], rhs=b2r[0:1, half * 512:(half + 1) * 512],
                    start=False, stop=False),
                    inc="woz" if half == 1 else None)

        def w2_load(hb, gidx):
            if hb == 0:
                return
            G0 = hb * 16 + gidx
            t0 = 2 * gidx
            w = []
            if G0 >= 4:
                w.append(("w2fq", (2 * G0 - 7) // 4 + 1))
            if hb == 1:
                w.append(("sp%d" % (t0 % 8), splv[24 + t0 % 8]))
                w.append(("sp%d" % ((t0 + 1) % 8), splv[24 + (t0 + 1) % 8]))
            w2lv[G0] = ("wl%d" % (gidx % 4), prg.dma(
                "sync", w2r[:, (t0 % 8):(t0 % 8) + 2, :],
                w2s_h[t0:t0 + 2].rearrange("t p h -> p t h"),
                waits=w, inc="wl%d" % (gidx % 4)))

        def ffn(hb, do_prep):
            c0, c1 = 2 * hb, 2 * hb + 1
            hb_p = hb + 1

            def fc2_quad(qq, final):
                gq = hb * 8 + qq
                for fi in range(4):
                    ft = 4 * qq + fi
                    G0 = hb * 16 + ft // 2
                    if hb == 0:
                        wf = [("yq", yqv[hb * 8 + qq + 1]), ("dqc", dqw2t[ft])]
                    else:
                        wf = [("yq", yqv[hb * 8 + qq + 1]), w2lv[G0]]
                    for cc2 in range(2):
                        for half in range(2):
                            b = (2 if cc2 == 0 else 4) + half
                            inc = None
                            if final and fi == 3 and half == 1:
                                inc = "pez"
                            elif (not final) and fi == 3 and half == 1 and cc2 == 1:
                                inc = "w2fq"
                            r = prg.op("tensor", lambda e, ft=ft, cc2=cc2, half=half, b=b, final=final, fi=fi: e.matmul(
                                out=PS[:, b * 512:(b + 1) * 512],
                                lhsT=yT[:, ft % 8, cc2 * P:(cc2 + 1) * P],
                                rhs=w2r[:, ft % 8, half * 512:(half + 1) * 512],
                                start=False, stop=(final and fi == 3)),
                                waits=wf if (cc2 == 0 and half == 0) else (),
                                inc=inc)
                            if inc == "pez":
                                pezv[c0 + cc2] = r
                if final:
                    w2fv[gq] = prg.op("tensor", lambda e: e.nop(), inc="w2fq")

            for q in range(8):
                gq = hb * 8 + q
                wq_ = [("xnn", xnnv[hb]), ("cbe", cbev[c1])]
                if hb == 0 and q == 0:
                    wq_.append(("dqc", dq_w1))
                if gq >= 1:
                    wq_.append(("yq", gq))
                if q == 0:
                    wq_.append(("ste", stev[c1 + 1]))
                for fi in range(4):
                    ft = 4 * q + fi
                    b = 0 if fi < 2 else 1
                    off = (fi % 2) * 256
                    for j in range(8):
                        peyv[gq] = prg.op("tensor", lambda e, ft=ft, j=j, b=b, off=off, hb=hb: e.matmul(
                            out=PS[:, b * 512 + off:b * 512 + off + 256],
                            lhsT=w1s[:, j, ft * P:(ft + 1) * P],
                            rhs=xnnT[:, hb % 2, j, :], start=(j == 0), stop=(j == 7)),
                            waits=wq_ if (fi == 0 and j == 0) else (),
                            inc="peyq" if (fi == 3 and j == 7) else None)
                if do_prep:
                    if q == 0:
                        prep_pe_trans(2 * hb_p, hb_p, 6)
                        prep_dve_evac_xnt(2 * hb_p, hb_p, 6)
                    elif q == 1:
                        prep_pe_trans(2 * hb_p + 1, hb_p, 7)
                        prep_dve_evac_xnt(2 * hb_p + 1, hb_p, 7)
                for fi in range(4):
                    ft = 4 * q + fi
                    b = 0 if fi < 2 else 1
                    off = (fi % 2) * 256
                    yqv[gq + 1] = prg.op("scalar", lambda e, ft=ft, b=b, off=off: e.activation(
                        out=yT[:, ft % 8, :], in_=PS[:, b * 512 + off:b * 512 + off + 256],
                        func=AF.Silu, bias=b1c[:, ft:ft + 1]),
                        waits=[("peyq", peyv[gq])] if fi == 0 else (),
                        inc="yq" if fi == 3 else None)
                if 1 <= q <= 6:
                    w2_load(hb, 4 + 2 * (q - 1))
                    w2_load(hb, 4 + 2 * (q - 1) + 1)
                if q >= 1:
                    fc2_quad(q - 1, False)
                    if hb == 0 and q <= 4:
                        for t in range(16 + 4 * (q - 1), 20 + 4 * (q - 1)):
                            dq_w2_tile(t)
                if do_prep and q == 7:
                    prep_scalar_ln2_late(2 * hb_p)
                    prep_dve_negm2(2 * hb_p)
                    prep_pe_bcast(hb_p)
                    prep_dve_evac_bcast(hb_p)
            fc2_quad(7, True)

        def assembly_u(c):
            zb = 2 if c % 2 == 0 else 4
            w = [("pez", pezv[c])]
            if c >= 2:
                w.append(("zev", zevv[c - 2]))
            zuv[c] = prg.op("scalar", lambda e, c=c, zb=zb: e.activation(
                out=ubuf[:, c % 2, :], in_=PS[:, zb * 512:(zb + 2) * 512],
                func=AF.Identity, bias=s_ap(c, 0)), waits=w, inc="zu")

        def assembly_rest(c):
            w = [("zu", zuv[c])]
            if c >= 2:
                w.append(("ot%d" % (c % 2), otv[c - 2]))
            zevv[c] = prg.op("vector", lambda e, c=c: e.scalar_tensor_tensor(
                out=orow[:, c % 2, :], in0=xnb[:, c % 4, :], scalar=s_ap(c, 6),
                in1=ubuf[:, c % 2, :], op0=ALU.mult, op1=ALU.add),
                waits=w, inc="zev")
            otv[c] = prg.dma("sync", out_h[c * P:(c + 1) * P, :], orow[:, c % 2, :],
                             waits=[("zev", zevv[c])], inc="ot%d" % (c % 2))

        for c in range(3):
            x_load(c)
        prep_dve_stats(0)
        prep_scalar_ln1(0)
        prep_dve_ln1(0)
        prep_pool_xnf(0)
        prep_dve_sums(0)
        prep_dve_ln2(0)
        prep_scalar_ln2_late(0)
        prep_dve_negm2(0)
        prep_pe_trans(0, 0, 6)
        prep_dve_evac_xnt(0, 0, 6)
        prep_pe_trans(1, 0, 7)
        prep_dve_evac_xnt(1, 0, 7)
        prep_pe_bcast(0)
        prep_dve_evac_bcast(0)
        prep_pool_xnn(0)

        for hb in range(NHB):
            c0, c1 = 2 * hb, 2 * hb + 1
            x_load(c0 + 3)
            x_load(c0 + 4)
            for gidx in range(4):
                w2_load(hb, gidx)
            if hb >= 2:
                assembly_u(c0 - 2)
                assembly_u(c1 - 2)
            p1(hb)
            sigma(hb)
            if hb + 1 < NHB:
                prep_dve_stats(c0 + 2)
                prep_scalar_ln1(c0 + 2)
            v_evac(hb)
            attn_full(c0, hb)
            if hb >= 2:
                assembly_rest(c0 - 2)
                assembly_rest(c1 - 2)
            if hb + 1 < NHB:
                prep_dve_ln1(c0 + 2)
                prep_pool_xnf(c0 + 2)
                prep_dve_sums(c0 + 2)
                prep_dve_ln2(c0 + 2)
            attn_full(c1, hb)
            if hb == 0:
                for t in range(16):
                    dq_w2_tile(t)
            ffn(hb, do_prep=(hb + 1 < NHB))
            if hb == 0:
                assembly_u(0)
                assembly_u(1)
                assembly_rest(0)
                assembly_rest(1)
            if hb + 1 < NHB:
                prep_pool_xnn(hb + 1)
        assembly_u(NCH - 2)
        assembly_u(NCH - 1)
        assembly_rest(NCH - 2)
        assembly_rest(NCH - 1)

        if DEBUG2:
            prg.dma("sync", w1d_h, w1s[:, :, :].rearrange("p g f -> p (g f)"),
                    inc="pcon")
            prg.dma("sync", w2d_h, w2s_h, inc="pcon")
        if DEBUG:
            prg.dma("sync", dbg_h, dbg[:], waits=[("zev", C["zev"])], inc="pcon")
        prg.op("sync", lambda e: e.nop(),
               waits=[("ot0", C["ot0"]), ("ot1", C["ot1"]), ("pcon", C["pcon"])])

        prg.replay(block)

    return nc


_NC_CACHE = None


def kernel(**inputs):
    global _NC_CACHE
    xs = np.asarray(inputs["L0_attn_hidden_states"], dtype=np.float32)
    B = xs.shape[0]
    ident_b = np.eye(P, dtype=ml_dtypes.bfloat16)
    ident_f = np.eye(P, dtype=np.float32)
    mask = np.triu(np.ones((P, P), dtype=np.float32)).astype(ml_dtypes.bfloat16)

    common = {
        "wq": np.ascontiguousarray(np.asarray(inputs["L0_attn_w_q"], np.float32)),
        "wk": np.ascontiguousarray(np.asarray(inputs["L0_attn_w_k"], np.float32)),
        "wv": np.ascontiguousarray(np.asarray(inputs["L0_attn_w_v"], np.float32)),
        "wo": np.ascontiguousarray(np.asarray(inputs["L0_attn_w_o"], np.float32)),
        "gate": np.ascontiguousarray(np.asarray(inputs["L0_attn_gate"], np.float32)),
        "mem": np.ascontiguousarray(np.asarray(inputs["L0_attn_memory"], np.float32)),
        "mn": np.ascontiguousarray(np.asarray(inputs["L0_attn_memory_norm"], np.float32)),
        "gam": np.ascontiguousarray(np.asarray(inputs["L0_ffn_ln_gamma"], np.float32)),
        "bet": np.ascontiguousarray(np.asarray(inputs["L0_ffn_ln_beta"], np.float32)),
        "q1": np.ascontiguousarray(np.asarray(inputs["L0_ffn_fc1_q"], np.int32)),
        "s1": np.ascontiguousarray(np.asarray(inputs["L0_ffn_fc1_scales"], np.float32)),
        "b1": np.ascontiguousarray(np.asarray(inputs["L0_ffn_fc1_bias"], np.float32)),
        "q2": np.ascontiguousarray(np.asarray(inputs["L0_ffn_fc2_q"], np.int32)),
        "s2": np.ascontiguousarray(np.asarray(inputs["L0_ffn_fc2_scales"], np.float32)),
        "b2": np.ascontiguousarray(np.asarray(inputs["L0_ffn_fc2_bias"], np.float32)),
        "idb_d": ident_b, "idf_d": ident_f, "msk_d": mask,
    }
    in_maps = []
    for b in range(B):
        m = dict(common)
        m["x"] = np.ascontiguousarray(xs[b])
        in_maps.append(m)

    if _NC_CACHE is None:
        _NC_CACHE = build_nc()
    res = run_bass_kernel_spmd(_NC_CACHE, in_maps, core_ids=list(range(B)))
    out = np.stack([np.asarray(r["out"], np.float32) for r in res.results], axis=0)
    if DEBUG:
        global LAST_DBG
        LAST_DBG = {b: np.asarray(res.results[b]["dbg"], np.float32) for b in range(B)}
    if DEBUG2:
        global LAST_DBG2
        LAST_DBG2 = {b: (np.asarray(res.results[b]["w1dump"]),
                         np.asarray(res.results[b]["w2dump"])) for b in range(B)}
    return out
